# version 32
# baseline (speedup 1.0000x reference)
import numpy as np
import ml_dtypes

import concourse.bass as bass
import concourse.mybir as mybir
from concourse.bass_utils import run_bass_kernel_spmd

F32 = mybir.dt.float32
BF16 = mybir.dt.bfloat16
AX = mybir.AxisListType.X
OP = mybir.AluOpType
AF = mybir.ActivationFunctionType
BFNP = ml_dtypes.bfloat16

NCORE = 8
B_TOT = 16
NSEQ = 2
T = 2048
D = 1024
H = 16
NT = 16
NG = 4
EPS = 1e-3
LNEPS = 1e-5
BIG = float(2.0 ** 100)

_DEBUG = False
MHOMES = {}



def _host_consts():
    c = {}
    inv_freq = 1.0 / 10000.0 ** (np.arange(0, 64, 2, dtype=np.float32) / 64)
    t = np.arange(T, dtype=np.float32)
    freqs = np.outer(t, inv_freq)
    cos = np.cos(freqs).astype(np.float32)
    sin = np.sin(freqs).astype(np.float32)
    C2 = np.zeros((128, T), np.float32)
    S2 = np.zeros((128, T), np.float32)
    for h2 in range(2):
        for j in range(64):
            C2[h2 * 64 + j] = cos[:, j % 32]
            S2[h2 * 64 + j] = -sin[:, j % 32] if j < 32 else sin[:, j - 32]
    c["c2"] = C2.astype(BFNP)
    c["s2"] = S2.astype(BFNP)

    P2 = np.zeros((128, 128), np.float32)
    for h2 in range(2):
        for i in range(64):
            P2[h2 * 64 + ((i + 32) % 64), h2 * 64 + i] = 1.0
    c["p2"] = P2.astype(BFNP)
    c["i128"] = np.eye(128, dtype=np.float32).astype(BFNP)
    c["tri"] = np.where(np.arange(128)[:, None] > np.arange(128)[None, :],
                        BIG, 0.0).astype(np.float32).astype(BFNP)

    q2cl = np.zeros((16, 128, 10), np.float32)
    for h in range(16):
        hl = h % 8
        for d in range(8):
            q2cl[h, hl * 16 + d, d] = -2.0 * EPS
        q2cl[h, hl * 16 + 8, 8] = 1.0
        q2cl[h, hl * 16 + 9, 9] = 1.0
    c["q2cl"] = np.ascontiguousarray(
        q2cl.transpose(1, 0, 2).reshape(128, 160)).astype(BFNP)

    rep = np.zeros((8, 16, 128), np.float32)
    for cc in range(8):
        rep[cc, 2 * cc, 0:64] = 1.0
        rep[cc, 2 * cc + 1, 64:128] = 1.0
    c["repc"] = np.ascontiguousarray(
        rep.transpose(1, 0, 2).reshape(16, 1024)).astype(BFNP)

    c["onesbf"] = np.ones((1, 128), BFNP)
    c["onesf32"] = np.ones((1, 128), np.float32)
    return c


def _host_layouts(ins):
    a = {}
    Wq = np.asarray(ins["L0_attn_Wq"], np.float32)
    Wk = np.asarray(ins["L0_attn_Wk"], np.float32)
    Wv = np.asarray(ins["L0_attn_Wv"], np.float32)
    Wo = np.asarray(ins["L0_attn_Wo"], np.float32)
    Wqm = np.asarray(ins["L0_attn_Wqm"], np.float32)
    Wkm = np.asarray(ins["L0_attn_Wkm"], np.float32)
    Wm = np.asarray(ins["L0_attn_W_metric"], np.float32)
    Wf = np.asarray(ins["L0_ffn_W"], np.float32)
    Vr = np.asarray(ins["L0_ffn_V_r"], np.float32)
    Cf = np.asarray(ins["L0_ffn_C"], np.float32)
    bf = np.asarray(ins["L0_ffn_b"], np.float32)

    def slab(W):
        return np.ascontiguousarray(
            W.reshape(8, 128, 8, 128).transpose(2, 1, 0, 3))

    a["wq_sl"] = slab(Wq)
    a["wk_sl"] = slab(Wk)
    a["wv_r"] = np.ascontiguousarray(Wv.reshape(8, 128, D).transpose(1, 0, 2))
    a["wo_r"] = np.ascontiguousarray(Wo.reshape(8, 128, D).transpose(1, 0, 2))
    a["wf_r"] = np.ascontiguousarray(
        np.ascontiguousarray(Wf.T).reshape(8, 128, D).transpose(1, 0, 2))
    a["vr_r"] = np.ascontiguousarray(Vr.reshape(8, 128, 32).transpose(1, 0, 2))
    a["ct_r"] = np.ascontiguousarray(Cf.T)
    a["b_r"] = np.ascontiguousarray(bf.reshape(1, D))

    def mlift(W8):
        L = np.zeros((D, 128), np.float32)
        for h in range(16):
            L[h * 64:(h + 1) * 64, h * 8:(h + 1) * 8] = W8
        return np.ascontiguousarray(L.reshape(8, 128, 128).transpose(1, 0, 2))

    a["wqm_l"] = mlift(Wqm)
    a["wkm_l"] = mlift(Wkm)
    Wu = np.zeros((64, 8), np.float32)
    for aa in range(8):
        for bb in range(8):
            Wu[aa * 8 + bb, :] = Wm[aa, bb * 8: bb * 8 + 8]
    a["wu_l"] = mlift(Wu)

    w74 = np.zeros((2, 128, 10), np.float32)
    for par in range(2):
        for m in range(8):
            for e in range(8):
                for d in range(8):
                    w74[par, par * 64 + m * 8 + e, d] = Wm[m, d * 8 + e]
    a["wm3r74"] = np.ascontiguousarray(w74)

    W3 = Wm.reshape(8, 8, 8)
    H1 = np.zeros((64, 8, 64), np.float32)
    H2 = np.zeros((64, 8, 64), np.float32)
    ab_a = np.arange(64) // 8
    ab_b = np.arange(64) % 8
    xy_x = np.arange(64) // 8
    xy_y = np.arange(64) % 8
    for e in range(8):
        H1[:, e, :] = W3[ab_a][:, xy_x, e]
        H2[:, e, :] = W3[ab_b][:, xy_y, e]
    a["h1"] = np.ascontiguousarray(H1.reshape(64, 512))
    a["h2"] = np.ascontiguousarray(H2.reshape(64, 512))

    a["temp"] = np.asarray(ins["L0_attn_temperature"], np.float32).reshape(1)
    return a



ENGS = ("sync", "vector", "scalar", "tensor", "gpsimd")


class Sem:
    def __init__(self, h, dma=False):
        self.h = h
        self.n = 0
        self.dma = dma

    def bump(self, kk):
        self.n += kk
        return self.n


class Buf:
    def __init__(self):
        self.w = None
        self.r = []

    def for_write(self):
        out = list(self.r)
        if self.w is not None:
            out.append(self.w)
        return out

    def wrote(self, tok):
        self.w = tok
        self.r = []

    def for_read(self):
        return [self.w] if self.w is not None else []

    def read(self, tok):
        self.r.append(tok)


NQSEM = 4


class K:
    def __init__(self, nc):
        self.nc = nc
        self.q = {e: [] for e in ENGS}
        self.sems = {}
        self.qsems = {}
        self.qcnt = {}
        for name in ("te", "ve", "se"):
            self.sems[name] = Sem(nc.alloc_semaphore(name=name))
        for name in ("di", "dt", "dts", "dg", "do"):
            self.qsems[name] = [Sem(nc.alloc_semaphore(name=f"{name}{i}"),
                                    dma=True) for i in range(NQSEM)]
            self.qcnt[name] = 0

    def qall(self, name):
        return [(s, s.n) for s in self.qsems[name] if s.n > 0]

    def emit(self, eng, fn, waits=(), inc=None):
        wd = {}
        for s, t in waits:
            if s.dma:
                t = s.n
            if s not in wd or t > wd[s]:
                wd[s] = t
        witems = tuple(wd.items())
        tok = None
        if inc is not None:
            tok = (inc[0], inc[0].bump(inc[1]))

        def g(e, fn=fn, witems=witems, inc=inc):
            for s, t in witems:
                e.wait_ge(s.h, t)
            ins = fn(e)
            if inc is not None and ins is not None:
                ins.then_inc(inc[0].h, inc[1])

        self.q[eng].append(g)
        return tok

    def _qsem(self, semn):
        s = self.qsems[semn][self.qcnt[semn] % NQSEM]
        self.qcnt[semn] += 1
        return s

    def dma(self, eng, semn, out_ap, in_ap, waits=()):
        return self.emit(eng, lambda e, o=out_ap, i=in_ap:
                         e.dma_start(out=o, in_=i),
                         waits=waits, inc=(self._qsem(semn), 16))

    def dma_t(self, eng, semn, out_ap, in_ap, waits=()):
        return self.emit(eng, lambda e, o=out_ap, i=in_ap:
                         e.dma_start(out=o, in_=i, transpose=True),
                         waits=waits, inc=(self._qsem(semn), 16))

    def build_block(self):
        nc = self.nc
        with nc.Block() as block:
            @block.sync
            def _(sync):
                for f in self.q["sync"]:
                    f(sync)

            @block.vector
            def _(vector):
                for f in self.q["vector"]:
                    f(vector)

            @block.scalar
            def _(scalar):
                for f in self.q["scalar"]:
                    f(scalar)

            @block.tensor
            def _(tensor):
                for f in self.q["tensor"]:
                    f(tensor)

            @block.gpsimd
            def _(gpsimd):
                for f in self.q["gpsimd"]:
                    f(gpsimd)



def build_nc(dbg=False):
    nc = bass.Bass("TRN2", target_bir_lowering=False)
    k = K(nc)
    te, ve, se = k.sems["te"], k.sems["ve"], k.sems["se"]

    dx = nc.dram_tensor("x", [NSEQ, T, D], F32, kind="ExternalInput")
    dy = nc.dram_tensor("y", [NSEQ, T, D], F32, kind="ExternalOutput")
    d_c2 = nc.dram_tensor("c2", [128, T], BF16, kind="ExternalInput")
    d_s2 = nc.dram_tensor("s2", [128, T], BF16, kind="ExternalInput")
    d_p2 = nc.dram_tensor("p2", [128, 128], BF16, kind="ExternalInput")
    d_i128 = nc.dram_tensor("i128", [128, 128], BF16, kind="ExternalInput")
    d_tri = nc.dram_tensor("tri", [128, 128], BF16, kind="ExternalInput")
    d_q2cl = nc.dram_tensor("q2cl", [128, 160], BF16, kind="ExternalInput")
    d_repc = nc.dram_tensor("repc", [16, 1024], BF16, kind="ExternalInput")
    d_onesbf = nc.dram_tensor("onesbf", [1, 128], BF16, kind="ExternalInput")
    d_onesf32 = nc.dram_tensor("onesf32", [1, 128], F32, kind="ExternalInput")
    d_wq = nc.dram_tensor("wq_sl", [8, 128, 8, 128], F32, kind="ExternalInput")
    d_wk = nc.dram_tensor("wk_sl", [8, 128, 8, 128], F32, kind="ExternalInput")
    d_wv = nc.dram_tensor("wv_r", [128, 8, 1024], F32, kind="ExternalInput")
    d_wo = nc.dram_tensor("wo_r", [128, 8, 1024], F32, kind="ExternalInput")
    d_wf = nc.dram_tensor("wf_r", [128, 8, 1024], F32, kind="ExternalInput")
    d_vr = nc.dram_tensor("vr_r", [128, 8, 32], F32, kind="ExternalInput")
    d_ct = nc.dram_tensor("ct_r", [32, 1024], F32, kind="ExternalInput")
    d_b = nc.dram_tensor("b_r", [1, D], F32, kind="ExternalInput")
    d_wqml = nc.dram_tensor("wqm_l", [128, 8, 128], F32, kind="ExternalInput")
    d_wkml = nc.dram_tensor("wkm_l", [128, 8, 128], F32, kind="ExternalInput")
    d_wul = nc.dram_tensor("wu_l", [128, 8, 128], F32, kind="ExternalInput")
    d_w74 = nc.dram_tensor("wm3r74", [2, 128, 10], F32, kind="ExternalInput")
    d_h1 = nc.dram_tensor("h1", [64, 512], F32, kind="ExternalInput")
    d_h2 = nc.dram_tensor("h2", [64, 512], F32, kind="ExternalInput")
    d_temp = nc.dram_tensor("temp", [1], F32, kind="ExternalInput")
    d_x1 = nc.dram_tensor("x1scr", [NSEQ, T, D], F32)
    d_rs = nc.dram_tensor("rsscr", [16, 512], BF16)
    d_xnt = nc.dram_tensor("xntscr", [NG, 4, 128, 1024], BF16)

    dbg_t = {}
    if dbg:
        dbg_t["rsA"] = nc.dram_tensor("dbg_rsA", [NSEQ, NG, 128, 2048], BF16,
                                      kind="ExternalOutput")
        dbg_t["rsT"] = nc.dram_tensor("dbg_rsT", [NSEQ, NG, 16, 512], BF16,
                                      kind="ExternalOutput")
        dbg_t["auU"] = nc.dram_tensor("dbg_auU", [NSEQ, NG, 128, 4096], BF16,
                                      kind="ExternalOutput")
        dbg_t["auN"] = nc.dram_tensor("dbg_auN", [NSEQ, NG, 128, 4096], BF16,
                                      kind="ExternalOutput")
        dbg_t["qf"] = nc.dram_tensor("dbg_qf", [NSEQ, NG, H, 74, 512], BF16,
                                     kind="ExternalOutput")

    sb = nc.alloc_sbuf_tensor
    wvof_sb = sb("wvof_sb", [128, 8 * 1024], BF16)
    wqml_sb = sb("wqml_sb", [128, 8 * 128], BF16)
    wkml_sb = sb("wkml_sb", [128, 8 * 128], BF16)
    wul_sb = sb("wul_sb", [128, 8 * 128], BF16)
    vr_sb = sb("vr_sb", [128, 8 * 32], BF16)
    ct_sb = sb("ct_sb", [32, 1024], BF16)
    b_sb = sb("b_sb", [1, D], BF16)
    c2_sb = sb("c2_sb", [128, 512], BF16)
    s2_sb = sb("s2_sb", [128, 512], BF16)
    p2_sb = sb("p2_sb", [128, 128], BF16)
    i128_sb = sb("i128_sb", [128, 128], BF16)
    tri_sb = sb("tri_sb", [128, 128], BF16)
    q2cl_sb = sb("q2cl_sb", [128, 160], BF16)
    repc_sb = sb("repc_sb", [16, 1024], BF16)
    onesb_sb = sb("onesb_sb", [1, 128], BF16)
    onesf_sb = sb("onesf_sb", [1, 128], F32)
    w74_sb = sb("w74_sb", [128, 20], BF16)
    mq_sb = sb("mq_sb", [128, 128], BF16)
    negc_sb = sb("negc_sb", [128, 1], F32)
    epsc_sb = sb("epsc_sb", [128, 1], F32)
    bmat_sb = sb("bmat_sb", [64, 64], F32)
    bscr_sb = sb("bscr_sb", [64, 64], F32)
    w74f_sb = sb("w74f_sb", [128, 20], F32)
    tmp1_sb = sb("tmp1_sb", [1, 1], F32)
    wqs_sb = sb("wqs_sb", [128, 2 * 1024], BF16)
    wks_sb = sb("wks_sb", [128, 2 * 1024], BF16)
    kxt_sb = sb("kxt_sb", [128, H * T], BF16, is_transpose_dest=True)
    vtm_sb = sb("vtm_sb", [128, NT * 1040], BF16)
    qmT_sb = sb("qmT_sb", [128, T], BF16)
    kmT_sb = sb("kmT_sb", [128, T], BF16)
    xq_sb = sb("xq_sb", [128, 8 * 512], BF16, is_transpose_dest=True)
    xnT_sb = xq_sb
    qq1T_sb = xq_sb
    quT_sb = sb("quT_sb", [128, 8 * 512], BF16, is_transpose_dest=True)
    qaT_sb = sb("qaT_sb", [128, 2 * 512], BF16, is_transpose_dest=True)
    qf_sb = sb("qf_sb", [74, 2 * 512], BF16)
    e_sb = sb("e_sb", [128, 4 * 512], BF16)
    scrA_sb = sb("scrA_sb", [128, 8 * 512], BF16)
    au_sb = scrA_sb
    kxtm_sb_ap = scrA_sb[:, 0:2048]
    qq1tm_ap = scrA_sb[:, 2048:3072]
    qutm_ap = scrA_sb[:, 3072:4096]
    qaux_ap = scrA_sb[:, 0:1024]
    uT_sb = sb("uT_sb", [128, 512], BF16)
    spr2_ap = uT_sb[:]
    qmtm_sb = sb("qmtm_sb", [128, 512], BF16, is_transpose_dest=True)
    utm_sb = sb("utm_sb", [128, 512], BF16, is_transpose_dest=True)
    kmtm_sb = sb("kmtm_sb", [128, 128], BF16, is_transpose_dest=True)
    sq_sb = sb("sq_sb", [128, 512], BF16)
    rep_sb = sb("rep_sb", [128, 1024], BF16)
    red1_sb = sb("red1_sb", [128, 64], F32)
    red2_sb = sb("red2_sb", [128, 64], F32)
    x1t_sb = sb("x1t_sb", [128, D], F32)
    xt_sb = sb("xt_sb", [128, D], F32)
    xorig_sb = xt_sb
    h1_sb = x1t_sb[0:64, 0:512]
    h2_sb = xt_sb[0:64, 0:512]
    xn_sb = sb("xn_sb", [128, D], BF16)
    yt_sb = sb("yt_sb", [128, 512], F32)
    st_sum = sb("st_sum", [128, 8], F32)
    st_sq = sb("st_sq", [128, 8], F32)
    st_mean = sb("st_mean", [128, 2], F32)
    st_var = sb("st_var", [128, 2], F32)
    st_var2 = sb("st_var2", [128, 2], F32)
    st_inv = sb("st_inv", [128, 2], F32)
    st_nb = sb("st_nb", [128, 2], F32)
    qT_sb = sb("qT_sb", [128, 512], BF16)
    kT_sb = sb("kT_sb", [128, 512], BF16)
    qc_sb = sb("qc_sb", [128, 512], BF16)
    qs_sb = sb("qs_sb", [128, 512], BF16)
    kc_sb = sb("kc_sb", [128, 512], BF16)
    ks_sb = sb("ks_sb", [128, 512], BF16)
    rsA_sb = sb("rsA_sb", [128, 2048], BF16)
    rsT_sb = sb("rsT_sb", [16, 512], BF16)
    rcpb_sb = rep_sb[0:16, 0:512]
    z1T_sb = rep_sb[0:32, 512:1024]

    ps = [nc.alloc_psum_tensor(f"ps{i}", [128, 512], F32) for i in range(8)]
    psb = [Buf() for _ in range(8)]
    bufs = {}

    def B_(name):
        if name not in bufs:
            bufs[name] = Buf()
        return bufs[name]

    def t_load(eng, semn, out_ap, in_ap, wbufs=(), rbufs=(), extra=()):
        waits = list(extra)
        for n in wbufs:
            waits += B_(n).for_write()
        for n in rbufs:
            waits += B_(n).for_read()
        tok = k.dma(eng, semn, out_ap, in_ap, waits=waits)
        for n in wbufs:
            B_(n).wrote(tok)
        for n in rbufs:
            B_(n).read(tok)
        return tok

    def t_tr(eng, out_ap, in_ap, wbufs=(), rbufs=(), extra=()):
        waits = list(extra)
        for n in wbufs:
            waits += B_(n).for_write()
        for n in rbufs:
            waits += B_(n).for_read()
        tok = k.dma_t("scalar", "dt", out_ap, in_ap, waits=waits)
        for n in wbufs:
            B_(n).wrote(tok)
        for n in rbufs:
            B_(n).read(tok)
        return tok

    def t_mm(bank, out_ap, lhsT, rhs, start, stop, rbufs=(), extra=(), skip=False):
        waits = list(extra)
        if start:
            waits += psb[bank].for_write()
        for n in rbufs:
            waits += B_(n).for_read()
        tok = k.emit(
            "tensor",
            lambda e, o=out_ap, l=lhsT, r=rhs, st=start, sp=stop, sk=skip:
                e.matmul(o, l, r, start=st, stop=sp, skip_group_check=sk),
            waits=waits, inc=(te, 1))
        if start:
            psb[bank].wrote(tok)
        else:
            psb[bank].w = tok
        for n in rbufs:
            B_(n).read(tok)
        return tok

    def t_ve(fn, wr=(), rd=(), banks_rd=(), extra=(), eng="vector"):
        sem = ve if eng == "vector" else se
        waits = list(extra)
        for n in rd:
            waits += B_(n).for_read()
        for n in wr:
            waits += B_(n).for_write()
        for b in banks_rd:
            if psb[b].w is not None:
                waits.append(psb[b].w)
        tok = k.emit(eng, fn, waits=waits, inc=(sem, 1))
        for n in rd:
            B_(n).read(tok)
        for n in wr:
            B_(n).wrote(tok)
        for b in banks_rd:
            psb[b].read(tok)
        return tok

    def outer_split(fast_src_ap, fast_name, slow_src_ap, slow_name, out3,
                    out_name, rd_extra=()):
        f3 = fast_src_ap.rearrange("p (h b) -> p h b", b=8)
        s3 = slow_src_ap.rearrange("p (h a) -> p h a", a=8)
        for aa in range(8):
            t_ve(lambda e, a_=aa: e.tensor_mul(
                out3[:, :, a_ * 8:(a_ + 1) * 8], f3,
                s3[:, :, a_:a_ + 1].broadcast_to([128, 16, 8])),
                wr=[out_name], rd=[fast_name, slow_name] + list(rd_extra))

    def ka():
        return

    def dbg_dump(name, src_ap, rd=(), banks_rd=()):
        if not dbg or name not in dbg_t:
            return
        scr = B_("dbgscr")
        shp = dbg_t[name].shape
        tmp = sb("dbgs_" + name, shp, F32)
        t_ve(lambda e, o=tmp, s=src_ap: e.tensor_copy(o[:], s),
             wr=["dbgs" + name], rd=list(rd), banks_rd=list(banks_rd))
        t_load("gpsimd", "do", dbg_t[name][:], tmp[:], rbufs=["dbgs" + name])

    def phase_consts():
        g = "gpsimd"
        t_load(g, "dg", p2_sb[:], d_p2[:], ["p2"])
        t_load(g, "dg", i128_sb[:], d_i128[:], ["i128"])
        t_load(g, "dg", tri_sb[:], d_tri[:], ["tri"])
        t_load(g, "dg", q2cl_sb[:], d_q2cl[:], ["q2cl"])
        t_load(g, "dg", repc_sb[:], d_repc[:], ["repc"])
        t_load(g, "dg", onesb_sb[:], d_onesbf[:], ["onesb"])
        t_load(g, "dg", onesf_sb[:], d_onesf32[:], ["onesf"])
        t_load(g, "dg", wqml_sb[:].rearrange("p (c j) -> p c j", j=128), d_wqml[:], ["wqml"])
        t_load(g, "dg", wkml_sb[:].rearrange("p (c j) -> p c j", j=128), d_wkml[:], ["wkml"])
        t_load(g, "dg", wul_sb[:].rearrange("p (c j) -> p c j", j=128), d_wul[:], ["wul"])
        t_load(g, "dg", vr_sb[:].rearrange("p (c j) -> p c j", j=32), d_vr[:], ["vr"])
        t_load(g, "dg", ct_sb[:], d_ct[:], ["ct"])
        t_load(g, "dg", b_sb[:], d_b[:], ["b"])
        t_load(g, "dg", h1_sb, d_h1[:], ["x1t"])
        t_load(g, "dg", h2_sb, d_h2[:], ["xt"])
        t_load(g, "dg", w74f_sb[:, 0:10], d_w74[0], ["w74f"])
        t_load(g, "dg", w74f_sb[:, 10:20], d_w74[1], ["w74f"])
        t_load(g, "dg", tmp1_sb[:], d_temp[None, :], ["tmpv"])

        t_ve(lambda e: e.memset(epsc_sb[:], LNEPS), wr=["epsc"])
        t_ve(lambda e: e.memset(bmat_sb[:], 0.0), wr=["bmat"])
        for e_i in range(8):
            t_ve(lambda e, i=e_i: e.tensor_mul(
                bscr_sb[:], h1_sb[:, i * 64:(i + 1) * 64],
                h2_sb[:, i * 64:(i + 1) * 64]),
                wr=["bscr"], rd=["x1t", "xt"])
            t_ve(lambda e: e.tensor_add(bmat_sb[:], bmat_sb[:], bscr_sb[:]),
                 wr=["bmat"], rd=["bscr", "bmat"])
        t_ve(lambda e: e.memset(mq_sb[:], 0.0), wr=["mq"])
        t_ve(lambda e: e.tensor_copy(mq_sb[0:64, 0:64], bmat_sb[:]),
             wr=["mq"], rd=["bmat"])
        t_ve(lambda e: e.tensor_copy(mq_sb[64:128, 64:128], bmat_sb[:]),
             wr=["mq"], rd=["bmat"])
        t_ve(lambda e: e.tensor_scalar_mul(w74_sb[:], w74f_sb[:], -2.0),
             wr=["w74"], rd=["w74f"])
        t_mm(7, ps[7][0:128, 0:1], onesf_sb[:], tmp1_sb[:], True, True,
             rbufs=["onesf", "tmpv"])
        t_ve(lambda e: e.tensor_scalar_max(negc_sb[:], ps[7][0:128, 0:1], 0.1),
             wr=["negc"], banks_rd=[7])
        t_ve(lambda e: e.reciprocal(negc_sb[:], negc_sb[:]),
             wr=["negc"], rd=["negc"])
        t_ve(lambda e: e.tensor_scalar_mul(negc_sb[:], negc_sb[:], -1.0),
             wr=["negc"], rd=["negc"])

    def ln_steps(src_name, src_ap, out_ap, out_name, par):
        sm = "stm%d" % par
        si = "sti%d" % par
        sn = "stn%d" % par
        ss = "sts%d" % par
        sqn = "stq%d" % par
        dummy = out_ap
        dn = out_name

        def s1():
            t_ve(lambda e: e.activation(
                dummy, src_ap, AF.Square, accum_out=st_sq[:, par:par + 1]),
                eng="scalar", wr=[dn, sqn], rd=[src_name])

        def s2():
            t_ve(lambda e: e.reduce_sum(st_sum[:, par:par + 1], src_ap, axis=AX),
                 wr=[ss], rd=[src_name])

        def s3():
            t_ve(lambda e: e.tensor_scalar_mul(
                st_mean[:, par:par + 1], st_sum[:, par:par + 1], 1.0 / D),
                wr=[sm], rd=[ss])

        def s4():
            t_ve(lambda e: e.tensor_mul(
                st_var[:, par:par + 1], st_mean[:, par:par + 1],
                st_mean[:, par:par + 1]), wr=["stv%d" % par], rd=[sm])

        def s5():
            t_ve(lambda e: e.scalar_tensor_tensor(
                st_var2[:, par:par + 1], st_sq[:, par:par + 1], 1.0 / D,
                st_var[:, par:par + 1], OP.mult, OP.subtract),
                wr=["st2%d" % par], rd=[sqn, "stv%d" % par])

        def s6():
            t_ve(lambda e: e.activation(
                st_inv[:, par:par + 1], st_var2[:, par:par + 1], AF.Ln,
                bias=epsc_sb[:]), eng="scalar", wr=[si],
                rd=["st2%d" % par, "epsc"])

        def s7():
            t_ve(lambda e: e.activation(
                st_inv[:, par:par + 1], st_inv[:, par:par + 1], AF.Exp,
                scale=-0.5), eng="scalar", wr=[si], rd=[si])

        def s8():
            t_ve(lambda e: e.scalar_tensor_tensor(
                st_nb[:, par:par + 1], st_mean[:, par:par + 1], -1.0,
                st_inv[:, par:par + 1], OP.mult, OP.mult),
                wr=[sn], rd=[sm, si])

        def s9():
            t_ve(lambda e: e.activation(
                out_ap, src_ap, AF.Identity, bias=st_nb[:, par:par + 1],
                scale=st_inv[:, par:par + 1]),
                eng="scalar", wr=[out_name], rd=[src_name, sn, si])

        return [s1, s2, s3, s4, s5, s6, s7, s8, s9]

    def ln_tile(src_name, src_ap, out_ap, out_name, par=0):
        for s in ln_steps(src_name, src_ap, out_ap, out_name, par):
            s()

    def phase_p2_tile(tile):
        sl = slice(tile * 128, (tile + 1) * 128)
        t_tr("scalar", kmtm_sb[:], kmT_sb[:, sl], wbufs=["kmtm"], rbufs=["kmT"])
        kx4 = kxtm_sb_ap.rearrange("p (h c) -> p h c", c=128)
        km3 = kmtm_sb[:].rearrange("p (h i) -> p h i", i=8)
        for aa in range(8):
            t_ve(lambda e, a_=aa: e.tensor_mul(
                kx4[:, :, a_ * 8:(a_ + 1) * 8],
                km3, km3[:, :, a_:a_ + 1].broadcast_to([128, 16, 8])),
                wr=["sA0", "sA1"], rd=["kmtm"])
        t_ve(lambda e: e.tensor_copy(kx4[:, :, 64:72], km3),
             wr=["sA0", "sA1"], rd=["kmtm"])
        t_ve(lambda e: e.memset(kx4[:, :, 72:73], 1.0), wr=["sA0", "sA1"])
        t_ve(lambda e: e.tensor_mul(sq_sb[:, 0:128], kmtm_sb[:], kmtm_sb[:]),
             wr=["sq"], rd=["kmtm"])
        t_ve(lambda e: e.reduce_sum(
            red1_sb[:, 0:16],
            sq_sb[:, 0:128].rearrange("p (h i) -> p h i", i=8), axis=AX),
            wr=["red1"], rd=["sq"])
        t_ve(lambda e: e.tensor_scalar(
            kx4[:, :, 73:74].rearrange("p h c -> p (h c)"),
            red1_sb[:, 0:16], EPS, None, OP.mult),
            wr=["sA0", "sA1"], rd=["red1"])
        src = kxtm_sb_ap.rearrange("t (h p) -> t h p", p=128)
        dst = kxt_sb[:].rearrange("p (h t) -> p h t", t=T)[
            :, :, tile * 128:(tile + 1) * 128]
        t_tr("sync", dst, src, wbufs=["kxt%d" % tile], rbufs=["sA0", "sA1"])
        ka()

    def rope_och(och, xnb, xnn):
        par = och % 2
        for c in range(8):
            t_mm(0, ps[0][0:128, :],
                 wqs_sb[:, par * 1024 + c * 128:par * 1024 + (c + 1) * 128],
                 xnb[:, c * 512:(c + 1) * 512],
                 start=(c == 0), stop=(c == 7), rbufs=["wqs%d" % par, xnn])
        t_ve(lambda e: e.tensor_copy(qT_sb[:], ps[0][0:128, :]),
             wr=["qTb"], banks_rd=[0])
        for c in range(8):
            t_mm(1, ps[1][0:128, :],
                 wks_sb[:, par * 1024 + c * 128:par * 1024 + (c + 1) * 128],
                 xnb[:, c * 512:(c + 1) * 512],
                 start=(c == 0), stop=(c == 7), rbufs=["wks%d" % par, xnn])
        t_ve(lambda e: e.tensor_copy(kT_sb[:], ps[1][0:128, :]),
             wr=["kTb"], banks_rd=[1])
        t_mm(2, ps[2][0:128, :], p2_sb[:], qT_sb[:],
             start=True, stop=True, rbufs=["p2", "qTb"])
        t_mm(5, ps[5][0:128, :], p2_sb[:], kT_sb[:],
             start=True, stop=True, rbufs=["p2", "kTb"])
        t_ve(lambda e: e.tensor_mul(qc_sb[:], qT_sb[:], c2_sb[:]),
             wr=["qcb"], rd=["qTb", "c2"])
        t_ve(lambda e: e.tensor_mul(qs_sb[:], ps[2][0:128, :], s2_sb[:]),
             wr=["qsb"], rd=["s2"], banks_rd=[2])
        t_ve(lambda e: e.tensor_mul(kc_sb[:], kT_sb[:], c2_sb[:]),
             wr=["kcb"], rd=["kTb", "c2"])
        t_ve(lambda e: e.tensor_mul(ks_sb[:], ps[5][0:128, :], s2_sb[:]),
             wr=["ksb"], rd=["s2"], banks_rd=[5])
        t_mm(3, ps[3][0:128, :], wqml_sb[:, och * 128:(och + 1) * 128], qc_sb[:],
             start=(och == 0), stop=False, rbufs=["wqml", "qcb"], skip=True)
        t_mm(3, ps[3][0:128, :], wqml_sb[:, och * 128:(och + 1) * 128], qs_sb[:],
             start=False, stop=(och == 7), rbufs=["wqml", "qsb"], skip=True)
        t_mm(4, ps[4][0:128, :], wkml_sb[:, och * 128:(och + 1) * 128], kc_sb[:],
             start=(och == 0), stop=False, rbufs=["wkml", "kcb"], skip=True)
        t_mm(4, ps[4][0:128, :], wkml_sb[:, och * 128:(och + 1) * 128], ks_sb[:],
             start=False, stop=(och == 7), rbufs=["wkml", "ksb"], skip=True)

    def ln_group_chunks(b, g):
        xnb = xnT_sb if g % 2 == 0 else quT_sb
        xnn = "xq" if g % 2 == 0 else "quT"
        chunks = []
        for half in range(2):
            plan = []
            for par in range(2):
                tl = half * 2 + par
                tile = g * 4 + tl
                rows = slice(tile * 128, (tile + 1) * 128)
                xbuf = xt_sb if par == 0 else x1t_sb
                xbn = "xt" if par == 0 else "x1t"
                onb = xn_sb[:] if par == 0 else rsA_sb[:, 0:1024]
                onn = "xn" if par == 0 else "rsAa"
                plan.append((rows, xbuf, xbn, onb, onn, par, tl))

            def c_load(plan=plan):
                for rows, xbuf, xbn, _, _, _, _ in plan:
                    t_load("sync", "di", xbuf[:], dx[b, rows, :], [xbn])
            chunks.append(c_load)
            stepss = []

            def c_mksteps(plan=plan, stepss=stepss):
                for _, xbuf, xbn, onb, onn, par, _ in plan:
                    stepss.append(ln_steps(xbn, xbuf[:], onb, onn, par))
            chunks.append(c_mksteps)
            for i in range(9):
                def c_step(i=i, stepss=stepss):
                    for st_ in stepss:
                        st_[i]()
                chunks.append(c_step)

            def c_tr(plan=plan, xnb=xnb, xnn=xnn):
                for _, _, _, onb, onn, _, tl in plan:
                    t_tr("scalar",
                         xnb[:].rearrange("p (c t) -> p c t", t=512)[
                             :, :, tl * 128:(tl + 1) * 128],
                         onb.rearrange("t (c p) -> t c p", p=128),
                         wbufs=[xnn], rbufs=[onn])
            chunks.append(c_tr)
        return chunks

    def phase_p1(b, m0_och=None, m0_tail=None):
        t_load("gpsimd", "dg", wvof_sb[:].rearrange("p (c o) -> p c o", o=1024),
               d_wv[:], ["wvof"])
        t_load("gpsimd", "dg", c2_sb[:], d_c2[:, 0:512], ["c2"])
        t_load("gpsimd", "dg", s2_sb[:], d_s2[:, 0:512], ["s2"])
        for c in ln_group_chunks(b, 0):
            c()
        for g in range(NG):
            xnb = xnT_sb if g % 2 == 0 else quT_sb
            xnn = "xq" if g % 2 == 0 else "quT"
            nxt = ln_group_chunks(b, g + 1) if g + 1 < NG else []
            ni = 0
            for och in range(8):
                par = och % 2
                t_load("gpsimd", "dg",
                       wqs_sb[:, par * 1024:(par + 1) * 1024].rearrange(
                           "p (c j) -> p c j", j=128),
                       d_wq[och], ["wqs%d" % par])
                t_load("gpsimd", "dg",
                       wks_sb[:, par * 1024:(par + 1) * 1024].rearrange(
                           "p (c j) -> p c j", j=128),
                       d_wk[och], ["wks%d" % par])
                rope_och(och, xnb, xnn)
                take = 4 if och < 6 else len(nxt) - ni
                for _ in range(max(0, take)):
                    if ni < len(nxt):
                        nxt[ni]()
                        ni += 1
                if g == NG - 1 and m0_och is not None:
                    m0_och(2)
            while ni < len(nxt):
                nxt[ni]()
                ni += 1
            if g == NG - 1 and m0_och is not None:
                m0_och(-1)
            T_sl = slice(g * 512, (g + 1) * 512)
            t_ve(lambda e, s=T_sl: e.activation(
                qmT_sb[:, s], ps[3][0:128, :], AF.Tanh, scale=0.5),
                eng="scalar", wr=["qmT"], banks_rd=[3])
            t_ve(lambda e, s=T_sl: e.tensor_scalar(
                qmT_sb[:, s], qmT_sb[:, s], 0.5, 0.5, OP.mult, OP.add),
                wr=["qmT"], rd=["qmT"])
            t_ve(lambda e, s=T_sl: e.activation(
                kmT_sb[:, s], ps[4][0:128, :], AF.Tanh, scale=0.5),
                eng="scalar", wr=["kmT"], banks_rd=[4])
            t_ve(lambda e, s=T_sl: e.tensor_scalar(
                kmT_sb[:, s], kmT_sb[:, s], 0.5, 0.5, OP.mult, OP.add),
                wr=["kmT"], rd=["kmT"])
            if g + 1 < NG:
                T_n = slice((g + 1) * 512, (g + 2) * 512)
                t_load("gpsimd", "dg", c2_sb[:], d_c2[:, T_n], ["c2"])
                t_load("gpsimd", "dg", s2_sb[:], d_s2[:, T_n], ["s2"])
            for tl in range(4):
                tile = g * 4 + tl
                for hf in range(2):
                    bank = 5 + hf
                    for c in range(8):
                        t_mm(bank, ps[bank][0:128, :],
                             xnb[:, c * 512 + tl * 128: c * 512 + (tl + 1) * 128],
                             wvof_sb[:, c * 1024 + hf * 512: c * 1024 + (hf + 1) * 512],
                             start=(c == 0), stop=(c == 7), rbufs=[xnn, "wvof"])
                    dstv = vtm_sb[:].rearrange(
                        "p (tt h j) -> p tt h j", tt=NT, h=H)[
                        :, tile, hf * 8:(hf + 1) * 8, 0:64]
                    t_ve(lambda e, d=dstv, bk=bank: e.tensor_copy(
                        d, ps[bk][0:128, :].rearrange("p (h j) -> p h j", j=64)),
                        wr=["vtm"], banks_rd=[bank])
                dstone = vtm_sb[:].rearrange(
                    "p (tt h j) -> p tt h j", tt=NT, h=H)[:, tile, :, 64:65]
                t_ve(lambda e, d=dstone: e.memset(d, 1.0), wr=["vtm"])
                if g == NG - 1 and tl == 3 and m0_tail is not None:
                    m0_tail()
                phase_p2_tile(tile)

    def metric_chunks(b, g, pre=False):
        T_sl = slice(g * 512, (g + 1) * 512)
        qm_ap, qm_n = qmtm_sb[:], "qmtm"
        ut_ap, ut_n = utm_sb[:], "utm"
        uT_ap, uT_n = uT_sb[:], "uT"
        if pre:
            homes = [(scrA_sb[:, 2048:3072], "sA2"),
                     (scrA_sb[:, 3072:4096], "sA3")]
        else:
            homes = [(rsA_sb[:, 0:1024], "rsAa"),
                     (rsA_sb[:, 1024:2048], "rsAb")]
        qq_homes = homes
        qu_homes = homes
        qa_ap, qa_n = homes[0]
        ubank = 2
        early = []
        late = []

        def c_qmtr():
            t_tr("scalar", qm_ap.rearrange("p (tt t) -> p tt t", t=128),
                 qmT_sb[:, T_sl].rearrange("p (tt t) -> p tt t", t=128),
                 wbufs=[qm_n], rbufs=["qmT"])
        early.append(c_qmtr)
        for tl in range(4):
            def c_qqo(tl=tl):
                qqr, qqn = qq_homes[tl % 2]
                outer_split(qm_ap[:, tl * 128:(tl + 1) * 128], qm_n,
                            qm_ap[:, tl * 128:(tl + 1) * 128], qm_n,
                            qqr.rearrange("p (h c) -> p h c", c=64), qqn)
            late.append(c_qqo)

            def c_qqt(tl=tl):
                qqr, qqn = qq_homes[tl % 2]
                t_tr("scalar",
                     qq1T_sb[:].rearrange("p (c t) -> p c t", t=512)[
                         :, :, tl * 128:(tl + 1) * 128],
                     qqr.rearrange("t (c p) -> t c p", p=128),
                     wbufs=["xq"], rbufs=[qqn])
            late.append(c_qqt)

        def c_umm():
            for c in range(8):
                t_mm(ubank, ps[ubank][0:128, :],
                     wul_sb[:, c * 128:(c + 1) * 128],
                     qq1T_sb[:, c * 512:(c + 1) * 512],
                     start=(c == 0), stop=(c == 7), rbufs=["wul", "xq"])
            t_ve(lambda e: e.tensor_copy(uT_ap, ps[ubank][0:128, :]),
                 wr=[uT_n], banks_rd=[ubank])
        late.append(c_umm)

        def c_utr():
            t_tr("scalar", ut_ap.rearrange("p (tt t) -> p tt t", t=128),
                 uT_ap.rearrange("p (tt t) -> p tt t", t=128),
                 wbufs=[ut_n], rbufs=[uT_n])
        late.append(c_utr)

        def c_sq():
            t_ve(lambda e: e.tensor_mul(sq_sb[:], ut_ap, ut_ap),
                 wr=["sq"], rd=[ut_n])
            t_ve(lambda e: e.reduce_sum(
                red1_sb[:], sq_sb[:].rearrange("p (th i) -> p th i", i=8),
                axis=AX), wr=["red1"], rd=["sq"])
            t_ve(lambda e: e.tensor_mul(sq_sb[:], qm_ap, qm_ap),
                 wr=["sq"], rd=[qm_n])
            t_ve(lambda e: e.reduce_sum(
                red2_sb[:], sq_sb[:].rearrange("p (th i) -> p th i", i=8),
                axis=AX), wr=["red2"], rd=["sq"])
        late.append(c_sq)

        def c_qaux():
            qa5 = qa_ap.rearrange(
                "p (c tt l i) -> p c tt l i", c=2, tt=4, i=16)
            for c2 in range(2):
                t_ve(lambda e, c_=c2: e.scalar_tensor_tensor(
                    qa5[:, c_, :, :, 8:9].squeeze(3),
                    red2_sb[:].rearrange("p (tt c l) -> p tt c l", tt=4, c=2)[
                        :, :, c_, :],
                    EPS,
                    red1_sb[:].rearrange("p (tt c l) -> p tt c l", tt=4, c=2)[
                        :, :, c_, :],
                    OP.mult, OP.add),
                    wr=[qa_n], rd=["red1", "red2"])
                for tl in range(4):
                    t_ve(lambda e, c_=c2, t_=tl: e.tensor_copy(
                        qa5[:, c_, t_, :, 0:8],
                        qm_ap.rearrange(
                            "p (tt c l i) -> p tt c l i", tt=4, c=2, i=8)[
                            :, t_, c_, :, :]),
                        wr=[qa_n], rd=[qm_n])
                t_ve(lambda e, c_=c2: e.memset(
                    qa5[:, c_, :, :, 9:10].squeeze(3), 1.0), wr=[qa_n])

        for tl in range(4):
            def c_quo(tl=tl):
                qur, qun = qu_homes[tl % 2]
                outer_split(ut_ap[:, tl * 128:(tl + 1) * 128], ut_n,
                            qm_ap[:, tl * 128:(tl + 1) * 128], qm_n,
                            qur.rearrange("p (h c) -> p h c", c=64), qun)
            late.append(c_quo)
            def c_qut(tl=tl):
                qur, qun = qu_homes[tl % 2]
                t_tr("scalar",
                     quT_sb[:].rearrange("p (c t) -> p c t", t=512)[
                         :, :, tl * 128:(tl + 1) * 128],
                     qur.rearrange("t (c p) -> t c p", p=128),
                     wbufs=["quT"], rbufs=[qun])
            late.append(c_qut)
        late.append(c_qaux)

        def c_qat():
            for c2 in range(2):
                t_tr("scalar",
                     qaT_sb[:, c2 * 512:(c2 + 1) * 512].rearrange(
                         "p (tt q) -> p tt q", q=128),
                     qa_ap[:, c2 * 512:(c2 + 1) * 512].rearrange(
                         "t (tt p) -> t tt p", p=128),
                     wbufs=["qaT"], rbufs=[qa_n])
        late.append(c_qat)
        return early, late

    def phase_p4(b, g, nxt_early, mfeed):
        nch = 4 * (g + 1)

        def qf_build(h):
            ch = h // 2
            par = h % 2
            qfs = h % 2
            qf_ap = qf_sb[:, qfs * 512:(qfs + 1) * 512]
            qfn = "qf%d" % qfs
            t_mm(2, ps[2][0:64, :], mq_sb[:, par * 64:(par + 1) * 64],
                 qq1T_sb[:, ch * 512:(ch + 1) * 512],
                 start=True, stop=True, rbufs=["mq", "xq"], skip=True)
            t_mm(2, ps[2][64:74, :], w74_sb[:, par * 10:(par + 1) * 10],
                 quT_sb[:, ch * 512:(ch + 1) * 512],
                 start=True, stop=False, rbufs=["w74", "quT"], skip=True)
            t_mm(2, ps[2][64:74, :], q2cl_sb[:, h * 10:(h + 1) * 10],
                 qaT_sb[:, (h // 8) * 512:(h // 8 + 1) * 512],
                 start=False, stop=True, rbufs=["q2cl", "qaT"], skip=True)
            t_ve(lambda e, o=qf_ap: e.tensor_copy(o[0:74, :], ps[2][0:74, :]),
                 wr=[qfn], banks_rd=[2])

        ZB = (0, 1, 3, 5)
        ei = 0
        qf_build(0)
        for h in range(H):
            qfn = "qf%d" % (h % 2)
            qf_ap = qf_sb[:, (h % 2) * 512:(h % 2 + 1) * 512]
            av_bank = 6 + (h % 2)

            def z_tile(c):
                zb = ZB[c % 4]
                diag = (c >= 4 * g)
                t_mm(zb, ps[zb][0:128, :],
                     kxt_sb[0:74, h * T + c * 128: h * T + (c + 1) * 128],
                     qf_ap[0:74, :],
                     start=True, stop=(not diag),
                     rbufs=["kxt%d" % c, qfn], skip=True)
                if diag:
                    lo = c * 128 - g * 512
                    t_mm(zb, ps[zb][0:128, lo:lo + 128], i128_sb[:], tri_sb[:],
                         start=False, stop=True, rbufs=["i128", "tri"], skip=True)
                es = c % 4
                e_ap = e_sb[:, es * 512:(es + 1) * 512]
                en = "e%d" % es
                lo = max(0, c * 128 - g * 512)
                if lo > 0:
                    t_ve(lambda e, a=e_ap, l=lo: e.memset(a[:, 0:l], 0.0),
                         wr=[en])
                t_ve(lambda e, a=e_ap, l=lo, bk=zb: e.activation(
                    a[:, l:512], ps[bk][0:128, l:512], AF.Exp, scale=negc_sb[:]),
                    eng="scalar", wr=[en], rd=["negc"], banks_rd=[zb])
                t_ve(lambda e, a=e_ap, l=lo: e.tensor_scalar_min(
                    a[:, l:512], a[:, l:512], 1.0), wr=[en], rd=[en])

            def av_tile(c):
                en = "e%d" % (c % 4)
                e_ap = e_sb[:, (c % 4) * 512:(c % 4 + 1) * 512]
                vsl = vtm_sb[:].rearrange(
                    "p (tt hh j) -> p tt hh j", tt=NT, hh=H)[:, c, h, :]
                t_mm(av_bank, ps[av_bank][0:65, :], vsl, e_ap,
                     start=(c == 0), stop=(c == nch - 1),
                     rbufs=["vtm", en], skip=True)

            for c in range(nch):
                z_tile(c)
                if c == 1 and h + 1 < H:
                    qf_build(h + 1)
                if c >= 1:
                    av_tile(c - 1)
            av_tile(nch - 1)
            if h >= 2 and ei < len(nxt_early):
                nxt_early[ei]()
                ei += 1
            rr = (h // 4) * 32
            rc = (h % 4) * 512
            t_ve(lambda e, r=rr, cc=rc, bk=av_bank: e.tensor_copy(
                rsA_sb[r:r + 1, cc:cc + 512], ps[bk][64:65, :]),
                wr=["rsAa" if (h % 4) < 2 else "rsAb"], banks_rd=[av_bank])
            ch = h // 2
            base = (h % 2) * 64
            t_ve(lambda e, bk=av_bank, c_=ch, b_=base: e.tensor_copy(
                au_sb[b_:b_ + 64, c_ * 512:(c_ + 1) * 512], ps[bk][0:64, :]),
                wr=["sA%d" % (ch // 2)], banks_rd=[av_bank])
        while ei < len(nxt_early):
            nxt_early[ei]()
            ei += 1
        srcr = rsA_sb[0:128:32, :].rearrange("r (s t) -> r s t", s=4)
        tok1 = t_load("sync", "di", d_rs[:].rearrange("(r s) t -> r s t", r=4),
                      srcr, rbufs=["rsAa", "rsAb"], wbufs=["drs"])
        t_load("sync", "di", rsT_sb[:], d_rs[:], wbufs=["rsT"], rbufs=["drs"],
               extra=[tok1])
        mfeed(1)
        rcp_ap = yt_sb[0:16, :]
        t_ve(lambda e, o=rcp_ap: e.reciprocal(o, rsT_sb[:]),
             wr=["yt"], rd=["rsT"])
        t_ve(lambda e, i=rcp_ap: e.tensor_copy(rcpb_sb, i),
             wr=["rep"], rd=["yt"])
        for c in range(8):
            t_mm(2, ps[2][0:128, :], repc_sb[:, c * 128:(c + 1) * 128],
                 rcpb_sb, start=True, stop=True, rbufs=["repc", "rep"])
            t_ve(lambda e: e.tensor_copy(spr2_ap, ps[2][0:128, :]),
                 wr=["uT"], banks_rd=[2])
            t_ve(lambda e, c_=c: e.tensor_mul(
                au_sb[:, c_ * 512:(c_ + 1) * 512],
                au_sb[:, c_ * 512:(c_ + 1) * 512], spr2_ap),
                wr=["sA%d" % (c // 2)], rd=["sA%d" % (c // 2), "uT"])
            mfeed(1)

    def phase_p5(b, g, mfeed):
        for tl in range(4):
            tile = g * 4 + tl
            rows = slice(tile * 128, (tile + 1) * 128)
            t_load("sync", "di", xorig_sb[:], dx[b, rows, :], ["xt"])
            for hf in range(2):
                bank = 3 + hf
                for c in range(8):
                    t_mm(bank, ps[bank][0:128, :],
                         au_sb[:, c * 512 + tl * 128: c * 512 + (tl + 1) * 128],
                         wvof_sb[:, c * 1024 + hf * 512: c * 1024 + (hf + 1) * 512],
                         start=(c == 0), stop=(c == 7),
                         rbufs=["sA%d" % (c // 2), "wvof"])
                t_ve(lambda e, bk=bank, h_=hf: e.scalar_tensor_tensor(
                    x1t_sb[:, h_ * 512:(h_ + 1) * 512], ps[bk][0:128, :], 1.0,
                    xorig_sb[:, h_ * 512:(h_ + 1) * 512], OP.mult, OP.add),
                    wr=["x1t"], rd=["xt"], banks_rd=[bank])
                mfeed(1)
            t_load("gpsimd", "dg", d_x1[b, rows, :], x1t_sb[:], rbufs=["x1t"])
            ln_tile("x1t", x1t_sb[:], xn_sb[:], "xn", par=0)
            t_load("gpsimd", "dg", d_xnt[g, tl],
                   xn_sb[:], wbufs=["dxnt%d" % g], rbufs=["xn"])
            mfeed(1)
        mfeed(-1)

    def p6_prefetch(b, g, use_e=False):
        xnb = xnT_sb if g % 2 == 0 else quT_sb
        xnn = "xq" if g % 2 == 0 else "quT"
        if use_e:
            scrs = [(e_sb[:, 0:1024], ["e0", "e1"]),
                    (e_sb[:, 1024:2048], ["e2", "e3"])] * 2
        else:
            scrs = [(xn_sb[:], ["xn"]), (rsA_sb[:, 0:1024], ["rsAa"]),
                    (rsA_sb[:, 1024:2048], ["rsAb"]),
                    (scrA_sb[:, 0:1024], ["sA0"])]
        for tl in range(4):
            scr, scrn = scrs[(g % 2) * 2 + (tl % 2)]
            t_load("sync", "di", scr, d_xnt[g, tl],
                   wbufs=scrn, rbufs=["dxnt%d" % g])
            t_tr("scalar",
                 xnb[:].rearrange("p (c t) -> p c t", t=512)[
                     :, :, tl * 128:(tl + 1) * 128],
                 scr.rearrange("t (c p) -> t c p", p=128),
                 wbufs=[xnn], rbufs=scrn)

    def phase_p6(b, g):
        xnb = xnT_sb if g % 2 == 0 else quT_sb
        xnn = "xq" if g % 2 == 0 else "quT"
        ka()
        for c in range(8):
            t_mm(2, ps[2][0:32, :], vr_sb[:, c * 32:(c + 1) * 32],
                 xnb[:, c * 512:(c + 1) * 512],
                 start=(c == 0), stop=(c == 7), rbufs=["vr", xnn])
        t_ve(lambda e: e.tensor_copy(z1T_sb, ps[2][0:32, :]),
             wr=["rep"], banks_rd=[2])
        for tl in range(4):
            tile = g * 4 + tl
            rows = slice(tile * 128, (tile + 1) * 128)
            par = tl % 2
            xbuf = x1t_sb if par == 0 else xt_sb
            xbn = "x1t" if par == 0 else "xt"
            t_load("sync", "di", xbuf[:], d_x1[b, rows, :], wbufs=[xbn],
                   extra=k.qall("dg"))
            for hf in range(2):
                bank = 3 + hf
                for c in range(8):
                    t_mm(bank, ps[bank][0:128, :],
                         xnb[:, c * 512 + tl * 128: c * 512 + (tl + 1) * 128],
                         wvof_sb[:, c * 1024 + hf * 512: c * 1024 + (hf + 1) * 512],
                         start=(c == 0), stop=False, rbufs=[xnn, "wvof"], skip=True)
                t_mm(bank, ps[bank][0:128, :],
                     z1T_sb[0:32, tl * 128:(tl + 1) * 128],
                     ct_sb[0:32, hf * 512:(hf + 1) * 512],
                     start=False, stop=False, rbufs=["rep", "ct"], skip=True)
                t_mm(bank, ps[bank][0:128, :], onesb_sb[:],
                     b_sb[:, hf * 512:(hf + 1) * 512],
                     start=False, stop=True, rbufs=["onesb", "b"], skip=True)
                t_ve(lambda e, bk=bank, h_=hf, xb=xbuf: e.scalar_tensor_tensor(
                    yt_sb[:], ps[bk][0:128, :], 1.0,
                    xb[:, h_ * 512:(h_ + 1) * 512], OP.mult, OP.add),
                    wr=["yt"], rd=[xbn], banks_rd=[bank])
                tok = k.dma("sync", "do", dy[b, rows, hf * 512:(hf + 1) * 512],
                            yt_sb[:], waits=B_("yt").for_write())
                B_("yt").read(tok)
        if g + 2 < NG:
            p6_prefetch(b, g + 2)

    def mk_feeder(chunks):
        st = {"i": 0}
        def feed(n):
            if n < 0:
                n = len(chunks)
            while n > 0 and st["i"] < len(chunks):
                chunks[st["i"]]()
                st["i"] += 1
                n -= 1
        return feed

    phase_consts()
    for b in range(NSEQ):
        me, ml = metric_chunks(b, 0, pre=True)
        m0_och = mk_feeder(me + ml[:11])
        ml_tail = ml[11:]

        def m0_tail(ml_tail=ml_tail):
            for c_fn in ml_tail:
                c_fn()
        phase_p1(b, m0_och, m0_tail)
        t_load("gpsimd", "dg", wvof_sb[:].rearrange("p (c o) -> p c o", o=1024),
               d_wo[:], ["wvof"])
        for g in range(NG):
            if g + 1 < NG:
                ne, nl = metric_chunks(b, g + 1)
            else:
                ne, nl = [], []
            mf = mk_feeder(nl)
            phase_p4(b, g, ne, mf)
            if g == NG - 1:
                p6_prefetch(b, 0, use_e=True)
                p6_prefetch(b, 1, use_e=True)
            phase_p5(b, g, mf)
        t_load("gpsimd", "dg", wvof_sb[:].rearrange("p (c o) -> p c o", o=1024),
               d_wf[:], ["wvof"])
        for g in range(NG):
            phase_p6(b, g)
    k.emit("sync", lambda e: e.nop(), waits=tuple(k.qall("do")))
    k.build_block()
    return nc



def kernel(**inputs):
    consts = _host_consts()
    lays = _host_layouts(inputs)
    x = np.ascontiguousarray(np.asarray(inputs["L0_attn_x"], np.float32))

    nc = build_nc(dbg=_DEBUG)
    shared = {}
    shared.update({kk: np.ascontiguousarray(v) for kk, v in consts.items()})
    shared.update({kk: np.ascontiguousarray(v) for kk, v in lays.items()})
    in_maps = []
    for ci in range(NCORE):
        m = dict(shared)
        m["x"] = x[ci * NSEQ:(ci + 1) * NSEQ]
        in_maps.append(m)
    res = run_bass_kernel_spmd(nc, in_maps, core_ids=list(range(NCORE)))
    out = np.empty((B_TOT, T, D), np.float32)
    for ci in range(NCORE):
        out[ci * NSEQ:(ci + 1) * NSEQ] = res.results[ci]["y"]
    if _DEBUG:
        kernel._dbg = [{kk: np.asarray(v) for kk, v in r.items() if kk.startswith("dbg_")}
                       for r in res.results]
    return out


# version 33
# speedup vs baseline: 1.0016x; 1.0016x over previous
import numpy as np
import ml_dtypes

import concourse.bass as bass
import concourse.mybir as mybir
from concourse.bass_utils import run_bass_kernel_spmd

F32 = mybir.dt.float32
BF16 = mybir.dt.bfloat16
AX = mybir.AxisListType.X
OP = mybir.AluOpType
AF = mybir.ActivationFunctionType
BFNP = ml_dtypes.bfloat16

NCORE = 8
B_TOT = 16
NSEQ = 2
T = 2048
D = 1024
H = 16
NT = 16
NG = 4
EPS = 1e-3
LNEPS = 1e-5
BIG = float(2.0 ** 100)

_DEBUG = False
MHOMES = {}



def _host_consts():
    c = {}
    inv_freq = 1.0 / 10000.0 ** (np.arange(0, 64, 2, dtype=np.float32) / 64)
    t = np.arange(T, dtype=np.float32)
    freqs = np.outer(t, inv_freq)
    cos = np.cos(freqs).astype(np.float32)
    sin = np.sin(freqs).astype(np.float32)
    C2 = np.zeros((128, T), np.float32)
    S2 = np.zeros((128, T), np.float32)
    for h2 in range(2):
        for j in range(64):
            C2[h2 * 64 + j] = cos[:, j % 32]
            S2[h2 * 64 + j] = -sin[:, j % 32] if j < 32 else sin[:, j - 32]
    c["c2"] = C2.astype(BFNP)
    c["s2"] = S2.astype(BFNP)

    P2 = np.zeros((128, 128), np.float32)
    for h2 in range(2):
        for i in range(64):
            P2[h2 * 64 + ((i + 32) % 64), h2 * 64 + i] = 1.0
    c["p2"] = P2.astype(BFNP)
    c["i128"] = np.eye(128, dtype=np.float32).astype(BFNP)
    c["tri"] = np.where(np.arange(128)[:, None] > np.arange(128)[None, :],
                        BIG, 0.0).astype(np.float32).astype(BFNP)

    q2cl = np.zeros((16, 128, 10), np.float32)
    for h in range(16):
        hl = h % 8
        for d in range(8):
            q2cl[h, hl * 16 + d, d] = -2.0 * EPS
        q2cl[h, hl * 16 + 8, 8] = 1.0
        q2cl[h, hl * 16 + 9, 9] = 1.0
    c["q2cl"] = np.ascontiguousarray(
        q2cl.transpose(1, 0, 2).reshape(128, 160)).astype(BFNP)

    rep = np.zeros((8, 16, 128), np.float32)
    for cc in range(8):
        rep[cc, 2 * cc, 0:64] = 1.0
        rep[cc, 2 * cc + 1, 64:128] = 1.0
    c["repc"] = np.ascontiguousarray(
        rep.transpose(1, 0, 2).reshape(16, 1024)).astype(BFNP)

    c["onesbf"] = np.ones((1, 128), BFNP)
    c["onesf32"] = np.ones((1, 128), np.float32)
    return c


def _host_layouts(ins):
    a = {}
    Wq = np.asarray(ins["L0_attn_Wq"], np.float32)
    Wk = np.asarray(ins["L0_attn_Wk"], np.float32)
    Wv = np.asarray(ins["L0_attn_Wv"], np.float32)
    Wo = np.asarray(ins["L0_attn_Wo"], np.float32)
    Wqm = np.asarray(ins["L0_attn_Wqm"], np.float32)
    Wkm = np.asarray(ins["L0_attn_Wkm"], np.float32)
    Wm = np.asarray(ins["L0_attn_W_metric"], np.float32)
    Wf = np.asarray(ins["L0_ffn_W"], np.float32)
    Vr = np.asarray(ins["L0_ffn_V_r"], np.float32)
    Cf = np.asarray(ins["L0_ffn_C"], np.float32)
    bf = np.asarray(ins["L0_ffn_b"], np.float32)

    def slab(W):
        return np.ascontiguousarray(
            W.reshape(8, 128, 8, 128).transpose(2, 1, 0, 3))

    a["wq_sl"] = slab(Wq)
    a["wk_sl"] = slab(Wk)
    a["wv_r"] = np.ascontiguousarray(Wv.reshape(8, 128, D).transpose(1, 0, 2))
    a["wo_r"] = np.ascontiguousarray(Wo.reshape(8, 128, D).transpose(1, 0, 2))
    a["wf_r"] = np.ascontiguousarray(
        np.ascontiguousarray(Wf.T).reshape(8, 128, D).transpose(1, 0, 2))
    a["vr_r"] = np.ascontiguousarray(Vr.reshape(8, 128, 32).transpose(1, 0, 2))
    a["ct_r"] = np.ascontiguousarray(Cf.T)
    a["b_r"] = np.ascontiguousarray(bf.reshape(1, D))

    def mlift(W8):
        L = np.zeros((D, 128), np.float32)
        for h in range(16):
            L[h * 64:(h + 1) * 64, h * 8:(h + 1) * 8] = W8
        return np.ascontiguousarray(L.reshape(8, 128, 128).transpose(1, 0, 2))

    a["wqm_l"] = mlift(Wqm)
    a["wkm_l"] = mlift(Wkm)
    Wu = np.zeros((64, 8), np.float32)
    for aa in range(8):
        for bb in range(8):
            Wu[aa * 8 + bb, :] = Wm[aa, bb * 8: bb * 8 + 8]
    a["wu_l"] = mlift(Wu)

    w74 = np.zeros((2, 128, 10), np.float32)
    for par in range(2):
        for m in range(8):
            for e in range(8):
                for d in range(8):
                    w74[par, par * 64 + m * 8 + e, d] = Wm[m, d * 8 + e]
    a["wm3r74"] = np.ascontiguousarray(w74)

    W3 = Wm.reshape(8, 8, 8)
    H1 = np.zeros((64, 8, 64), np.float32)
    H2 = np.zeros((64, 8, 64), np.float32)
    ab_a = np.arange(64) // 8
    ab_b = np.arange(64) % 8
    xy_x = np.arange(64) // 8
    xy_y = np.arange(64) % 8
    for e in range(8):
        H1[:, e, :] = W3[ab_a][:, xy_x, e]
        H2[:, e, :] = W3[ab_b][:, xy_y, e]
    a["h1"] = np.ascontiguousarray(H1.reshape(64, 512))
    a["h2"] = np.ascontiguousarray(H2.reshape(64, 512))

    a["temp"] = np.asarray(ins["L0_attn_temperature"], np.float32).reshape(1)
    return a



ENGS = ("sync", "vector", "scalar", "tensor", "gpsimd")


class Sem:
    def __init__(self, h, dma=False):
        self.h = h
        self.n = 0
        self.dma = dma

    def bump(self, kk):
        self.n += kk
        return self.n


class Buf:
    def __init__(self):
        self.w = None
        self.r = []

    def for_write(self):
        out = list(self.r)
        if self.w is not None:
            out.append(self.w)
        return out

    def wrote(self, tok):
        self.w = tok
        self.r = []

    def for_read(self):
        return [self.w] if self.w is not None else []

    def read(self, tok):
        self.r.append(tok)


NQSEM = 4


class K:
    def __init__(self, nc):
        self.nc = nc
        self.q = {e: [] for e in ENGS}
        self.sems = {}
        self.qsems = {}
        self.qcnt = {}
        for name in ("te", "ve", "se"):
            self.sems[name] = Sem(nc.alloc_semaphore(name=name))
        for name in ("di", "dt", "dts", "dg", "do"):
            self.qsems[name] = [Sem(nc.alloc_semaphore(name=f"{name}{i}"),
                                    dma=True) for i in range(NQSEM)]
            self.qcnt[name] = 0

    def qall(self, name):
        return [(s, s.n) for s in self.qsems[name] if s.n > 0]

    def emit(self, eng, fn, waits=(), inc=None):
        wd = {}
        for s, t in waits:
            if s.dma:
                t = s.n
            if s not in wd or t > wd[s]:
                wd[s] = t
        witems = tuple(wd.items())
        tok = None
        if inc is not None:
            tok = (inc[0], inc[0].bump(inc[1]))

        def g(e, fn=fn, witems=witems, inc=inc):
            for s, t in witems:
                e.wait_ge(s.h, t)
            ins = fn(e)
            if inc is not None and ins is not None:
                ins.then_inc(inc[0].h, inc[1])

        self.q[eng].append(g)
        return tok

    def _qsem(self, semn):
        s = self.qsems[semn][self.qcnt[semn] % NQSEM]
        self.qcnt[semn] += 1
        return s

    def dma(self, eng, semn, out_ap, in_ap, waits=()):
        return self.emit(eng, lambda e, o=out_ap, i=in_ap:
                         e.dma_start(out=o, in_=i),
                         waits=waits, inc=(self._qsem(semn), 16))

    def dma_t(self, eng, semn, out_ap, in_ap, waits=()):
        return self.emit(eng, lambda e, o=out_ap, i=in_ap:
                         e.dma_start(out=o, in_=i, transpose=True),
                         waits=waits, inc=(self._qsem(semn), 16))

    def build_block(self):
        nc = self.nc
        with nc.Block() as block:
            @block.sync
            def _(sync):
                for f in self.q["sync"]:
                    f(sync)

            @block.vector
            def _(vector):
                for f in self.q["vector"]:
                    f(vector)

            @block.scalar
            def _(scalar):
                for f in self.q["scalar"]:
                    f(scalar)

            @block.tensor
            def _(tensor):
                for f in self.q["tensor"]:
                    f(tensor)

            @block.gpsimd
            def _(gpsimd):
                for f in self.q["gpsimd"]:
                    f(gpsimd)



def build_nc(dbg=False):
    nc = bass.Bass("TRN2", target_bir_lowering=False)
    k = K(nc)
    te, ve, se = k.sems["te"], k.sems["ve"], k.sems["se"]

    dx = nc.dram_tensor("x", [NSEQ, T, D], F32, kind="ExternalInput")
    dy = nc.dram_tensor("y", [NSEQ, T, D], F32, kind="ExternalOutput")
    d_c2 = nc.dram_tensor("c2", [128, T], BF16, kind="ExternalInput")
    d_s2 = nc.dram_tensor("s2", [128, T], BF16, kind="ExternalInput")
    d_p2 = nc.dram_tensor("p2", [128, 128], BF16, kind="ExternalInput")
    d_i128 = nc.dram_tensor("i128", [128, 128], BF16, kind="ExternalInput")
    d_tri = nc.dram_tensor("tri", [128, 128], BF16, kind="ExternalInput")
    d_q2cl = nc.dram_tensor("q2cl", [128, 160], BF16, kind="ExternalInput")
    d_repc = nc.dram_tensor("repc", [16, 1024], BF16, kind="ExternalInput")
    d_onesbf = nc.dram_tensor("onesbf", [1, 128], BF16, kind="ExternalInput")
    d_onesf32 = nc.dram_tensor("onesf32", [1, 128], F32, kind="ExternalInput")
    d_wq = nc.dram_tensor("wq_sl", [8, 128, 8, 128], F32, kind="ExternalInput")
    d_wk = nc.dram_tensor("wk_sl", [8, 128, 8, 128], F32, kind="ExternalInput")
    d_wv = nc.dram_tensor("wv_r", [128, 8, 1024], F32, kind="ExternalInput")
    d_wo = nc.dram_tensor("wo_r", [128, 8, 1024], F32, kind="ExternalInput")
    d_wf = nc.dram_tensor("wf_r", [128, 8, 1024], F32, kind="ExternalInput")
    d_vr = nc.dram_tensor("vr_r", [128, 8, 32], F32, kind="ExternalInput")
    d_ct = nc.dram_tensor("ct_r", [32, 1024], F32, kind="ExternalInput")
    d_b = nc.dram_tensor("b_r", [1, D], F32, kind="ExternalInput")
    d_wqml = nc.dram_tensor("wqm_l", [128, 8, 128], F32, kind="ExternalInput")
    d_wkml = nc.dram_tensor("wkm_l", [128, 8, 128], F32, kind="ExternalInput")
    d_wul = nc.dram_tensor("wu_l", [128, 8, 128], F32, kind="ExternalInput")
    d_w74 = nc.dram_tensor("wm3r74", [2, 128, 10], F32, kind="ExternalInput")
    d_h1 = nc.dram_tensor("h1", [64, 512], F32, kind="ExternalInput")
    d_h2 = nc.dram_tensor("h2", [64, 512], F32, kind="ExternalInput")
    d_temp = nc.dram_tensor("temp", [1], F32, kind="ExternalInput")
    d_x1 = nc.dram_tensor("x1scr", [NSEQ, T, D], F32)
    d_rs = nc.dram_tensor("rsscr", [16, 512], BF16)
    d_xnt = nc.dram_tensor("xntscr", [NG, 4, 128, 1024], BF16)

    dbg_t = {}
    if dbg:
        dbg_t["rsA"] = nc.dram_tensor("dbg_rsA", [NSEQ, NG, 128, 2048], BF16,
                                      kind="ExternalOutput")
        dbg_t["rsT"] = nc.dram_tensor("dbg_rsT", [NSEQ, NG, 16, 512], BF16,
                                      kind="ExternalOutput")
        dbg_t["auU"] = nc.dram_tensor("dbg_auU", [NSEQ, NG, 128, 4096], BF16,
                                      kind="ExternalOutput")
        dbg_t["auN"] = nc.dram_tensor("dbg_auN", [NSEQ, NG, 128, 4096], BF16,
                                      kind="ExternalOutput")
        dbg_t["qf"] = nc.dram_tensor("dbg_qf", [NSEQ, NG, H, 74, 512], BF16,
                                     kind="ExternalOutput")

    sb = nc.alloc_sbuf_tensor
    wvof_sb = sb("wvof_sb", [128, 8 * 1024], BF16)
    wqml_sb = sb("wqml_sb", [128, 8 * 128], BF16)
    wkml_sb = sb("wkml_sb", [128, 8 * 128], BF16)
    wul_sb = sb("wul_sb", [128, 8 * 128], BF16)
    vr_sb = sb("vr_sb", [128, 8 * 32], BF16)
    ct_sb = sb("ct_sb", [32, 1024], BF16)
    b_sb = sb("b_sb", [1, D], BF16)
    c2_sb = sb("c2_sb", [128, 512], BF16)
    s2_sb = sb("s2_sb", [128, 512], BF16)
    p2_sb = sb("p2_sb", [128, 128], BF16)
    i128_sb = sb("i128_sb", [128, 128], BF16)
    tri_sb = sb("tri_sb", [128, 128], BF16)
    q2cl_sb = sb("q2cl_sb", [128, 160], BF16)
    repc_sb = sb("repc_sb", [16, 1024], BF16)
    onesb_sb = sb("onesb_sb", [1, 128], BF16)
    onesf_sb = sb("onesf_sb", [1, 128], F32)
    w74_sb = sb("w74_sb", [128, 20], BF16)
    mq_sb = sb("mq_sb", [128, 128], BF16)
    negc_sb = sb("negc_sb", [128, 1], F32)
    epsc_sb = sb("epsc_sb", [128, 1], F32)
    bmat_sb = sb("bmat_sb", [64, 64], F32)
    bscr_sb = sb("bscr_sb", [64, 64], F32)
    w74f_sb = sb("w74f_sb", [128, 20], F32)
    tmp1_sb = sb("tmp1_sb", [1, 1], F32)
    wqs_sb = sb("wqs_sb", [128, 2 * 1024], BF16)
    wks_sb = sb("wks_sb", [128, 2 * 1024], BF16)
    kxt_sb = sb("kxt_sb", [128, H * T], BF16, is_transpose_dest=True)
    vtm_sb = sb("vtm_sb", [128, NT * 1040], BF16)
    qmT_sb = sb("qmT_sb", [128, T], BF16)
    kmT_sb = sb("kmT_sb", [128, T], BF16)
    xq_sb = sb("xq_sb", [128, 8 * 512], BF16, is_transpose_dest=True)
    xnT_sb = xq_sb
    qq1T_sb = xq_sb
    quT_sb = sb("quT_sb", [128, 8 * 512], BF16, is_transpose_dest=True)
    qaT_sb = sb("qaT_sb", [128, 2 * 512], BF16, is_transpose_dest=True)
    qf_sb = sb("qf_sb", [74, 2 * 512], BF16)
    e_sb = sb("e_sb", [128, 4 * 512], BF16)
    scrA_sb = sb("scrA_sb", [128, 8 * 512], BF16)
    au_sb = scrA_sb
    kxtm_sb_ap = scrA_sb[:, 0:2048]
    qq1tm_ap = scrA_sb[:, 2048:3072]
    qutm_ap = scrA_sb[:, 3072:4096]
    qaux_ap = scrA_sb[:, 0:1024]
    uT_sb = sb("uT_sb", [128, 512], BF16)
    spr2_ap = uT_sb[:]
    qmtm_sb = sb("qmtm_sb", [128, 512], BF16, is_transpose_dest=True)
    utm_sb = sb("utm_sb", [128, 512], BF16, is_transpose_dest=True)
    kmtm_sb = sb("kmtm_sb", [128, 128], BF16, is_transpose_dest=True)
    sq_sb = sb("sq_sb", [128, 512], BF16)
    rep_sb = sb("rep_sb", [128, 1024], BF16)
    red1_sb = sb("red1_sb", [128, 64], F32)
    red2_sb = sb("red2_sb", [128, 64], F32)
    x1t_sb = sb("x1t_sb", [128, D], F32)
    xt_sb = sb("xt_sb", [128, D], F32)
    xorig_sb = xt_sb
    h1_sb = x1t_sb[0:64, 0:512]
    h2_sb = xt_sb[0:64, 0:512]
    xn_sb = sb("xn_sb", [128, D], BF16)
    yt_sb = sb("yt_sb", [128, 512], F32)
    st_sum = sb("st_sum", [128, 8], F32)
    st_sq = sb("st_sq", [128, 8], F32)
    st_mean = sb("st_mean", [128, 2], F32)
    st_var = sb("st_var", [128, 2], F32)
    st_var2 = sb("st_var2", [128, 2], F32)
    st_inv = sb("st_inv", [128, 2], F32)
    st_nb = sb("st_nb", [128, 2], F32)
    qT_sb = sb("qT_sb", [128, 512], BF16)
    kT_sb = sb("kT_sb", [128, 512], BF16)
    qc_sb = sb("qc_sb", [128, 512], BF16)
    qs_sb = sb("qs_sb", [128, 512], BF16)
    kc_sb = sb("kc_sb", [128, 512], BF16)
    ks_sb = sb("ks_sb", [128, 512], BF16)
    rsA_sb = sb("rsA_sb", [128, 2048], BF16)
    rsT_sb = sb("rsT_sb", [16, 512], BF16)
    rcpb_sb = rep_sb[0:16, 0:512]
    z1T_sb = rep_sb[0:32, 512:1024]

    ps = [nc.alloc_psum_tensor(f"ps{i}", [128, 512], F32) for i in range(8)]
    psb = [Buf() for _ in range(8)]
    bufs = {}

    def B_(name):
        if name not in bufs:
            bufs[name] = Buf()
        return bufs[name]

    def t_load(eng, semn, out_ap, in_ap, wbufs=(), rbufs=(), extra=()):
        waits = list(extra)
        for n in wbufs:
            waits += B_(n).for_write()
        for n in rbufs:
            waits += B_(n).for_read()
        tok = k.dma(eng, semn, out_ap, in_ap, waits=waits)
        for n in wbufs:
            B_(n).wrote(tok)
        for n in rbufs:
            B_(n).read(tok)
        return tok

    def t_tr(eng, out_ap, in_ap, wbufs=(), rbufs=(), extra=()):
        waits = list(extra)
        for n in wbufs:
            waits += B_(n).for_write()
        for n in rbufs:
            waits += B_(n).for_read()
        tok = k.dma_t("scalar", "dt", out_ap, in_ap, waits=waits)
        for n in wbufs:
            B_(n).wrote(tok)
        for n in rbufs:
            B_(n).read(tok)
        return tok

    def t_mm(bank, out_ap, lhsT, rhs, start, stop, rbufs=(), extra=(), skip=False):
        waits = list(extra)
        if start:
            waits += psb[bank].for_write()
        for n in rbufs:
            waits += B_(n).for_read()
        tok = k.emit(
            "tensor",
            lambda e, o=out_ap, l=lhsT, r=rhs, st=start, sp=stop, sk=skip:
                e.matmul(o, l, r, start=st, stop=sp, skip_group_check=sk),
            waits=waits, inc=(te, 1))
        if start:
            psb[bank].wrote(tok)
        else:
            psb[bank].w = tok
        for n in rbufs:
            B_(n).read(tok)
        return tok

    def t_ve(fn, wr=(), rd=(), banks_rd=(), extra=(), eng="vector"):
        sem = ve if eng == "vector" else se
        waits = list(extra)
        for n in rd:
            waits += B_(n).for_read()
        for n in wr:
            waits += B_(n).for_write()
        for b in banks_rd:
            if psb[b].w is not None:
                waits.append(psb[b].w)
        tok = k.emit(eng, fn, waits=waits, inc=(sem, 1))
        for n in rd:
            B_(n).read(tok)
        for n in wr:
            B_(n).wrote(tok)
        for b in banks_rd:
            psb[b].read(tok)
        return tok

    def outer_split(fast_src_ap, fast_name, slow_src_ap, slow_name, out3,
                    out_name, rd_extra=()):
        f3 = fast_src_ap.rearrange("p (h b) -> p h b", b=8)
        s3 = slow_src_ap.rearrange("p (h a) -> p h a", a=8)
        for aa in range(8):
            t_ve(lambda e, a_=aa: e.tensor_mul(
                out3[:, :, a_ * 8:(a_ + 1) * 8], f3,
                s3[:, :, a_:a_ + 1].broadcast_to([128, 16, 8])),
                wr=[out_name], rd=[fast_name, slow_name] + list(rd_extra))

    def ka():
        return

    def dbg_dump(name, src_ap, rd=(), banks_rd=()):
        if not dbg or name not in dbg_t:
            return
        scr = B_("dbgscr")
        shp = dbg_t[name].shape
        tmp = sb("dbgs_" + name, shp, F32)
        t_ve(lambda e, o=tmp, s=src_ap: e.tensor_copy(o[:], s),
             wr=["dbgs" + name], rd=list(rd), banks_rd=list(banks_rd))
        t_load("gpsimd", "do", dbg_t[name][:], tmp[:], rbufs=["dbgs" + name])

    def phase_consts():
        g = "gpsimd"
        t_load(g, "dg", p2_sb[:], d_p2[:], ["p2"])
        t_load(g, "dg", i128_sb[:], d_i128[:], ["i128"])
        t_load(g, "dg", tri_sb[:], d_tri[:], ["tri"])
        t_load(g, "dg", q2cl_sb[:], d_q2cl[:], ["q2cl"])
        t_load(g, "dg", repc_sb[:], d_repc[:], ["repc"])
        t_load(g, "dg", onesb_sb[:], d_onesbf[:], ["onesb"])
        t_load(g, "dg", onesf_sb[:], d_onesf32[:], ["onesf"])
        t_load(g, "dg", wqml_sb[:].rearrange("p (c j) -> p c j", j=128), d_wqml[:], ["wqml"])
        t_load(g, "dg", wkml_sb[:].rearrange("p (c j) -> p c j", j=128), d_wkml[:], ["wkml"])
        t_load(g, "dg", wul_sb[:].rearrange("p (c j) -> p c j", j=128), d_wul[:], ["wul"])
        t_load(g, "dg", vr_sb[:].rearrange("p (c j) -> p c j", j=32), d_vr[:], ["vr"])
        t_load(g, "dg", ct_sb[:], d_ct[:], ["ct"])
        t_load(g, "dg", b_sb[:], d_b[:], ["b"])
        t_load(g, "dg", h1_sb, d_h1[:], ["x1t"])
        t_load(g, "dg", h2_sb, d_h2[:], ["xt"])
        t_load(g, "dg", w74f_sb[:, 0:10], d_w74[0], ["w74f"])
        t_load(g, "dg", w74f_sb[:, 10:20], d_w74[1], ["w74f"])
        t_load(g, "dg", tmp1_sb[:], d_temp[None, :], ["tmpv"])

        t_ve(lambda e: e.memset(epsc_sb[:], LNEPS), wr=["epsc"])
        t_ve(lambda e: e.memset(bmat_sb[:], 0.0), wr=["bmat"])
        for e_i in range(8):
            t_ve(lambda e, i=e_i: e.tensor_mul(
                bscr_sb[:], h1_sb[:, i * 64:(i + 1) * 64],
                h2_sb[:, i * 64:(i + 1) * 64]),
                wr=["bscr"], rd=["x1t", "xt"])
            t_ve(lambda e: e.tensor_add(bmat_sb[:], bmat_sb[:], bscr_sb[:]),
                 wr=["bmat"], rd=["bscr", "bmat"])
        t_ve(lambda e: e.memset(mq_sb[:], 0.0), wr=["mq"])
        t_ve(lambda e: e.tensor_copy(mq_sb[0:64, 0:64], bmat_sb[:]),
             wr=["mq"], rd=["bmat"])
        t_ve(lambda e: e.tensor_copy(mq_sb[64:128, 64:128], bmat_sb[:]),
             wr=["mq"], rd=["bmat"])
        t_ve(lambda e: e.tensor_scalar_mul(w74_sb[:], w74f_sb[:], -2.0),
             wr=["w74"], rd=["w74f"])
        t_mm(7, ps[7][0:128, 0:1], onesf_sb[:], tmp1_sb[:], True, True,
             rbufs=["onesf", "tmpv"])
        t_ve(lambda e: e.tensor_scalar_max(negc_sb[:], ps[7][0:128, 0:1], 0.1),
             wr=["negc"], banks_rd=[7])
        t_ve(lambda e: e.reciprocal(negc_sb[:], negc_sb[:]),
             wr=["negc"], rd=["negc"])
        t_ve(lambda e: e.tensor_scalar_mul(negc_sb[:], negc_sb[:], -1.0),
             wr=["negc"], rd=["negc"])

    def ln_steps(src_name, src_ap, out_ap, out_name, par):
        sm = "stm%d" % par
        si = "sti%d" % par
        sn = "stn%d" % par
        ss = "sts%d" % par
        sqn = "stq%d" % par
        dummy = out_ap
        dn = out_name

        def s1():
            t_ve(lambda e: e.activation(
                dummy, src_ap, AF.Square, accum_out=st_sq[:, par:par + 1]),
                eng="scalar", wr=[dn, sqn], rd=[src_name])

        def s2():
            t_ve(lambda e: e.reduce_sum(st_sum[:, par:par + 1], src_ap, axis=AX),
                 wr=[ss], rd=[src_name])

        def s3():
            t_ve(lambda e: e.tensor_scalar_mul(
                st_mean[:, par:par + 1], st_sum[:, par:par + 1], 1.0 / D),
                wr=[sm], rd=[ss])

        def s4():
            t_ve(lambda e: e.tensor_mul(
                st_var[:, par:par + 1], st_mean[:, par:par + 1],
                st_mean[:, par:par + 1]), wr=["stv%d" % par], rd=[sm])

        def s5():
            t_ve(lambda e: e.scalar_tensor_tensor(
                st_var2[:, par:par + 1], st_sq[:, par:par + 1], 1.0 / D,
                st_var[:, par:par + 1], OP.mult, OP.subtract),
                wr=["st2%d" % par], rd=[sqn, "stv%d" % par])

        def s6():
            t_ve(lambda e: e.activation(
                st_inv[:, par:par + 1], st_var2[:, par:par + 1], AF.Ln,
                bias=epsc_sb[:]), eng="scalar", wr=[si],
                rd=["st2%d" % par, "epsc"])

        def s7():
            t_ve(lambda e: e.activation(
                st_inv[:, par:par + 1], st_inv[:, par:par + 1], AF.Exp,
                scale=-0.5), eng="scalar", wr=[si], rd=[si])

        def s8():
            t_ve(lambda e: e.scalar_tensor_tensor(
                st_nb[:, par:par + 1], st_mean[:, par:par + 1], -1.0,
                st_inv[:, par:par + 1], OP.mult, OP.mult),
                wr=[sn], rd=[sm, si])

        def s9():
            t_ve(lambda e: e.activation(
                out_ap, src_ap, AF.Identity, bias=st_nb[:, par:par + 1],
                scale=st_inv[:, par:par + 1]),
                eng="scalar", wr=[out_name], rd=[src_name, sn, si])

        return [s1, s2, s3, s4, s5, s6, s7, s8, s9]

    def ln_tile(src_name, src_ap, out_ap, out_name, par=0):
        for s in ln_steps(src_name, src_ap, out_ap, out_name, par):
            s()

    def phase_p2_tile(tile):
        sl = slice(tile * 128, (tile + 1) * 128)
        t_tr("scalar", kmtm_sb[:], kmT_sb[:, sl], wbufs=["kmtm"], rbufs=["kmT"])
        kx4 = kxtm_sb_ap.rearrange("p (h c) -> p h c", c=128)
        km3 = kmtm_sb[:].rearrange("p (h i) -> p h i", i=8)
        for aa in range(8):
            t_ve(lambda e, a_=aa: e.tensor_mul(
                kx4[:, :, a_ * 8:(a_ + 1) * 8],
                km3, km3[:, :, a_:a_ + 1].broadcast_to([128, 16, 8])),
                wr=["sA0", "sA1"], rd=["kmtm"])
        t_ve(lambda e: e.tensor_copy(kx4[:, :, 64:72], km3),
             wr=["sA0", "sA1"], rd=["kmtm"])
        t_ve(lambda e: e.memset(kx4[:, :, 72:73], 1.0), wr=["sA0", "sA1"])
        t_ve(lambda e: e.tensor_mul(sq_sb[:, 0:128], kmtm_sb[:], kmtm_sb[:]),
             wr=["sq"], rd=["kmtm"])
        t_ve(lambda e: e.reduce_sum(
            red1_sb[:, 0:16],
            sq_sb[:, 0:128].rearrange("p (h i) -> p h i", i=8), axis=AX),
            wr=["red1"], rd=["sq"])
        t_ve(lambda e: e.tensor_scalar(
            kx4[:, :, 73:74].rearrange("p h c -> p (h c)"),
            red1_sb[:, 0:16], EPS, None, OP.mult),
            wr=["sA0", "sA1"], rd=["red1"])
        src = kxtm_sb_ap.rearrange("t (h p) -> t h p", p=128)
        dst = kxt_sb[:].rearrange("p (h t) -> p h t", t=T)[
            :, :, tile * 128:(tile + 1) * 128]
        t_tr("sync", dst, src, wbufs=["kxt%d" % tile], rbufs=["sA0", "sA1"])
        ka()

    lift_pend = [None]

    def rope_och(och, xnb, xnn):
        par = och % 2
        for c in range(8):
            t_mm(0, ps[0][0:128, :],
                 wqs_sb[:, par * 1024 + c * 128:par * 1024 + (c + 1) * 128],
                 xnb[:, c * 512:(c + 1) * 512],
                 start=(c == 0), stop=(c == 7), rbufs=["wqs%d" % par, xnn])
        t_ve(lambda e: e.tensor_copy(qT_sb[:], ps[0][0:128, :]),
             wr=["qTb"], banks_rd=[0])
        for c in range(8):
            t_mm(1, ps[1][0:128, :],
                 wks_sb[:, par * 1024 + c * 128:par * 1024 + (c + 1) * 128],
                 xnb[:, c * 512:(c + 1) * 512],
                 start=(c == 0), stop=(c == 7), rbufs=["wks%d" % par, xnn])
        t_ve(lambda e: e.tensor_copy(kT_sb[:], ps[1][0:128, :]),
             wr=["kTb"], banks_rd=[1])
        t_mm(2, ps[2][0:128, :], p2_sb[:], qT_sb[:],
             start=True, stop=True, rbufs=["p2", "qTb"])
        t_mm(5, ps[5][0:128, :], p2_sb[:], kT_sb[:],
             start=True, stop=True, rbufs=["p2", "kTb"])
        if lift_pend[0] is not None:
            lift_pend[0]()
            lift_pend[0] = None
        t_ve(lambda e: e.tensor_mul(qc_sb[:], qT_sb[:], c2_sb[:]),
             wr=["qcb"], rd=["qTb", "c2"])
        t_ve(lambda e: e.tensor_mul(qs_sb[:], ps[2][0:128, :], s2_sb[:]),
             wr=["qsb"], rd=["s2"], banks_rd=[2])
        t_ve(lambda e: e.tensor_mul(kc_sb[:], kT_sb[:], c2_sb[:]),
             wr=["kcb"], rd=["kTb", "c2"])
        t_ve(lambda e: e.tensor_mul(ks_sb[:], ps[5][0:128, :], s2_sb[:]),
             wr=["ksb"], rd=["s2"], banks_rd=[5])

        def lifts(och=och):
            t_mm(3, ps[3][0:128, :], wqml_sb[:, och * 128:(och + 1) * 128],
                 qc_sb[:],
                 start=(och == 0), stop=False, rbufs=["wqml", "qcb"], skip=True)
            t_mm(3, ps[3][0:128, :], wqml_sb[:, och * 128:(och + 1) * 128],
                 qs_sb[:],
                 start=False, stop=(och == 7), rbufs=["wqml", "qsb"], skip=True)
            t_mm(4, ps[4][0:128, :], wkml_sb[:, och * 128:(och + 1) * 128],
                 kc_sb[:],
                 start=(och == 0), stop=False, rbufs=["wkml", "kcb"], skip=True)
            t_mm(4, ps[4][0:128, :], wkml_sb[:, och * 128:(och + 1) * 128],
                 ks_sb[:],
                 start=False, stop=(och == 7), rbufs=["wkml", "ksb"], skip=True)
        lift_pend[0] = lifts

    def ln_group_chunks(b, g):
        xnb = xnT_sb if g % 2 == 0 else quT_sb
        xnn = "xq" if g % 2 == 0 else "quT"
        chunks = []
        for half in range(2):
            plan = []
            for par in range(2):
                tl = half * 2 + par
                tile = g * 4 + tl
                rows = slice(tile * 128, (tile + 1) * 128)
                xbuf = xt_sb if par == 0 else x1t_sb
                xbn = "xt" if par == 0 else "x1t"
                onb = xn_sb[:] if par == 0 else rsA_sb[:, 0:1024]
                onn = "xn" if par == 0 else "rsAa"
                plan.append((rows, xbuf, xbn, onb, onn, par, tl))

            def c_load(plan=plan):
                for rows, xbuf, xbn, _, _, _, _ in plan:
                    t_load("sync", "di", xbuf[:], dx[b, rows, :], [xbn])
            chunks.append(c_load)
            stepss = []

            def c_mksteps(plan=plan, stepss=stepss):
                for _, xbuf, xbn, onb, onn, par, _ in plan:
                    stepss.append(ln_steps(xbn, xbuf[:], onb, onn, par))
            chunks.append(c_mksteps)
            for i in range(9):
                def c_step(i=i, stepss=stepss):
                    for st_ in stepss:
                        st_[i]()
                chunks.append(c_step)

            def c_tr(plan=plan, xnb=xnb, xnn=xnn):
                for _, _, _, onb, onn, _, tl in plan:
                    t_tr("scalar",
                         xnb[:].rearrange("p (c t) -> p c t", t=512)[
                             :, :, tl * 128:(tl + 1) * 128],
                         onb.rearrange("t (c p) -> t c p", p=128),
                         wbufs=[xnn], rbufs=[onn])
            chunks.append(c_tr)
        return chunks

    def phase_p1(b, m0_och=None, m0_tail=None):
        t_load("gpsimd", "dg", wvof_sb[:].rearrange("p (c o) -> p c o", o=1024),
               d_wv[:], ["wvof"])
        t_load("gpsimd", "dg", c2_sb[:], d_c2[:, 0:512], ["c2"])
        t_load("gpsimd", "dg", s2_sb[:], d_s2[:, 0:512], ["s2"])
        for c in ln_group_chunks(b, 0):
            c()
        for g in range(NG):
            xnb = xnT_sb if g % 2 == 0 else quT_sb
            xnn = "xq" if g % 2 == 0 else "quT"
            nxt = ln_group_chunks(b, g + 1) if g + 1 < NG else []
            ni = 0
            for och in range(8):
                par = och % 2
                t_load("gpsimd", "dg",
                       wqs_sb[:, par * 1024:(par + 1) * 1024].rearrange(
                           "p (c j) -> p c j", j=128),
                       d_wq[och], ["wqs%d" % par])
                t_load("gpsimd", "dg",
                       wks_sb[:, par * 1024:(par + 1) * 1024].rearrange(
                           "p (c j) -> p c j", j=128),
                       d_wk[och], ["wks%d" % par])
                rope_och(och, xnb, xnn)
                take = 4 if och < 6 else len(nxt) - ni
                for _ in range(max(0, take)):
                    if ni < len(nxt):
                        nxt[ni]()
                        ni += 1
                if g == NG - 1 and m0_och is not None:
                    m0_och(2)
            if lift_pend[0] is not None:
                lift_pend[0]()
                lift_pend[0] = None
            while ni < len(nxt):
                nxt[ni]()
                ni += 1
            if g == NG - 1 and m0_och is not None:
                m0_och(-1)
            T_sl = slice(g * 512, (g + 1) * 512)
            t_ve(lambda e, s=T_sl: e.activation(
                qmT_sb[:, s], ps[3][0:128, :], AF.Tanh, scale=0.5),
                eng="scalar", wr=["qmT"], banks_rd=[3])
            t_ve(lambda e, s=T_sl: e.tensor_scalar(
                qmT_sb[:, s], qmT_sb[:, s], 0.5, 0.5, OP.mult, OP.add),
                wr=["qmT"], rd=["qmT"])
            t_ve(lambda e, s=T_sl: e.activation(
                kmT_sb[:, s], ps[4][0:128, :], AF.Tanh, scale=0.5),
                eng="scalar", wr=["kmT"], banks_rd=[4])
            t_ve(lambda e, s=T_sl: e.tensor_scalar(
                kmT_sb[:, s], kmT_sb[:, s], 0.5, 0.5, OP.mult, OP.add),
                wr=["kmT"], rd=["kmT"])
            if g + 1 < NG:
                T_n = slice((g + 1) * 512, (g + 2) * 512)
                t_load("gpsimd", "dg", c2_sb[:], d_c2[:, T_n], ["c2"])
                t_load("gpsimd", "dg", s2_sb[:], d_s2[:, T_n], ["s2"])
            for tl in range(4):
                tile = g * 4 + tl
                for hf in range(2):
                    bank = 5 + hf
                    for c in range(8):
                        t_mm(bank, ps[bank][0:128, :],
                             xnb[:, c * 512 + tl * 128: c * 512 + (tl + 1) * 128],
                             wvof_sb[:, c * 1024 + hf * 512: c * 1024 + (hf + 1) * 512],
                             start=(c == 0), stop=(c == 7), rbufs=[xnn, "wvof"])
                    dstv = vtm_sb[:].rearrange(
                        "p (tt h j) -> p tt h j", tt=NT, h=H)[
                        :, tile, hf * 8:(hf + 1) * 8, 0:64]
                    t_ve(lambda e, d=dstv, bk=bank: e.tensor_copy(
                        d, ps[bk][0:128, :].rearrange("p (h j) -> p h j", j=64)),
                        wr=["vtm"], banks_rd=[bank])
                dstone = vtm_sb[:].rearrange(
                    "p (tt h j) -> p tt h j", tt=NT, h=H)[:, tile, :, 64:65]
                t_ve(lambda e, d=dstone: e.memset(d, 1.0), wr=["vtm"])
                if g == NG - 1 and tl == 3 and m0_tail is not None:
                    m0_tail()
                phase_p2_tile(tile)

    def metric_chunks(b, g, pre=False):
        T_sl = slice(g * 512, (g + 1) * 512)
        qm_ap, qm_n = qmtm_sb[:], "qmtm"
        ut_ap, ut_n = utm_sb[:], "utm"
        uT_ap, uT_n = uT_sb[:], "uT"
        if pre:
            homes = [(scrA_sb[:, 2048:3072], "sA2"),
                     (scrA_sb[:, 3072:4096], "sA3")]
        else:
            homes = [(rsA_sb[:, 0:1024], "rsAa"),
                     (rsA_sb[:, 1024:2048], "rsAb")]
        qq_homes = homes
        qu_homes = homes
        qa_ap, qa_n = homes[0]
        ubank = 2
        early = []
        late = []

        def c_qmtr():
            t_tr("scalar", qm_ap.rearrange("p (tt t) -> p tt t", t=128),
                 qmT_sb[:, T_sl].rearrange("p (tt t) -> p tt t", t=128),
                 wbufs=[qm_n], rbufs=["qmT"])
        early.append(c_qmtr)
        for tl in range(4):
            def c_qqo(tl=tl):
                qqr, qqn = qq_homes[tl % 2]
                outer_split(qm_ap[:, tl * 128:(tl + 1) * 128], qm_n,
                            qm_ap[:, tl * 128:(tl + 1) * 128], qm_n,
                            qqr.rearrange("p (h c) -> p h c", c=64), qqn)
            late.append(c_qqo)

            def c_qqt(tl=tl):
                qqr, qqn = qq_homes[tl % 2]
                t_tr("scalar",
                     qq1T_sb[:].rearrange("p (c t) -> p c t", t=512)[
                         :, :, tl * 128:(tl + 1) * 128],
                     qqr.rearrange("t (c p) -> t c p", p=128),
                     wbufs=["xq"], rbufs=[qqn])
            late.append(c_qqt)

        def c_umm():
            for c in range(8):
                t_mm(ubank, ps[ubank][0:128, :],
                     wul_sb[:, c * 128:(c + 1) * 128],
                     qq1T_sb[:, c * 512:(c + 1) * 512],
                     start=(c == 0), stop=(c == 7), rbufs=["wul", "xq"])
            t_ve(lambda e: e.tensor_copy(uT_ap, ps[ubank][0:128, :]),
                 wr=[uT_n], banks_rd=[ubank])
        late.append(c_umm)

        def c_utr():
            t_tr("scalar", ut_ap.rearrange("p (tt t) -> p tt t", t=128),
                 uT_ap.rearrange("p (tt t) -> p tt t", t=128),
                 wbufs=[ut_n], rbufs=[uT_n])
        late.append(c_utr)

        def c_sq():
            t_ve(lambda e: e.tensor_mul(sq_sb[:], ut_ap, ut_ap),
                 wr=["sq"], rd=[ut_n])
            t_ve(lambda e: e.reduce_sum(
                red1_sb[:], sq_sb[:].rearrange("p (th i) -> p th i", i=8),
                axis=AX), wr=["red1"], rd=["sq"])
            t_ve(lambda e: e.tensor_mul(sq_sb[:], qm_ap, qm_ap),
                 wr=["sq"], rd=[qm_n])
            t_ve(lambda e: e.reduce_sum(
                red2_sb[:], sq_sb[:].rearrange("p (th i) -> p th i", i=8),
                axis=AX), wr=["red2"], rd=["sq"])
        late.append(c_sq)

        def c_qaux():
            qa5 = qa_ap.rearrange(
                "p (c tt l i) -> p c tt l i", c=2, tt=4, i=16)
            for c2 in range(2):
                t_ve(lambda e, c_=c2: e.scalar_tensor_tensor(
                    qa5[:, c_, :, :, 8:9].squeeze(3),
                    red2_sb[:].rearrange("p (tt c l) -> p tt c l", tt=4, c=2)[
                        :, :, c_, :],
                    EPS,
                    red1_sb[:].rearrange("p (tt c l) -> p tt c l", tt=4, c=2)[
                        :, :, c_, :],
                    OP.mult, OP.add),
                    wr=[qa_n], rd=["red1", "red2"])
                for tl in range(4):
                    t_ve(lambda e, c_=c2, t_=tl: e.tensor_copy(
                        qa5[:, c_, t_, :, 0:8],
                        qm_ap.rearrange(
                            "p (tt c l i) -> p tt c l i", tt=4, c=2, i=8)[
                            :, t_, c_, :, :]),
                        wr=[qa_n], rd=[qm_n])
                t_ve(lambda e, c_=c2: e.memset(
                    qa5[:, c_, :, :, 9:10].squeeze(3), 1.0), wr=[qa_n])

        for tl in range(4):
            def c_quo(tl=tl):
                qur, qun = qu_homes[tl % 2]
                outer_split(ut_ap[:, tl * 128:(tl + 1) * 128], ut_n,
                            qm_ap[:, tl * 128:(tl + 1) * 128], qm_n,
                            qur.rearrange("p (h c) -> p h c", c=64), qun)
            late.append(c_quo)
            def c_qut(tl=tl):
                qur, qun = qu_homes[tl % 2]
                t_tr("scalar",
                     quT_sb[:].rearrange("p (c t) -> p c t", t=512)[
                         :, :, tl * 128:(tl + 1) * 128],
                     qur.rearrange("t (c p) -> t c p", p=128),
                     wbufs=["quT"], rbufs=[qun])
            late.append(c_qut)
        late.append(c_qaux)

        def c_qat():
            for c2 in range(2):
                t_tr("scalar",
                     qaT_sb[:, c2 * 512:(c2 + 1) * 512].rearrange(
                         "p (tt q) -> p tt q", q=128),
                     qa_ap[:, c2 * 512:(c2 + 1) * 512].rearrange(
                         "t (tt p) -> t tt p", p=128),
                     wbufs=["qaT"], rbufs=[qa_n])
        late.append(c_qat)
        return early, late

    def phase_p4(b, g, nxt_early, mfeed):
        nch = 4 * (g + 1)

        def qf_build(h):
            ch = h // 2
            par = h % 2
            qfs = h % 2
            qf_ap = qf_sb[:, qfs * 512:(qfs + 1) * 512]
            qfn = "qf%d" % qfs
            t_mm(2, ps[2][0:64, :], mq_sb[:, par * 64:(par + 1) * 64],
                 qq1T_sb[:, ch * 512:(ch + 1) * 512],
                 start=True, stop=True, rbufs=["mq", "xq"], skip=True)
            t_mm(2, ps[2][64:74, :], w74_sb[:, par * 10:(par + 1) * 10],
                 quT_sb[:, ch * 512:(ch + 1) * 512],
                 start=True, stop=False, rbufs=["w74", "quT"], skip=True)
            t_mm(2, ps[2][64:74, :], q2cl_sb[:, h * 10:(h + 1) * 10],
                 qaT_sb[:, (h // 8) * 512:(h // 8 + 1) * 512],
                 start=False, stop=True, rbufs=["q2cl", "qaT"], skip=True)
            t_ve(lambda e, o=qf_ap: e.tensor_copy(o[0:74, :], ps[2][0:74, :]),
                 wr=[qfn], banks_rd=[2])

        ZB = (0, 1, 5)
        ei = 0
        qf_build(0)
        for h in range(H):
            qfn = "qf%d" % (h % 2)
            qf_ap = qf_sb[:, (h % 2) * 512:(h % 2 + 1) * 512]
            av_bank = 6 + (h % 2)

            def z_tile(c):
                zb = ZB[c % 3]
                diag = (c >= 4 * g)
                t_mm(zb, ps[zb][0:128, :],
                     kxt_sb[0:74, h * T + c * 128: h * T + (c + 1) * 128],
                     qf_ap[0:74, :],
                     start=True, stop=(not diag),
                     rbufs=["kxt%d" % c, qfn], skip=True)
                if diag:
                    lo = c * 128 - g * 512
                    t_mm(zb, ps[zb][0:128, lo:lo + 128], i128_sb[:], tri_sb[:],
                         start=False, stop=True, rbufs=["i128", "tri"], skip=True)
                es = c % 4
                e_ap = e_sb[:, es * 512:(es + 1) * 512]
                en = "e%d" % es
                lo = max(0, c * 128 - g * 512)
                if lo > 0:
                    t_ve(lambda e, a=e_ap, l=lo: e.memset(a[:, 0:l], 0.0),
                         wr=[en])
                t_ve(lambda e, a=e_ap, l=lo, bk=zb: e.activation(
                    a[:, l:512], ps[bk][0:128, l:512], AF.Exp, scale=negc_sb[:]),
                    eng="scalar", wr=[en], rd=["negc"], banks_rd=[zb])
                t_ve(lambda e, a=e_ap, l=lo: e.tensor_scalar_min(
                    a[:, l:512], a[:, l:512], 1.0), wr=[en], rd=[en])

            def av_tile(c):
                en = "e%d" % (c % 4)
                e_ap = e_sb[:, (c % 4) * 512:(c % 4 + 1) * 512]
                vsl = vtm_sb[:].rearrange(
                    "p (tt hh j) -> p tt hh j", tt=NT, hh=H)[:, c, h, :]
                t_mm(av_bank, ps[av_bank][0:65, :], vsl, e_ap,
                     start=(c == 0), stop=(c == nch - 1),
                     rbufs=["vtm", en], skip=True)

            for c in range(nch):
                z_tile(c)
                if c == 1 and h + 1 < H:
                    qf_build(h + 1)
                if c >= 1:
                    av_tile(c - 1)
            av_tile(nch - 1)
            if h >= 2 and ei < len(nxt_early):
                nxt_early[ei]()
                ei += 1
            rr = (h // 4) * 32
            rc = (h % 4) * 512
            t_ve(lambda e, r=rr, cc=rc, bk=av_bank: e.tensor_copy(
                rsA_sb[r:r + 1, cc:cc + 512], ps[bk][64:65, :]),
                wr=["rsAa" if (h % 4) < 2 else "rsAb"], banks_rd=[av_bank])
            ch = h // 2
            base = (h % 2) * 64
            t_ve(lambda e, bk=av_bank, c_=ch, b_=base: e.tensor_copy(
                au_sb[b_:b_ + 64, c_ * 512:(c_ + 1) * 512], ps[bk][0:64, :]),
                wr=["sA%d" % (ch // 2)], banks_rd=[av_bank])
        while ei < len(nxt_early):
            nxt_early[ei]()
            ei += 1
        srcr = rsA_sb[0:128:32, :].rearrange("r (s t) -> r s t", s=4)
        tok1 = t_load("sync", "di", d_rs[:].rearrange("(r s) t -> r s t", r=4),
                      srcr, rbufs=["rsAa", "rsAb"], wbufs=["drs"])
        t_load("sync", "di", rsT_sb[:], d_rs[:], wbufs=["rsT"], rbufs=["drs"],
               extra=[tok1])
        mfeed(1)
        rcp_ap = yt_sb[0:16, :]
        t_ve(lambda e, o=rcp_ap: e.reciprocal(o, rsT_sb[:]),
             wr=["yt"], rd=["rsT"])
        t_ve(lambda e, i=rcp_ap: e.tensor_copy(rcpb_sb, i),
             wr=["rep"], rd=["yt"])
        for c in range(8):
            t_mm(2, ps[2][0:128, :], repc_sb[:, c * 128:(c + 1) * 128],
                 rcpb_sb, start=True, stop=True, rbufs=["repc", "rep"])
            t_ve(lambda e: e.tensor_copy(spr2_ap, ps[2][0:128, :]),
                 wr=["uT"], banks_rd=[2])
            t_ve(lambda e, c_=c: e.tensor_mul(
                au_sb[:, c_ * 512:(c_ + 1) * 512],
                au_sb[:, c_ * 512:(c_ + 1) * 512], spr2_ap),
                wr=["sA%d" % (c // 2)], rd=["sA%d" % (c // 2), "uT"])
            mfeed(1)

    def phase_p5(b, g, mfeed):
        for tl in range(4):
            tile = g * 4 + tl
            rows = slice(tile * 128, (tile + 1) * 128)
            t_load("sync", "di", xorig_sb[:], dx[b, rows, :], ["xt"])
            for hf in range(2):
                bank = 3 + hf
                for c in range(8):
                    t_mm(bank, ps[bank][0:128, :],
                         au_sb[:, c * 512 + tl * 128: c * 512 + (tl + 1) * 128],
                         wvof_sb[:, c * 1024 + hf * 512: c * 1024 + (hf + 1) * 512],
                         start=(c == 0), stop=(c == 7),
                         rbufs=["sA%d" % (c // 2), "wvof"])
                t_ve(lambda e, bk=bank, h_=hf: e.scalar_tensor_tensor(
                    x1t_sb[:, h_ * 512:(h_ + 1) * 512], ps[bk][0:128, :], 1.0,
                    xorig_sb[:, h_ * 512:(h_ + 1) * 512], OP.mult, OP.add),
                    wr=["x1t"], rd=["xt"], banks_rd=[bank])
                mfeed(1)
            t_load("gpsimd", "dg", d_x1[b, rows, :], x1t_sb[:], rbufs=["x1t"])
            ln_tile("x1t", x1t_sb[:], xn_sb[:], "xn", par=0)
            t_load("gpsimd", "dg", d_xnt[g, tl],
                   xn_sb[:], wbufs=["dxnt%d" % g], rbufs=["xn"])
            mfeed(1)
        mfeed(-1)

    def p6_prefetch(b, g, use_e=False):
        xnb = xnT_sb if g % 2 == 0 else quT_sb
        xnn = "xq" if g % 2 == 0 else "quT"
        if use_e:
            scrs = [(e_sb[:, 0:1024], ["e0", "e1"]),
                    (e_sb[:, 1024:2048], ["e2", "e3"])] * 2
        else:
            scrs = [(xn_sb[:], ["xn"]), (rsA_sb[:, 0:1024], ["rsAa"]),
                    (rsA_sb[:, 1024:2048], ["rsAb"]),
                    (scrA_sb[:, 0:1024], ["sA0"])]
        for tl in range(4):
            scr, scrn = scrs[(g % 2) * 2 + (tl % 2)]
            t_load("sync", "di", scr, d_xnt[g, tl],
                   wbufs=scrn, rbufs=["dxnt%d" % g])
            t_tr("scalar",
                 xnb[:].rearrange("p (c t) -> p c t", t=512)[
                     :, :, tl * 128:(tl + 1) * 128],
                 scr.rearrange("t (c p) -> t c p", p=128),
                 wbufs=[xnn], rbufs=scrn)

    def phase_p6(b, g):
        xnb = xnT_sb if g % 2 == 0 else quT_sb
        xnn = "xq" if g % 2 == 0 else "quT"
        ka()
        for c in range(8):
            t_mm(2, ps[2][0:32, :], vr_sb[:, c * 32:(c + 1) * 32],
                 xnb[:, c * 512:(c + 1) * 512],
                 start=(c == 0), stop=(c == 7), rbufs=["vr", xnn])
        t_ve(lambda e: e.tensor_copy(z1T_sb, ps[2][0:32, :]),
             wr=["rep"], banks_rd=[2])
        for tl in range(4):
            tile = g * 4 + tl
            rows = slice(tile * 128, (tile + 1) * 128)
            par = tl % 2
            xbuf = x1t_sb if par == 0 else xt_sb
            xbn = "x1t" if par == 0 else "xt"
            t_load("sync", "di", xbuf[:], d_x1[b, rows, :], wbufs=[xbn],
                   extra=k.qall("dg"))
            for hf in range(2):
                bank = 3 + hf
                for c in range(8):
                    t_mm(bank, ps[bank][0:128, :],
                         xnb[:, c * 512 + tl * 128: c * 512 + (tl + 1) * 128],
                         wvof_sb[:, c * 1024 + hf * 512: c * 1024 + (hf + 1) * 512],
                         start=(c == 0), stop=False, rbufs=[xnn, "wvof"], skip=True)
                t_mm(bank, ps[bank][0:128, :],
                     z1T_sb[0:32, tl * 128:(tl + 1) * 128],
                     ct_sb[0:32, hf * 512:(hf + 1) * 512],
                     start=False, stop=False, rbufs=["rep", "ct"], skip=True)
                t_mm(bank, ps[bank][0:128, :], onesb_sb[:],
                     b_sb[:, hf * 512:(hf + 1) * 512],
                     start=False, stop=True, rbufs=["onesb", "b"], skip=True)
                t_ve(lambda e, bk=bank, h_=hf, xb=xbuf: e.scalar_tensor_tensor(
                    yt_sb[:], ps[bk][0:128, :], 1.0,
                    xb[:, h_ * 512:(h_ + 1) * 512], OP.mult, OP.add),
                    wr=["yt"], rd=[xbn], banks_rd=[bank])
                tok = k.dma("sync", "do", dy[b, rows, hf * 512:(hf + 1) * 512],
                            yt_sb[:], waits=B_("yt").for_write())
                B_("yt").read(tok)
        if g + 2 < NG:
            p6_prefetch(b, g + 2)

    def mk_feeder(chunks):
        st = {"i": 0}
        def feed(n):
            if n < 0:
                n = len(chunks)
            while n > 0 and st["i"] < len(chunks):
                chunks[st["i"]]()
                st["i"] += 1
                n -= 1
        return feed

    phase_consts()
    for b in range(NSEQ):
        me, ml = metric_chunks(b, 0, pre=True)
        m0_och = mk_feeder(me + ml[:11])
        ml_tail = ml[11:]

        def m0_tail(ml_tail=ml_tail):
            for c_fn in ml_tail:
                c_fn()
        phase_p1(b, m0_och, m0_tail)
        t_load("gpsimd", "dg", wvof_sb[:].rearrange("p (c o) -> p c o", o=1024),
               d_wo[:], ["wvof"])
        for g in range(NG):
            if g + 1 < NG:
                ne, nl = metric_chunks(b, g + 1)
            else:
                ne, nl = [], []
            mf = mk_feeder(nl)
            phase_p4(b, g, ne, mf)
            if g == NG - 1:
                p6_prefetch(b, 0, use_e=True)
                p6_prefetch(b, 1, use_e=True)
            phase_p5(b, g, mf)
        t_load("gpsimd", "dg", wvof_sb[:].rearrange("p (c o) -> p c o", o=1024),
               d_wf[:], ["wvof"])
        for g in range(NG):
            phase_p6(b, g)
    k.emit("sync", lambda e: e.nop(), waits=tuple(k.qall("do")))
    k.build_block()
    return nc



def kernel(**inputs):
    consts = _host_consts()
    lays = _host_layouts(inputs)
    x = np.ascontiguousarray(np.asarray(inputs["L0_attn_x"], np.float32))

    nc = build_nc(dbg=_DEBUG)
    shared = {}
    shared.update({kk: np.ascontiguousarray(v) for kk, v in consts.items()})
    shared.update({kk: np.ascontiguousarray(v) for kk, v in lays.items()})
    in_maps = []
    for ci in range(NCORE):
        m = dict(shared)
        m["x"] = x[ci * NSEQ:(ci + 1) * NSEQ]
        in_maps.append(m)
    res = run_bass_kernel_spmd(nc, in_maps, core_ids=list(range(NCORE)))
    out = np.empty((B_TOT, T, D), np.float32)
    for ci in range(NCORE):
        out[ci * NSEQ:(ci + 1) * NSEQ] = res.results[ci]["y"]
    if _DEBUG:
        kernel._dbg = [{kk: np.asarray(v) for kk, v in r.items() if kk.startswith("dbg_")}
                       for r in res.results]
    return out


# version 34
# speedup vs baseline: 1.0497x; 1.0481x over previous
import numpy as np
import ml_dtypes

import concourse.bass as bass
import concourse.mybir as mybir
from concourse.bass_utils import run_bass_kernel_spmd

F32 = mybir.dt.float32
BF16 = mybir.dt.bfloat16
AX = mybir.AxisListType.X
OP = mybir.AluOpType
AF = mybir.ActivationFunctionType
BFNP = ml_dtypes.bfloat16

NCORE = 8
B_TOT = 16
NSEQ = 2
T = 2048
D = 1024
H = 16
NT = 16
NG = 4
EPS = 1e-3
LNEPS = 1e-5
BIG = float(2.0 ** 100)

_DEBUG = False
MHOMES = {}



def _host_consts():
    c = {}
    inv_freq = 1.0 / 10000.0 ** (np.arange(0, 64, 2, dtype=np.float32) / 64)
    t = np.arange(T, dtype=np.float32)
    freqs = np.outer(t, inv_freq)
    cos = np.cos(freqs).astype(np.float32)
    sin = np.sin(freqs).astype(np.float32)
    C2 = np.zeros((128, T), np.float32)
    S2 = np.zeros((128, T), np.float32)
    for h2 in range(2):
        for j in range(64):
            C2[h2 * 64 + j] = cos[:, j % 32]
            S2[h2 * 64 + j] = -sin[:, j % 32] if j < 32 else sin[:, j - 32]
    c["c2"] = C2.astype(BFNP)
    c["s2"] = S2.astype(BFNP)

    P2 = np.zeros((128, 128), np.float32)
    for h2 in range(2):
        for i in range(64):
            P2[h2 * 64 + ((i + 32) % 64), h2 * 64 + i] = 1.0
    c["p2"] = P2.astype(BFNP)
    c["i128"] = np.eye(128, dtype=np.float32).astype(BFNP)
    c["tri"] = np.where(np.arange(128)[:, None] > np.arange(128)[None, :],
                        BIG, 0.0).astype(np.float32).astype(BFNP)

    q2cl = np.zeros((16, 128, 10), np.float32)
    for h in range(16):
        hl = h % 8
        for d in range(8):
            q2cl[h, hl * 16 + d, d] = -2.0 * EPS
        q2cl[h, hl * 16 + 8, 8] = 1.0
        q2cl[h, hl * 16 + 9, 9] = 1.0
    c["q2cl"] = np.ascontiguousarray(
        q2cl.transpose(1, 0, 2).reshape(128, 160)).astype(BFNP)

    rep = np.zeros((8, 16, 128), np.float32)
    for cc in range(8):
        rep[cc, 2 * cc, 0:64] = 1.0
        rep[cc, 2 * cc + 1, 64:128] = 1.0
    c["repc"] = np.ascontiguousarray(
        rep.transpose(1, 0, 2).reshape(16, 1024)).astype(BFNP)

    c["onesbf"] = np.ones((1, 128), BFNP)
    c["onesf32"] = np.ones((1, 128), np.float32)
    return c


def _host_layouts(ins):
    a = {}
    Wq = np.asarray(ins["L0_attn_Wq"], np.float32)
    Wk = np.asarray(ins["L0_attn_Wk"], np.float32)
    Wv = np.asarray(ins["L0_attn_Wv"], np.float32)
    Wo = np.asarray(ins["L0_attn_Wo"], np.float32)
    Wqm = np.asarray(ins["L0_attn_Wqm"], np.float32)
    Wkm = np.asarray(ins["L0_attn_Wkm"], np.float32)
    Wm = np.asarray(ins["L0_attn_W_metric"], np.float32)
    Wf = np.asarray(ins["L0_ffn_W"], np.float32)
    Vr = np.asarray(ins["L0_ffn_V_r"], np.float32)
    Cf = np.asarray(ins["L0_ffn_C"], np.float32)
    bf = np.asarray(ins["L0_ffn_b"], np.float32)

    def slab(W):
        return np.ascontiguousarray(
            W.reshape(8, 128, 8, 128).transpose(2, 1, 0, 3))

    a["wq_sl"] = slab(Wq)
    a["wk_sl"] = slab(Wk)
    a["wv_r"] = np.ascontiguousarray(Wv.reshape(8, 128, D).transpose(1, 0, 2))
    a["wo_r"] = np.ascontiguousarray(Wo.reshape(8, 128, D).transpose(1, 0, 2))
    a["wf_r"] = np.ascontiguousarray(
        np.ascontiguousarray(Wf.T).reshape(8, 128, D).transpose(1, 0, 2))
    a["vr_r"] = np.ascontiguousarray(Vr.reshape(8, 128, 32).transpose(1, 0, 2))
    a["ct_r"] = np.ascontiguousarray(Cf.T)
    a["b_r"] = np.ascontiguousarray(bf.reshape(1, D))

    def mlift(W8):
        L = np.zeros((D, 128), np.float32)
        for h in range(16):
            L[h * 64:(h + 1) * 64, h * 8:(h + 1) * 8] = W8
        return np.ascontiguousarray(L.reshape(8, 128, 128).transpose(1, 0, 2))

    a["wqm_l"] = mlift(Wqm)
    a["wkm_l"] = mlift(Wkm)
    Wu = np.zeros((64, 8), np.float32)
    for aa in range(8):
        for bb in range(8):
            Wu[aa * 8 + bb, :] = Wm[aa, bb * 8: bb * 8 + 8]
    a["wu_l"] = mlift(Wu)

    w74 = np.zeros((2, 128, 10), np.float32)
    for par in range(2):
        for m in range(8):
            for e in range(8):
                for d in range(8):
                    w74[par, par * 64 + m * 8 + e, d] = Wm[m, d * 8 + e]
    a["wm3r74"] = np.ascontiguousarray(w74)

    W3 = Wm.reshape(8, 8, 8)
    H1 = np.zeros((64, 8, 64), np.float32)
    H2 = np.zeros((64, 8, 64), np.float32)
    ab_a = np.arange(64) // 8
    ab_b = np.arange(64) % 8
    xy_x = np.arange(64) // 8
    xy_y = np.arange(64) % 8
    for e in range(8):
        H1[:, e, :] = W3[ab_a][:, xy_x, e]
        H2[:, e, :] = W3[ab_b][:, xy_y, e]
    a["h1"] = np.ascontiguousarray(H1.reshape(64, 512))
    a["h2"] = np.ascontiguousarray(H2.reshape(64, 512))

    a["temp"] = np.asarray(ins["L0_attn_temperature"], np.float32).reshape(1)
    return a



ENGS = ("sync", "vector", "scalar", "tensor", "gpsimd")


class Sem:
    def __init__(self, h, dma=False):
        self.h = h
        self.n = 0
        self.dma = dma

    def bump(self, kk):
        self.n += kk
        return self.n


class Buf:
    def __init__(self):
        self.w = None
        self.r = []

    def for_write(self):
        out = list(self.r)
        if self.w is not None:
            out.append(self.w)
        return out

    def wrote(self, tok):
        self.w = tok
        self.r = []

    def for_read(self):
        return [self.w] if self.w is not None else []

    def read(self, tok):
        self.r.append(tok)


NQSEM = 8


class K:
    def __init__(self, nc):
        self.nc = nc
        self.q = {e: [] for e in ENGS}
        self.sems = {}
        self.qsems = {}
        self.qcnt = {}
        for name in ("te", "ve", "se"):
            self.sems[name] = Sem(nc.alloc_semaphore(name=name))
        for name in ("di", "dt", "dts", "dg", "do"):
            self.qsems[name] = [Sem(nc.alloc_semaphore(name=f"{name}{i}"),
                                    dma=True) for i in range(NQSEM)]
            self.qcnt[name] = 0

    def qall(self, name):
        return [(s, s.n) for s in self.qsems[name] if s.n > 0]

    def emit(self, eng, fn, waits=(), inc=None):
        wd = {}
        for s, t in waits:
            if s.dma:
                t = s.n
            if s not in wd or t > wd[s]:
                wd[s] = t
        witems = tuple(wd.items())
        tok = None
        if inc is not None:
            tok = (inc[0], inc[0].bump(inc[1]))

        def g(e, fn=fn, witems=witems, inc=inc):
            for s, t in witems:
                e.wait_ge(s.h, t)
            ins = fn(e)
            if inc is not None and ins is not None:
                ins.then_inc(inc[0].h, inc[1])

        self.q[eng].append(g)
        return tok

    def _qsem(self, semn):
        s = self.qsems[semn][self.qcnt[semn] % NQSEM]
        self.qcnt[semn] += 1
        return s

    def dma(self, eng, semn, out_ap, in_ap, waits=()):
        return self.emit(eng, lambda e, o=out_ap, i=in_ap:
                         e.dma_start(out=o, in_=i),
                         waits=waits, inc=(self._qsem(semn), 16))

    def dma_t(self, eng, semn, out_ap, in_ap, waits=()):
        return self.emit(eng, lambda e, o=out_ap, i=in_ap:
                         e.dma_start(out=o, in_=i, transpose=True),
                         waits=waits, inc=(self._qsem(semn), 16))

    def build_block(self):
        nc = self.nc
        with nc.Block() as block:
            @block.sync
            def _(sync):
                for f in self.q["sync"]:
                    f(sync)

            @block.vector
            def _(vector):
                for f in self.q["vector"]:
                    f(vector)

            @block.scalar
            def _(scalar):
                for f in self.q["scalar"]:
                    f(scalar)

            @block.tensor
            def _(tensor):
                for f in self.q["tensor"]:
                    f(tensor)

            @block.gpsimd
            def _(gpsimd):
                for f in self.q["gpsimd"]:
                    f(gpsimd)



def build_nc(dbg=False):
    nc = bass.Bass("TRN2", target_bir_lowering=False)
    k = K(nc)
    te, ve, se = k.sems["te"], k.sems["ve"], k.sems["se"]

    dx = nc.dram_tensor("x", [NSEQ, T, D], F32, kind="ExternalInput")
    dy = nc.dram_tensor("y", [NSEQ, T, D], F32, kind="ExternalOutput")
    d_c2 = nc.dram_tensor("c2", [128, T], BF16, kind="ExternalInput")
    d_s2 = nc.dram_tensor("s2", [128, T], BF16, kind="ExternalInput")
    d_p2 = nc.dram_tensor("p2", [128, 128], BF16, kind="ExternalInput")
    d_i128 = nc.dram_tensor("i128", [128, 128], BF16, kind="ExternalInput")
    d_tri = nc.dram_tensor("tri", [128, 128], BF16, kind="ExternalInput")
    d_q2cl = nc.dram_tensor("q2cl", [128, 160], BF16, kind="ExternalInput")
    d_repc = nc.dram_tensor("repc", [16, 1024], BF16, kind="ExternalInput")
    d_onesbf = nc.dram_tensor("onesbf", [1, 128], BF16, kind="ExternalInput")
    d_onesf32 = nc.dram_tensor("onesf32", [1, 128], F32, kind="ExternalInput")
    d_wq = nc.dram_tensor("wq_sl", [8, 128, 8, 128], F32, kind="ExternalInput")
    d_wk = nc.dram_tensor("wk_sl", [8, 128, 8, 128], F32, kind="ExternalInput")
    d_wv = nc.dram_tensor("wv_r", [128, 8, 1024], F32, kind="ExternalInput")
    d_wo = nc.dram_tensor("wo_r", [128, 8, 1024], F32, kind="ExternalInput")
    d_wf = nc.dram_tensor("wf_r", [128, 8, 1024], F32, kind="ExternalInput")
    d_vr = nc.dram_tensor("vr_r", [128, 8, 32], F32, kind="ExternalInput")
    d_ct = nc.dram_tensor("ct_r", [32, 1024], F32, kind="ExternalInput")
    d_b = nc.dram_tensor("b_r", [1, D], F32, kind="ExternalInput")
    d_wqml = nc.dram_tensor("wqm_l", [128, 8, 128], F32, kind="ExternalInput")
    d_wkml = nc.dram_tensor("wkm_l", [128, 8, 128], F32, kind="ExternalInput")
    d_wul = nc.dram_tensor("wu_l", [128, 8, 128], F32, kind="ExternalInput")
    d_w74 = nc.dram_tensor("wm3r74", [2, 128, 10], F32, kind="ExternalInput")
    d_h1 = nc.dram_tensor("h1", [64, 512], F32, kind="ExternalInput")
    d_h2 = nc.dram_tensor("h2", [64, 512], F32, kind="ExternalInput")
    d_temp = nc.dram_tensor("temp", [1], F32, kind="ExternalInput")
    d_x1 = nc.dram_tensor("x1scr", [NSEQ, T, D], F32)
    d_rs = nc.dram_tensor("rsscr", [16, 512], BF16)
    d_xnt = nc.dram_tensor("xntscr", [NG, 4, 128, 1024], BF16)

    dbg_t = {}
    if dbg:
        dbg_t["rsA"] = nc.dram_tensor("dbg_rsA", [NSEQ, NG, 128, 2048], BF16,
                                      kind="ExternalOutput")
        dbg_t["rsT"] = nc.dram_tensor("dbg_rsT", [NSEQ, NG, 16, 512], BF16,
                                      kind="ExternalOutput")
        dbg_t["auU"] = nc.dram_tensor("dbg_auU", [NSEQ, NG, 128, 4096], BF16,
                                      kind="ExternalOutput")
        dbg_t["auN"] = nc.dram_tensor("dbg_auN", [NSEQ, NG, 128, 4096], BF16,
                                      kind="ExternalOutput")
        dbg_t["qf"] = nc.dram_tensor("dbg_qf", [NSEQ, NG, H, 74, 512], BF16,
                                     kind="ExternalOutput")

    sb = nc.alloc_sbuf_tensor
    wvof_sb = sb("wvof_sb", [128, 8 * 1024], BF16)
    wqml_sb = sb("wqml_sb", [128, 8 * 128], BF16)
    wkml_sb = sb("wkml_sb", [128, 8 * 128], BF16)
    wul_sb = sb("wul_sb", [128, 8 * 128], BF16)
    vr_sb = sb("vr_sb", [128, 8 * 32], BF16)
    ct_sb = sb("ct_sb", [32, 1024], BF16)
    b_sb = sb("b_sb", [1, D], BF16)
    c2_sb = sb("c2_sb", [128, 512], BF16)
    s2_sb = sb("s2_sb", [128, 512], BF16)
    p2_sb = sb("p2_sb", [128, 128], BF16)
    i128_sb = sb("i128_sb", [128, 128], BF16)
    tri_sb = sb("tri_sb", [128, 128], BF16)
    q2cl_sb = sb("q2cl_sb", [128, 160], BF16)
    repc_sb = sb("repc_sb", [16, 1024], BF16)
    onesb_sb = sb("onesb_sb", [1, 128], BF16)
    onesf_sb = sb("onesf_sb", [1, 128], F32)
    w74_sb = sb("w74_sb", [128, 20], BF16)
    mq_sb = sb("mq_sb", [128, 128], BF16)
    negc_sb = sb("negc_sb", [128, 1], F32)
    epsc_sb = sb("epsc_sb", [128, 1], F32)
    bmat_sb = sb("bmat_sb", [64, 64], F32)
    bscr_sb = sb("bscr_sb", [64, 64], F32)
    w74f_sb = sb("w74f_sb", [128, 20], F32)
    tmp1_sb = sb("tmp1_sb", [1, 1], F32)
    wqs_sb = sb("wqs_sb", [128, 2 * 1024], BF16)
    wks_sb = sb("wks_sb", [128, 2 * 1024], BF16)
    kxt_sb = sb("kxt_sb", [128, H * T], BF16, is_transpose_dest=True)
    vtm_sb = sb("vtm_sb", [128, NT * 1040], BF16)
    qmT_sb = sb("qmT_sb", [128, T], BF16)
    kmT_sb = sb("kmT_sb", [128, T], BF16)
    xq_sb = sb("xq_sb", [128, 8 * 512], BF16, is_transpose_dest=True)
    xnT_sb = xq_sb
    qq1T_sb = xq_sb
    quT_sb = sb("quT_sb", [128, 8 * 512], BF16, is_transpose_dest=True)
    qaT_sb = sb("qaT_sb", [128, 2 * 512], BF16, is_transpose_dest=True)
    qf_sb = sb("qf_sb", [74, 2 * 512], BF16)
    e_sb = sb("e_sb", [128, 4 * 512], BF16)
    scrA_sb = sb("scrA_sb", [128, 8 * 512], BF16)
    au_sb = scrA_sb
    kxtm_sb_ap = scrA_sb[:, 0:2048]
    qq1tm_ap = scrA_sb[:, 2048:3072]
    qutm_ap = scrA_sb[:, 3072:4096]
    qaux_ap = scrA_sb[:, 0:1024]
    uT_sb = sb("uT_sb", [128, 512], BF16)
    spr2_ap = uT_sb[:]
    qmtm_sb = sb("qmtm_sb", [128, 512], BF16, is_transpose_dest=True)
    utm_sb = sb("utm_sb", [128, 512], BF16, is_transpose_dest=True)
    kmtm_sb = sb("kmtm_sb", [128, 128], BF16, is_transpose_dest=True)
    sq_sb = sb("sq_sb", [128, 512], BF16)
    rep_sb = sb("rep_sb", [128, 1024], BF16)
    red1_sb = sb("red1_sb", [128, 64], F32)
    red2_sb = sb("red2_sb", [128, 64], F32)
    x1t_sb = sb("x1t_sb", [128, D], F32)
    xt_sb = sb("xt_sb", [128, D], F32)
    xorig_sb = xt_sb
    h1_sb = x1t_sb[0:64, 0:512]
    h2_sb = xt_sb[0:64, 0:512]
    xn_sb = sb("xn_sb", [128, D], BF16)
    yt_sb = sb("yt_sb", [128, 512], F32)
    st_sum = sb("st_sum", [128, 8], F32)
    st_sq = sb("st_sq", [128, 8], F32)
    st_mean = sb("st_mean", [128, 2], F32)
    st_var = sb("st_var", [128, 2], F32)
    st_var2 = sb("st_var2", [128, 2], F32)
    st_inv = sb("st_inv", [128, 2], F32)
    st_nb = sb("st_nb", [128, 2], F32)
    qT_sb = sb("qT_sb", [128, 512], BF16)
    kT_sb = sb("kT_sb", [128, 512], BF16)
    qc_sb = sb("qc_sb", [128, 512], BF16)
    qs_sb = sb("qs_sb", [128, 512], BF16)
    kc_sb = sb("kc_sb", [128, 512], BF16)
    ks_sb = sb("ks_sb", [128, 512], BF16)
    rsA_sb = sb("rsA_sb", [128, 2048], BF16)
    rsT_sb = sb("rsT_sb", [16, 512], BF16)
    rcpb_sb = rep_sb[0:16, 0:512]
    z1T_sb = rep_sb[0:32, 512:1024]

    ps = [nc.alloc_psum_tensor(f"ps{i}", [128, 512], F32) for i in range(8)]
    psb = [Buf() for _ in range(8)]
    bufs = {}

    def B_(name):
        if name not in bufs:
            bufs[name] = Buf()
        return bufs[name]

    def t_load(eng, semn, out_ap, in_ap, wbufs=(), rbufs=(), extra=()):
        waits = list(extra)
        for n in wbufs:
            waits += B_(n).for_write()
        for n in rbufs:
            waits += B_(n).for_read()
        tok = k.dma(eng, semn, out_ap, in_ap, waits=waits)
        for n in wbufs:
            B_(n).wrote(tok)
        for n in rbufs:
            B_(n).read(tok)
        return tok

    def t_tr(eng, out_ap, in_ap, wbufs=(), rbufs=(), extra=()):
        waits = list(extra)
        for n in wbufs:
            waits += B_(n).for_write()
        for n in rbufs:
            waits += B_(n).for_read()
        tok = k.dma_t("scalar", "dt", out_ap, in_ap, waits=waits)
        for n in wbufs:
            B_(n).wrote(tok)
        for n in rbufs:
            B_(n).read(tok)
        return tok

    def t_mm(bank, out_ap, lhsT, rhs, start, stop, rbufs=(), extra=(), skip=False):
        waits = list(extra)
        if start:
            waits += psb[bank].for_write()
        for n in rbufs:
            waits += B_(n).for_read()
        tok = k.emit(
            "tensor",
            lambda e, o=out_ap, l=lhsT, r=rhs, st=start, sp=stop, sk=skip:
                e.matmul(o, l, r, start=st, stop=sp, skip_group_check=sk),
            waits=waits, inc=(te, 1))
        if start:
            psb[bank].wrote(tok)
        else:
            psb[bank].w = tok
        for n in rbufs:
            B_(n).read(tok)
        return tok

    def t_ve(fn, wr=(), rd=(), banks_rd=(), extra=(), eng="vector"):
        sem = ve if eng == "vector" else se
        waits = list(extra)
        for n in rd:
            waits += B_(n).for_read()
        for n in wr:
            waits += B_(n).for_write()
        for b in banks_rd:
            if psb[b].w is not None:
                waits.append(psb[b].w)
        tok = k.emit(eng, fn, waits=waits, inc=(sem, 1))
        for n in rd:
            B_(n).read(tok)
        for n in wr:
            B_(n).wrote(tok)
        for b in banks_rd:
            psb[b].read(tok)
        return tok

    def outer_split(fast_src_ap, fast_name, slow_src_ap, slow_name, out3,
                    out_name, rd_extra=()):
        f3 = fast_src_ap.rearrange("p (h b) -> p h b", b=8)
        s3 = slow_src_ap.rearrange("p (h a) -> p h a", a=8)
        for aa in range(8):
            t_ve(lambda e, a_=aa: e.tensor_mul(
                out3[:, :, a_ * 8:(a_ + 1) * 8], f3,
                s3[:, :, a_:a_ + 1].broadcast_to([128, 16, 8])),
                wr=[out_name], rd=[fast_name, slow_name] + list(rd_extra))

    def ka():
        return

    def dbg_dump(name, src_ap, rd=(), banks_rd=()):
        if not dbg or name not in dbg_t:
            return
        scr = B_("dbgscr")
        shp = dbg_t[name].shape
        tmp = sb("dbgs_" + name, shp, F32)
        t_ve(lambda e, o=tmp, s=src_ap: e.tensor_copy(o[:], s),
             wr=["dbgs" + name], rd=list(rd), banks_rd=list(banks_rd))
        t_load("gpsimd", "do", dbg_t[name][:], tmp[:], rbufs=["dbgs" + name])

    def phase_consts():
        g = "gpsimd"
        t_load(g, "dg", p2_sb[:], d_p2[:], ["p2"])
        t_load(g, "dg", i128_sb[:], d_i128[:], ["i128"])
        t_load(g, "dg", tri_sb[:], d_tri[:], ["tri"])
        t_load(g, "dg", q2cl_sb[:], d_q2cl[:], ["q2cl"])
        t_load(g, "dg", repc_sb[:], d_repc[:], ["repc"])
        t_load(g, "dg", onesb_sb[:], d_onesbf[:], ["onesb"])
        t_load(g, "dg", onesf_sb[:], d_onesf32[:], ["onesf"])
        t_load(g, "dg", wqml_sb[:].rearrange("p (c j) -> p c j", j=128), d_wqml[:], ["wqml"])
        t_load(g, "dg", wkml_sb[:].rearrange("p (c j) -> p c j", j=128), d_wkml[:], ["wkml"])
        t_load(g, "dg", wul_sb[:].rearrange("p (c j) -> p c j", j=128), d_wul[:], ["wul"])
        t_load(g, "dg", vr_sb[:].rearrange("p (c j) -> p c j", j=32), d_vr[:], ["vr"])
        t_load(g, "dg", ct_sb[:], d_ct[:], ["ct"])
        t_load(g, "dg", b_sb[:], d_b[:], ["b"])
        t_load(g, "dg", h1_sb, d_h1[:], ["x1t"])
        t_load(g, "dg", h2_sb, d_h2[:], ["xt"])
        t_load(g, "dg", w74f_sb[:, 0:10], d_w74[0], ["w74f"])
        t_load(g, "dg", w74f_sb[:, 10:20], d_w74[1], ["w74f"])
        t_load(g, "dg", tmp1_sb[:], d_temp[None, :], ["tmpv"])

        t_ve(lambda e: e.memset(epsc_sb[:], LNEPS), wr=["epsc"])
        t_ve(lambda e: e.memset(bmat_sb[:], 0.0), wr=["bmat"])
        for e_i in range(8):
            t_ve(lambda e, i=e_i: e.tensor_mul(
                bscr_sb[:], h1_sb[:, i * 64:(i + 1) * 64],
                h2_sb[:, i * 64:(i + 1) * 64]),
                wr=["bscr"], rd=["x1t", "xt"])
            t_ve(lambda e: e.tensor_add(bmat_sb[:], bmat_sb[:], bscr_sb[:]),
                 wr=["bmat"], rd=["bscr", "bmat"])
        t_ve(lambda e: e.memset(mq_sb[:], 0.0), wr=["mq"])
        t_ve(lambda e: e.tensor_copy(mq_sb[0:64, 0:64], bmat_sb[:]),
             wr=["mq"], rd=["bmat"])
        t_ve(lambda e: e.tensor_copy(mq_sb[64:128, 64:128], bmat_sb[:]),
             wr=["mq"], rd=["bmat"])
        t_ve(lambda e: e.tensor_scalar_mul(w74_sb[:], w74f_sb[:], -2.0),
             wr=["w74"], rd=["w74f"])
        t_mm(7, ps[7][0:128, 0:1], onesf_sb[:], tmp1_sb[:], True, True,
             rbufs=["onesf", "tmpv"])
        t_ve(lambda e: e.tensor_scalar_max(negc_sb[:], ps[7][0:128, 0:1], 0.1),
             wr=["negc"], banks_rd=[7])
        t_ve(lambda e: e.reciprocal(negc_sb[:], negc_sb[:]),
             wr=["negc"], rd=["negc"])
        t_ve(lambda e: e.tensor_scalar_mul(negc_sb[:], negc_sb[:], -1.0),
             wr=["negc"], rd=["negc"])

    def ln_steps(src_name, src_ap, out_ap, out_name, par):
        sm = "stm%d" % par
        si = "sti%d" % par
        sn = "stn%d" % par
        ss = "sts%d" % par
        sqn = "stq%d" % par
        dummy = out_ap
        dn = out_name

        def s1():
            t_ve(lambda e: e.activation(
                dummy, src_ap, AF.Square, accum_out=st_sq[:, par:par + 1]),
                eng="scalar", wr=[dn, sqn], rd=[src_name])

        def s2():
            t_ve(lambda e: e.reduce_sum(st_sum[:, par:par + 1], src_ap, axis=AX),
                 wr=[ss], rd=[src_name])

        def s3():
            t_ve(lambda e: e.tensor_scalar_mul(
                st_mean[:, par:par + 1], st_sum[:, par:par + 1], 1.0 / D),
                wr=[sm], rd=[ss])

        def s4():
            t_ve(lambda e: e.tensor_mul(
                st_var[:, par:par + 1], st_mean[:, par:par + 1],
                st_mean[:, par:par + 1]), wr=["stv%d" % par], rd=[sm])

        def s5():
            t_ve(lambda e: e.scalar_tensor_tensor(
                st_var2[:, par:par + 1], st_sq[:, par:par + 1], 1.0 / D,
                st_var[:, par:par + 1], OP.mult, OP.subtract),
                wr=["st2%d" % par], rd=[sqn, "stv%d" % par])

        def s6():
            t_ve(lambda e: e.activation(
                st_inv[:, par:par + 1], st_var2[:, par:par + 1], AF.Ln,
                bias=epsc_sb[:]), eng="scalar", wr=[si],
                rd=["st2%d" % par, "epsc"])

        def s7():
            t_ve(lambda e: e.activation(
                st_inv[:, par:par + 1], st_inv[:, par:par + 1], AF.Exp,
                scale=-0.5), eng="scalar", wr=[si], rd=[si])

        def s8():
            t_ve(lambda e: e.scalar_tensor_tensor(
                st_nb[:, par:par + 1], st_mean[:, par:par + 1], -1.0,
                st_inv[:, par:par + 1], OP.mult, OP.mult),
                wr=[sn], rd=[sm, si])

        def s9():
            t_ve(lambda e: e.activation(
                out_ap, src_ap, AF.Identity, bias=st_nb[:, par:par + 1],
                scale=st_inv[:, par:par + 1]),
                eng="scalar", wr=[out_name], rd=[src_name, sn, si])

        return [s1, s2, s3, s4, s5, s6, s7, s8, s9]

    def ln_tile(src_name, src_ap, out_ap, out_name, par=0):
        for s in ln_steps(src_name, src_ap, out_ap, out_name, par):
            s()

    def phase_p2_tile(tile):
        sl = slice(tile * 128, (tile + 1) * 128)
        t_tr("scalar", kmtm_sb[:], kmT_sb[:, sl], wbufs=["kmtm"], rbufs=["kmT"])
        kx4 = kxtm_sb_ap.rearrange("p (h c) -> p h c", c=128)
        km3 = kmtm_sb[:].rearrange("p (h i) -> p h i", i=8)
        for aa in range(8):
            t_ve(lambda e, a_=aa: e.tensor_mul(
                kx4[:, :, a_ * 8:(a_ + 1) * 8],
                km3, km3[:, :, a_:a_ + 1].broadcast_to([128, 16, 8])),
                wr=["sA0", "sA1"], rd=["kmtm"])
        t_ve(lambda e: e.tensor_copy(kx4[:, :, 64:72], km3),
             wr=["sA0", "sA1"], rd=["kmtm"])
        t_ve(lambda e: e.memset(kx4[:, :, 72:73], 1.0), wr=["sA0", "sA1"])
        t_ve(lambda e: e.tensor_mul(sq_sb[:, 0:128], kmtm_sb[:], kmtm_sb[:]),
             wr=["sq"], rd=["kmtm"])
        t_ve(lambda e: e.reduce_sum(
            red1_sb[:, 0:16],
            sq_sb[:, 0:128].rearrange("p (h i) -> p h i", i=8), axis=AX),
            wr=["red1"], rd=["sq"])
        t_ve(lambda e: e.tensor_scalar(
            kx4[:, :, 73:74].rearrange("p h c -> p (h c)"),
            red1_sb[:, 0:16], EPS, None, OP.mult),
            wr=["sA0", "sA1"], rd=["red1"])
        src = kxtm_sb_ap.rearrange("t (h p) -> t h p", p=128)
        dst = kxt_sb[:].rearrange("p (h t) -> p h t", t=T)[
            :, :, tile * 128:(tile + 1) * 128]
        t_tr("sync", dst, src, wbufs=["kxt%d" % tile], rbufs=["sA0", "sA1"])
        ka()

    def rope_och(och, xnb, xnn):
        par = och % 2
        for c in range(8):
            t_mm(0, ps[0][0:128, :],
                 wqs_sb[:, par * 1024 + c * 128:par * 1024 + (c + 1) * 128],
                 xnb[:, c * 512:(c + 1) * 512],
                 start=(c == 0), stop=(c == 7), rbufs=["wqs%d" % par, xnn])
        t_ve(lambda e: e.tensor_copy(qT_sb[:], ps[0][0:128, :]),
             wr=["qTb"], banks_rd=[0])
        for c in range(8):
            t_mm(1, ps[1][0:128, :],
                 wks_sb[:, par * 1024 + c * 128:par * 1024 + (c + 1) * 128],
                 xnb[:, c * 512:(c + 1) * 512],
                 start=(c == 0), stop=(c == 7), rbufs=["wks%d" % par, xnn])
        t_ve(lambda e: e.tensor_copy(kT_sb[:], ps[1][0:128, :]),
             wr=["kTb"], banks_rd=[1])
        t_mm(2, ps[2][0:128, :], p2_sb[:], qT_sb[:],
             start=True, stop=True, rbufs=["p2", "qTb"])
        t_mm(5, ps[5][0:128, :], p2_sb[:], kT_sb[:],
             start=True, stop=True, rbufs=["p2", "kTb"])
        t_ve(lambda e: e.tensor_mul(qc_sb[:], qT_sb[:], c2_sb[:]),
             wr=["qcb"], rd=["qTb", "c2"])
        t_ve(lambda e: e.tensor_mul(qs_sb[:], ps[2][0:128, :], s2_sb[:]),
             wr=["qsb"], rd=["s2"], banks_rd=[2])
        t_ve(lambda e: e.tensor_mul(kc_sb[:], kT_sb[:], c2_sb[:]),
             wr=["kcb"], rd=["kTb", "c2"])
        t_ve(lambda e: e.tensor_mul(ks_sb[:], ps[5][0:128, :], s2_sb[:]),
             wr=["ksb"], rd=["s2"], banks_rd=[5])
        t_mm(3, ps[3][0:128, :], wqml_sb[:, och * 128:(och + 1) * 128], qc_sb[:],
             start=(och == 0), stop=False, rbufs=["wqml", "qcb"], skip=True)
        t_mm(3, ps[3][0:128, :], wqml_sb[:, och * 128:(och + 1) * 128], qs_sb[:],
             start=False, stop=(och == 7), rbufs=["wqml", "qsb"], skip=True)
        t_mm(4, ps[4][0:128, :], wkml_sb[:, och * 128:(och + 1) * 128], kc_sb[:],
             start=(och == 0), stop=False, rbufs=["wkml", "kcb"], skip=True)
        t_mm(4, ps[4][0:128, :], wkml_sb[:, och * 128:(och + 1) * 128], ks_sb[:],
             start=False, stop=(och == 7), rbufs=["wkml", "ksb"], skip=True)

    def ln_group_chunks(b, g):
        xnb = xnT_sb if g % 2 == 0 else quT_sb
        xnn = "xq" if g % 2 == 0 else "quT"
        chunks = []
        for half in range(2):
            plan = []
            for par in range(2):
                tl = half * 2 + par
                tile = g * 4 + tl
                rows = slice(tile * 128, (tile + 1) * 128)
                xbuf = xt_sb if par == 0 else x1t_sb
                xbn = "xt" if par == 0 else "x1t"
                onb = xn_sb[:] if par == 0 else rsA_sb[:, 0:1024]
                onn = "xn" if par == 0 else "rsAa"
                plan.append((rows, xbuf, xbn, onb, onn, par, tl))

            def c_load(plan=plan):
                for rows, xbuf, xbn, _, _, _, _ in plan:
                    t_load("sync", "di", xbuf[:], dx[b, rows, :], [xbn])
            chunks.append(c_load)
            stepss = []

            def c_mksteps(plan=plan, stepss=stepss):
                for _, xbuf, xbn, onb, onn, par, _ in plan:
                    stepss.append(ln_steps(xbn, xbuf[:], onb, onn, par))
            chunks.append(c_mksteps)
            for i in range(9):
                def c_step(i=i, stepss=stepss):
                    for st_ in stepss:
                        st_[i]()
                chunks.append(c_step)

            def c_tr(plan=plan, xnb=xnb, xnn=xnn):
                for _, _, _, onb, onn, _, tl in plan:
                    t_tr("scalar",
                         xnb[:].rearrange("p (c t) -> p c t", t=512)[
                             :, :, tl * 128:(tl + 1) * 128],
                         onb.rearrange("t (c p) -> t c p", p=128),
                         wbufs=[xnn], rbufs=[onn])
            chunks.append(c_tr)
        return chunks

    def phase_p1(b, m0_och=None, m0_tail=None):
        t_load("gpsimd", "dg", wvof_sb[:].rearrange("p (c o) -> p c o", o=1024),
               d_wv[:], ["wvof"])
        t_load("gpsimd", "dg", c2_sb[:], d_c2[:, 0:512], ["c2"])
        t_load("gpsimd", "dg", s2_sb[:], d_s2[:, 0:512], ["s2"])
        for c in ln_group_chunks(b, 0):
            c()
        for g in range(NG):
            xnb = xnT_sb if g % 2 == 0 else quT_sb
            xnn = "xq" if g % 2 == 0 else "quT"
            nxt = ln_group_chunks(b, g + 1) if g + 1 < NG else []
            ni = 0
            for och in range(8):
                par = och % 2
                t_load("gpsimd", "dg",
                       wqs_sb[:, par * 1024:(par + 1) * 1024].rearrange(
                           "p (c j) -> p c j", j=128),
                       d_wq[och], ["wqs%d" % par])
                t_load("gpsimd", "dg",
                       wks_sb[:, par * 1024:(par + 1) * 1024].rearrange(
                           "p (c j) -> p c j", j=128),
                       d_wk[och], ["wks%d" % par])
                rope_och(och, xnb, xnn)
                take = 4 if och < 6 else len(nxt) - ni
                for _ in range(max(0, take)):
                    if ni < len(nxt):
                        nxt[ni]()
                        ni += 1
                if g == NG - 1 and m0_och is not None:
                    m0_och(2)
            while ni < len(nxt):
                nxt[ni]()
                ni += 1
            if g == NG - 1 and m0_och is not None:
                m0_och(-1)
            T_sl = slice(g * 512, (g + 1) * 512)
            t_ve(lambda e, s=T_sl: e.activation(
                qmT_sb[:, s], ps[3][0:128, :], AF.Tanh, scale=0.5),
                eng="scalar", wr=["qmT"], banks_rd=[3])
            t_ve(lambda e, s=T_sl: e.tensor_scalar(
                qmT_sb[:, s], qmT_sb[:, s], 0.5, 0.5, OP.mult, OP.add),
                wr=["qmT"], rd=["qmT"])
            t_ve(lambda e, s=T_sl: e.activation(
                kmT_sb[:, s], ps[4][0:128, :], AF.Tanh, scale=0.5),
                eng="scalar", wr=["kmT"], banks_rd=[4])
            t_ve(lambda e, s=T_sl: e.tensor_scalar(
                kmT_sb[:, s], kmT_sb[:, s], 0.5, 0.5, OP.mult, OP.add),
                wr=["kmT"], rd=["kmT"])
            if g + 1 < NG:
                T_n = slice((g + 1) * 512, (g + 2) * 512)
                t_load("gpsimd", "dg", c2_sb[:], d_c2[:, T_n], ["c2"])
                t_load("gpsimd", "dg", s2_sb[:], d_s2[:, T_n], ["s2"])
            for tl in range(4):
                tile = g * 4 + tl
                for hf in range(2):
                    bank = 5 + hf
                    for c in range(8):
                        t_mm(bank, ps[bank][0:128, :],
                             xnb[:, c * 512 + tl * 128: c * 512 + (tl + 1) * 128],
                             wvof_sb[:, c * 1024 + hf * 512: c * 1024 + (hf + 1) * 512],
                             start=(c == 0), stop=(c == 7), rbufs=[xnn, "wvof"])
                    dstv = vtm_sb[:].rearrange(
                        "p (tt h j) -> p tt h j", tt=NT, h=H)[
                        :, tile, hf * 8:(hf + 1) * 8, 0:64]
                    t_ve(lambda e, d=dstv, bk=bank: e.tensor_copy(
                        d, ps[bk][0:128, :].rearrange("p (h j) -> p h j", j=64)),
                        wr=["vtm"], banks_rd=[bank])
                dstone = vtm_sb[:].rearrange(
                    "p (tt h j) -> p tt h j", tt=NT, h=H)[:, tile, :, 64:65]
                t_ve(lambda e, d=dstone: e.memset(d, 1.0), wr=["vtm"])
                if g == NG - 1 and tl == 3 and m0_tail is not None:
                    m0_tail()
                phase_p2_tile(tile)

    def metric_chunks(b, g, pre=False):
        T_sl = slice(g * 512, (g + 1) * 512)
        qm_ap, qm_n = qmtm_sb[:], "qmtm"
        ut_ap, ut_n = utm_sb[:], "utm"
        uT_ap, uT_n = uT_sb[:], "uT"
        if pre:
            homes = [(scrA_sb[:, 2048:3072], "sA2"),
                     (scrA_sb[:, 3072:4096], "sA3")]
        else:
            homes = [(rsA_sb[:, 0:1024], "rsAa"),
                     (rsA_sb[:, 1024:2048], "rsAb")]
        qq_homes = homes
        qu_homes = homes
        qa_ap, qa_n = homes[0]
        ubank = 2
        early = []
        late = []

        def c_qmtr():
            t_tr("scalar", qm_ap.rearrange("p (tt t) -> p tt t", t=128),
                 qmT_sb[:, T_sl].rearrange("p (tt t) -> p tt t", t=128),
                 wbufs=[qm_n], rbufs=["qmT"])
        early.append(c_qmtr)
        for tl in range(4):
            def c_qqo(tl=tl):
                qqr, qqn = qq_homes[tl % 2]
                outer_split(qm_ap[:, tl * 128:(tl + 1) * 128], qm_n,
                            qm_ap[:, tl * 128:(tl + 1) * 128], qm_n,
                            qqr.rearrange("p (h c) -> p h c", c=64), qqn)
            late.append(c_qqo)

            def c_qqt(tl=tl):
                qqr, qqn = qq_homes[tl % 2]
                t_tr("scalar",
                     qq1T_sb[:].rearrange("p (c t) -> p c t", t=512)[
                         :, :, tl * 128:(tl + 1) * 128],
                     qqr.rearrange("t (c p) -> t c p", p=128),
                     wbufs=["xq"], rbufs=[qqn])
            late.append(c_qqt)

        def c_umm():
            for c in range(8):
                t_mm(ubank, ps[ubank][0:128, :],
                     wul_sb[:, c * 128:(c + 1) * 128],
                     qq1T_sb[:, c * 512:(c + 1) * 512],
                     start=(c == 0), stop=(c == 7), rbufs=["wul", "xq"])
            t_ve(lambda e: e.tensor_copy(uT_ap, ps[ubank][0:128, :]),
                 wr=[uT_n], banks_rd=[ubank])
        late.append(c_umm)

        def c_utr():
            t_tr("scalar", ut_ap.rearrange("p (tt t) -> p tt t", t=128),
                 uT_ap.rearrange("p (tt t) -> p tt t", t=128),
                 wbufs=[ut_n], rbufs=[uT_n])
        late.append(c_utr)

        def c_sq():
            t_ve(lambda e: e.tensor_mul(sq_sb[:], ut_ap, ut_ap),
                 wr=["sq"], rd=[ut_n])
            t_ve(lambda e: e.reduce_sum(
                red1_sb[:], sq_sb[:].rearrange("p (th i) -> p th i", i=8),
                axis=AX), wr=["red1"], rd=["sq"])
            t_ve(lambda e: e.tensor_mul(sq_sb[:], qm_ap, qm_ap),
                 wr=["sq"], rd=[qm_n])
            t_ve(lambda e: e.reduce_sum(
                red2_sb[:], sq_sb[:].rearrange("p (th i) -> p th i", i=8),
                axis=AX), wr=["red2"], rd=["sq"])
        late.append(c_sq)

        def c_qaux():
            qa5 = qa_ap.rearrange(
                "p (c tt l i) -> p c tt l i", c=2, tt=4, i=16)
            for c2 in range(2):
                t_ve(lambda e, c_=c2: e.scalar_tensor_tensor(
                    qa5[:, c_, :, :, 8:9].squeeze(3),
                    red2_sb[:].rearrange("p (tt c l) -> p tt c l", tt=4, c=2)[
                        :, :, c_, :],
                    EPS,
                    red1_sb[:].rearrange("p (tt c l) -> p tt c l", tt=4, c=2)[
                        :, :, c_, :],
                    OP.mult, OP.add),
                    wr=[qa_n], rd=["red1", "red2"])
                for tl in range(4):
                    t_ve(lambda e, c_=c2, t_=tl: e.tensor_copy(
                        qa5[:, c_, t_, :, 0:8],
                        qm_ap.rearrange(
                            "p (tt c l i) -> p tt c l i", tt=4, c=2, i=8)[
                            :, t_, c_, :, :]),
                        wr=[qa_n], rd=[qm_n])
                t_ve(lambda e, c_=c2: e.memset(
                    qa5[:, c_, :, :, 9:10].squeeze(3), 1.0), wr=[qa_n])

        for tl in range(4):
            def c_quo(tl=tl):
                qur, qun = qu_homes[tl % 2]
                outer_split(ut_ap[:, tl * 128:(tl + 1) * 128], ut_n,
                            qm_ap[:, tl * 128:(tl + 1) * 128], qm_n,
                            qur.rearrange("p (h c) -> p h c", c=64), qun)
            late.append(c_quo)
            def c_qut(tl=tl):
                qur, qun = qu_homes[tl % 2]
                t_tr("scalar",
                     quT_sb[:].rearrange("p (c t) -> p c t", t=512)[
                         :, :, tl * 128:(tl + 1) * 128],
                     qur.rearrange("t (c p) -> t c p", p=128),
                     wbufs=["quT"], rbufs=[qun])
            late.append(c_qut)
        late.append(c_qaux)

        def c_qat():
            for c2 in range(2):
                t_tr("scalar",
                     qaT_sb[:, c2 * 512:(c2 + 1) * 512].rearrange(
                         "p (tt q) -> p tt q", q=128),
                     qa_ap[:, c2 * 512:(c2 + 1) * 512].rearrange(
                         "t (tt p) -> t tt p", p=128),
                     wbufs=["qaT"], rbufs=[qa_n])
        late.append(c_qat)
        return early, late

    def phase_p4(b, g, nxt_early, mfeed):
        nch = 4 * (g + 1)

        def qf_build(h):
            ch = h // 2
            par = h % 2
            qfs = h % 2
            qf_ap = qf_sb[:, qfs * 512:(qfs + 1) * 512]
            qfn = "qf%d" % qfs
            t_mm(2, ps[2][0:64, :], mq_sb[:, par * 64:(par + 1) * 64],
                 qq1T_sb[:, ch * 512:(ch + 1) * 512],
                 start=True, stop=True, rbufs=["mq", "xq"], skip=True)
            t_mm(2, ps[2][64:74, :], w74_sb[:, par * 10:(par + 1) * 10],
                 quT_sb[:, ch * 512:(ch + 1) * 512],
                 start=True, stop=False, rbufs=["w74", "quT"], skip=True)
            t_mm(2, ps[2][64:74, :], q2cl_sb[:, h * 10:(h + 1) * 10],
                 qaT_sb[:, (h // 8) * 512:(h // 8 + 1) * 512],
                 start=False, stop=True, rbufs=["q2cl", "qaT"], skip=True)
            t_ve(lambda e, o=qf_ap: e.tensor_copy(o[0:74, :], ps[2][0:74, :]),
                 wr=[qfn], banks_rd=[2])

        ZB = (0, 1, 5)
        ei = 0
        qf_build(0)
        for h in range(H):
            qfn = "qf%d" % (h % 2)
            qf_ap = qf_sb[:, (h % 2) * 512:(h % 2 + 1) * 512]
            av_bank = 6 + (h % 2)

            def z_tile(c):
                zb = ZB[c % 3]
                diag = (c >= 4 * g)
                t_mm(zb, ps[zb][0:128, :],
                     kxt_sb[0:74, h * T + c * 128: h * T + (c + 1) * 128],
                     qf_ap[0:74, :],
                     start=True, stop=(not diag),
                     rbufs=["kxt%d" % c, qfn], skip=True)
                if diag:
                    lo = c * 128 - g * 512
                    t_mm(zb, ps[zb][0:128, lo:lo + 128], i128_sb[:], tri_sb[:],
                         start=False, stop=True, rbufs=["i128", "tri"], skip=True)
                es = c % 4
                e_ap = e_sb[:, es * 512:(es + 1) * 512]
                en = "e%d" % es
                lo = max(0, c * 128 - g * 512)
                if lo > 0:
                    t_ve(lambda e, a=e_ap, l=lo: e.memset(a[:, 0:l], 0.0),
                         wr=[en])
                t_ve(lambda e, a=e_ap, l=lo, bk=zb: e.activation(
                    a[:, l:512], ps[bk][0:128, l:512], AF.Exp, scale=negc_sb[:]),
                    eng="scalar", wr=[en], rd=["negc"], banks_rd=[zb])
                t_ve(lambda e, a=e_ap, l=lo: e.tensor_scalar_min(
                    a[:, l:512], a[:, l:512], 1.0), wr=[en], rd=[en])

            def av_tile(c):
                en = "e%d" % (c % 4)
                e_ap = e_sb[:, (c % 4) * 512:(c % 4 + 1) * 512]
                vsl = vtm_sb[:].rearrange(
                    "p (tt hh j) -> p tt hh j", tt=NT, hh=H)[:, c, h, :]
                t_mm(av_bank, ps[av_bank][0:65, :], vsl, e_ap,
                     start=(c == 0), stop=(c == nch - 1),
                     rbufs=["vtm", en], skip=True)

            for c in range(nch):
                z_tile(c)
                if c == 1 and h + 1 < H:
                    qf_build(h + 1)
                if c >= 1:
                    av_tile(c - 1)
            av_tile(nch - 1)
            if h >= 2 and ei < len(nxt_early):
                nxt_early[ei]()
                ei += 1
            rr = (h // 4) * 32
            rc = (h % 4) * 512
            t_ve(lambda e, r=rr, cc=rc, bk=av_bank: e.tensor_copy(
                rsA_sb[r:r + 1, cc:cc + 512], ps[bk][64:65, :]),
                wr=["rsAa" if (h % 4) < 2 else "rsAb"], banks_rd=[av_bank])
            ch = h // 2
            base = (h % 2) * 64
            t_ve(lambda e, bk=av_bank, c_=ch, b_=base: e.tensor_copy(
                au_sb[b_:b_ + 64, c_ * 512:(c_ + 1) * 512], ps[bk][0:64, :]),
                wr=["sA%d" % (ch // 2)], banks_rd=[av_bank])
        while ei < len(nxt_early):
            nxt_early[ei]()
            ei += 1
        srcr = rsA_sb[0:128:32, :].rearrange("r (s t) -> r s t", s=4)
        tok1 = t_load("sync", "di", d_rs[:].rearrange("(r s) t -> r s t", r=4),
                      srcr, rbufs=["rsAa", "rsAb"], wbufs=["drs"])
        t_load("sync", "di", rsT_sb[:], d_rs[:], wbufs=["rsT"], rbufs=["drs"],
               extra=[tok1])
        mfeed(1)
        rcp_ap = yt_sb[0:16, :]
        t_ve(lambda e, o=rcp_ap: e.reciprocal(o, rsT_sb[:]),
             wr=["yt"], rd=["rsT"])
        t_ve(lambda e, i=rcp_ap: e.tensor_copy(rcpb_sb, i),
             wr=["rep"], rd=["yt"])
        for c in range(8):
            t_mm(2, ps[2][0:128, :], repc_sb[:, c * 128:(c + 1) * 128],
                 rcpb_sb, start=True, stop=True, rbufs=["repc", "rep"])
            t_ve(lambda e: e.tensor_copy(spr2_ap, ps[2][0:128, :]),
                 wr=["uT"], banks_rd=[2])
            t_ve(lambda e, c_=c: e.tensor_mul(
                au_sb[:, c_ * 512:(c_ + 1) * 512],
                au_sb[:, c_ * 512:(c_ + 1) * 512], spr2_ap),
                wr=["sA%d" % (c // 2)], rd=["sA%d" % (c // 2), "uT"])
            mfeed(1)

    def phase_p5(b, g, mfeed):
        for tl in range(4):
            tile = g * 4 + tl
            rows = slice(tile * 128, (tile + 1) * 128)
            t_load("sync", "di", xorig_sb[:], dx[b, rows, :], ["xt"])
            for hf in range(2):
                bank = 3 + hf
                for c in range(8):
                    t_mm(bank, ps[bank][0:128, :],
                         au_sb[:, c * 512 + tl * 128: c * 512 + (tl + 1) * 128],
                         wvof_sb[:, c * 1024 + hf * 512: c * 1024 + (hf + 1) * 512],
                         start=(c == 0), stop=(c == 7),
                         rbufs=["sA%d" % (c // 2), "wvof"])
                t_ve(lambda e, bk=bank, h_=hf: e.scalar_tensor_tensor(
                    x1t_sb[:, h_ * 512:(h_ + 1) * 512], ps[bk][0:128, :], 1.0,
                    xorig_sb[:, h_ * 512:(h_ + 1) * 512], OP.mult, OP.add),
                    wr=["x1t"], rd=["xt"], banks_rd=[bank])
                mfeed(1)
            t_load("gpsimd", "dg", d_x1[b, rows, :], x1t_sb[:], rbufs=["x1t"])
            ln_tile("x1t", x1t_sb[:], xn_sb[:], "xn", par=0)
            t_load("gpsimd", "dg", d_xnt[g, tl],
                   xn_sb[:], wbufs=["dxnt%d" % g], rbufs=["xn"])
            mfeed(1)
        mfeed(-1)

    def p6_prefetch(b, g, use_e=False):
        xnb = xnT_sb if g % 2 == 0 else quT_sb
        xnn = "xq" if g % 2 == 0 else "quT"
        if use_e:
            scrs = [(e_sb[:, 0:1024], ["e0", "e1"]),
                    (e_sb[:, 1024:2048], ["e2", "e3"])] * 2
        else:
            scrs = [(xn_sb[:], ["xn"]), (rsA_sb[:, 0:1024], ["rsAa"]),
                    (rsA_sb[:, 1024:2048], ["rsAb"]),
                    (scrA_sb[:, 0:1024], ["sA0"])]
        for tl in range(4):
            scr, scrn = scrs[(g % 2) * 2 + (tl % 2)]
            t_load("sync", "di", scr, d_xnt[g, tl],
                   wbufs=scrn, rbufs=["dxnt%d" % g])
            t_tr("scalar",
                 xnb[:].rearrange("p (c t) -> p c t", t=512)[
                     :, :, tl * 128:(tl + 1) * 128],
                 scr.rearrange("t (c p) -> t c p", p=128),
                 wbufs=[xnn], rbufs=scrn)

    def phase_p6(b, g):
        xnb = xnT_sb if g % 2 == 0 else quT_sb
        xnn = "xq" if g % 2 == 0 else "quT"
        ka()
        for c in range(8):
            t_mm(2, ps[2][0:32, :], vr_sb[:, c * 32:(c + 1) * 32],
                 xnb[:, c * 512:(c + 1) * 512],
                 start=(c == 0), stop=(c == 7), rbufs=["vr", xnn])
        t_ve(lambda e: e.tensor_copy(z1T_sb, ps[2][0:32, :]),
             wr=["rep"], banks_rd=[2])
        for tl in range(4):
            tile = g * 4 + tl
            rows = slice(tile * 128, (tile + 1) * 128)
            par = tl % 2
            xbuf = x1t_sb if par == 0 else xt_sb
            xbn = "x1t" if par == 0 else "xt"
            t_load("sync", "di", xbuf[:], d_x1[b, rows, :], wbufs=[xbn],
                   extra=k.qall("dg"))
            for hf in range(2):
                bank = 3 + hf
                for c in range(8):
                    t_mm(bank, ps[bank][0:128, :],
                         xnb[:, c * 512 + tl * 128: c * 512 + (tl + 1) * 128],
                         wvof_sb[:, c * 1024 + hf * 512: c * 1024 + (hf + 1) * 512],
                         start=(c == 0), stop=False, rbufs=[xnn, "wvof"], skip=True)
                t_mm(bank, ps[bank][0:128, :],
                     z1T_sb[0:32, tl * 128:(tl + 1) * 128],
                     ct_sb[0:32, hf * 512:(hf + 1) * 512],
                     start=False, stop=False, rbufs=["rep", "ct"], skip=True)
                t_mm(bank, ps[bank][0:128, :], onesb_sb[:],
                     b_sb[:, hf * 512:(hf + 1) * 512],
                     start=False, stop=True, rbufs=["onesb", "b"], skip=True)
                t_ve(lambda e, bk=bank, h_=hf, xb=xbuf: e.scalar_tensor_tensor(
                    yt_sb[:], ps[bk][0:128, :], 1.0,
                    xb[:, h_ * 512:(h_ + 1) * 512], OP.mult, OP.add),
                    wr=["yt"], rd=[xbn], banks_rd=[bank])
                tok = k.dma("sync", "do", dy[b, rows, hf * 512:(hf + 1) * 512],
                            yt_sb[:], waits=B_("yt").for_write())
                B_("yt").read(tok)
        if g + 2 < NG:
            p6_prefetch(b, g + 2)

    def mk_feeder(chunks):
        st = {"i": 0}
        def feed(n):
            if n < 0:
                n = len(chunks)
            while n > 0 and st["i"] < len(chunks):
                chunks[st["i"]]()
                st["i"] += 1
                n -= 1
        return feed

    phase_consts()
    for b in range(NSEQ):
        me, ml = metric_chunks(b, 0, pre=True)
        m0_och = mk_feeder(me + ml[:11])
        ml_tail = ml[11:]

        def m0_tail(ml_tail=ml_tail):
            for c_fn in ml_tail:
                c_fn()
        phase_p1(b, m0_och, m0_tail)
        t_load("gpsimd", "dg", wvof_sb[:].rearrange("p (c o) -> p c o", o=1024),
               d_wo[:], ["wvof"])
        for g in range(NG):
            if g + 1 < NG:
                ne, nl = metric_chunks(b, g + 1)
            else:
                ne, nl = [], []
            mf = mk_feeder(nl)
            phase_p4(b, g, ne, mf)
            if g == NG - 1:
                p6_prefetch(b, 0, use_e=True)
                p6_prefetch(b, 1, use_e=True)
            phase_p5(b, g, mf)
        t_load("gpsimd", "dg", wvof_sb[:].rearrange("p (c o) -> p c o", o=1024),
               d_wf[:], ["wvof"])
        for g in range(NG):
            phase_p6(b, g)
    k.emit("sync", lambda e: e.nop(), waits=tuple(k.qall("do")))
    k.build_block()
    return nc



def kernel(**inputs):
    consts = _host_consts()
    lays = _host_layouts(inputs)
    x = np.ascontiguousarray(np.asarray(inputs["L0_attn_x"], np.float32))

    nc = build_nc(dbg=_DEBUG)
    shared = {}
    shared.update({kk: np.ascontiguousarray(v) for kk, v in consts.items()})
    shared.update({kk: np.ascontiguousarray(v) for kk, v in lays.items()})
    in_maps = []
    for ci in range(NCORE):
        m = dict(shared)
        m["x"] = x[ci * NSEQ:(ci + 1) * NSEQ]
        in_maps.append(m)
    res = run_bass_kernel_spmd(nc, in_maps, core_ids=list(range(NCORE)))
    out = np.empty((B_TOT, T, D), np.float32)
    for ci in range(NCORE):
        out[ci * NSEQ:(ci + 1) * NSEQ] = res.results[ci]["y"]
    if _DEBUG:
        kernel._dbg = [{kk: np.asarray(v) for kk, v in r.items() if kk.startswith("dbg_")}
                       for r in res.results]
    return out


# version 35
# speedup vs baseline: 1.0688x; 1.0182x over previous
import numpy as np
import ml_dtypes

import concourse.bass as bass
import concourse.mybir as mybir
from concourse.bass_utils import run_bass_kernel_spmd

F32 = mybir.dt.float32
BF16 = mybir.dt.bfloat16
AX = mybir.AxisListType.X
OP = mybir.AluOpType
AF = mybir.ActivationFunctionType
BFNP = ml_dtypes.bfloat16

NCORE = 8
B_TOT = 16
NSEQ = 2
T = 2048
D = 1024
H = 16
NT = 16
NG = 4
EPS = 1e-3
LNEPS = 1e-5
BIG = float(2.0 ** 100)

_DEBUG = False
MHOMES = {}



def _host_consts():
    c = {}
    inv_freq = 1.0 / 10000.0 ** (np.arange(0, 64, 2, dtype=np.float32) / 64)
    t = np.arange(T, dtype=np.float32)
    freqs = np.outer(t, inv_freq)
    cos = np.cos(freqs).astype(np.float32)
    sin = np.sin(freqs).astype(np.float32)
    C2 = np.zeros((128, T), np.float32)
    S2 = np.zeros((128, T), np.float32)
    for h2 in range(2):
        for j in range(64):
            C2[h2 * 64 + j] = cos[:, j % 32]
            S2[h2 * 64 + j] = -sin[:, j % 32] if j < 32 else sin[:, j - 32]
    c["c2"] = C2.astype(BFNP)
    c["s2"] = S2.astype(BFNP)

    P2 = np.zeros((128, 128), np.float32)
    for h2 in range(2):
        for i in range(64):
            P2[h2 * 64 + ((i + 32) % 64), h2 * 64 + i] = 1.0
    c["p2"] = P2.astype(BFNP)
    c["i128"] = np.eye(128, dtype=np.float32).astype(BFNP)
    c["tri"] = np.where(np.arange(128)[:, None] > np.arange(128)[None, :],
                        BIG, 0.0).astype(np.float32).astype(BFNP)

    q2cl = np.zeros((16, 128, 10), np.float32)
    for h in range(16):
        hl = h % 8
        for d in range(8):
            q2cl[h, hl * 16 + d, d] = -2.0 * EPS
        q2cl[h, hl * 16 + 8, 8] = 1.0
        q2cl[h, hl * 16 + 9, 9] = 1.0
    c["q2cl"] = np.ascontiguousarray(
        q2cl.transpose(1, 0, 2).reshape(128, 160)).astype(BFNP)

    rep = np.zeros((8, 16, 128), np.float32)
    for cc in range(8):
        rep[cc, 2 * cc, 0:64] = 1.0
        rep[cc, 2 * cc + 1, 64:128] = 1.0
    c["repc"] = np.ascontiguousarray(
        rep.transpose(1, 0, 2).reshape(16, 1024)).astype(BFNP)

    c["onesbf"] = np.ones((1, 128), BFNP)
    c["onesf32"] = np.ones((1, 128), np.float32)
    return c


def _host_layouts(ins):
    a = {}
    Wq = np.asarray(ins["L0_attn_Wq"], np.float32)
    Wk = np.asarray(ins["L0_attn_Wk"], np.float32)
    Wv = np.asarray(ins["L0_attn_Wv"], np.float32)
    Wo = np.asarray(ins["L0_attn_Wo"], np.float32)
    Wqm = np.asarray(ins["L0_attn_Wqm"], np.float32)
    Wkm = np.asarray(ins["L0_attn_Wkm"], np.float32)
    Wm = np.asarray(ins["L0_attn_W_metric"], np.float32)
    Wf = np.asarray(ins["L0_ffn_W"], np.float32)
    Vr = np.asarray(ins["L0_ffn_V_r"], np.float32)
    Cf = np.asarray(ins["L0_ffn_C"], np.float32)
    bf = np.asarray(ins["L0_ffn_b"], np.float32)

    def slab(W):
        return np.ascontiguousarray(
            W.reshape(8, 128, 8, 128).transpose(2, 1, 0, 3))

    a["wq_sl"] = slab(Wq)
    a["wk_sl"] = slab(Wk)
    a["wv_r"] = np.ascontiguousarray(Wv.reshape(8, 128, D).transpose(1, 0, 2))
    a["wo_r"] = np.ascontiguousarray(Wo.reshape(8, 128, D).transpose(1, 0, 2))
    a["wf_r"] = np.ascontiguousarray(
        np.ascontiguousarray(Wf.T).reshape(8, 128, D).transpose(1, 0, 2))
    a["vr_r"] = np.ascontiguousarray(Vr.reshape(8, 128, 32).transpose(1, 0, 2))
    a["ct_r"] = np.ascontiguousarray(Cf.T)
    a["b_r"] = np.ascontiguousarray(bf.reshape(1, D))

    def mlift(W8):
        L = np.zeros((D, 128), np.float32)
        for h in range(16):
            L[h * 64:(h + 1) * 64, h * 8:(h + 1) * 8] = W8
        return np.ascontiguousarray(L.reshape(8, 128, 128).transpose(1, 0, 2))

    a["wqm_l"] = mlift(Wqm)
    a["wkm_l"] = mlift(Wkm)
    Wu = np.zeros((64, 8), np.float32)
    for aa in range(8):
        for bb in range(8):
            Wu[aa * 8 + bb, :] = Wm[aa, bb * 8: bb * 8 + 8]
    a["wu_l"] = mlift(Wu)

    w74 = np.zeros((2, 128, 10), np.float32)
    for par in range(2):
        for m in range(8):
            for e in range(8):
                for d in range(8):
                    w74[par, par * 64 + m * 8 + e, d] = Wm[m, d * 8 + e]
    a["wm3r74"] = np.ascontiguousarray(w74)

    W3 = Wm.reshape(8, 8, 8)
    H1 = np.zeros((64, 8, 64), np.float32)
    H2 = np.zeros((64, 8, 64), np.float32)
    ab_a = np.arange(64) // 8
    ab_b = np.arange(64) % 8
    xy_x = np.arange(64) // 8
    xy_y = np.arange(64) % 8
    for e in range(8):
        H1[:, e, :] = W3[ab_a][:, xy_x, e]
        H2[:, e, :] = W3[ab_b][:, xy_y, e]
    a["h1"] = np.ascontiguousarray(H1.reshape(64, 512))
    a["h2"] = np.ascontiguousarray(H2.reshape(64, 512))

    a["temp"] = np.asarray(ins["L0_attn_temperature"], np.float32).reshape(1)
    return a



ENGS = ("sync", "vector", "scalar", "tensor", "gpsimd")


class Sem:
    def __init__(self, h, dma=False):
        self.h = h
        self.n = 0
        self.dma = dma

    def bump(self, kk):
        self.n += kk
        return self.n


class Buf:
    def __init__(self):
        self.w = None
        self.r = []

    def for_write(self):
        out = list(self.r)
        if self.w is not None:
            out.append(self.w)
        return out

    def wrote(self, tok):
        self.w = tok
        self.r = []

    def for_read(self):
        return [self.w] if self.w is not None else []

    def read(self, tok):
        self.r.append(tok)


NQSEM = 16


class K:
    def __init__(self, nc):
        self.nc = nc
        self.q = {e: [] for e in ENGS}
        self.sems = {}
        self.qsems = {}
        self.qcnt = {}
        for name in ("te", "ve", "se"):
            self.sems[name] = Sem(nc.alloc_semaphore(name=name))
        for name in ("di", "dt", "dts", "dg", "do"):
            self.qsems[name] = [Sem(nc.alloc_semaphore(name=f"{name}{i}"),
                                    dma=True) for i in range(NQSEM)]
            self.qcnt[name] = 0

    def qall(self, name):
        return [(s, s.n) for s in self.qsems[name] if s.n > 0]

    def emit(self, eng, fn, waits=(), inc=None):
        wd = {}
        for s, t in waits:
            if s.dma:
                t = s.n
            if s not in wd or t > wd[s]:
                wd[s] = t
        witems = tuple(wd.items())
        tok = None
        if inc is not None:
            tok = (inc[0], inc[0].bump(inc[1]))

        def g(e, fn=fn, witems=witems, inc=inc):
            for s, t in witems:
                e.wait_ge(s.h, t)
            ins = fn(e)
            if inc is not None and ins is not None:
                ins.then_inc(inc[0].h, inc[1])

        self.q[eng].append(g)
        return tok

    def _qsem(self, semn):
        s = self.qsems[semn][self.qcnt[semn] % NQSEM]
        self.qcnt[semn] += 1
        return s

    def dma(self, eng, semn, out_ap, in_ap, waits=()):
        return self.emit(eng, lambda e, o=out_ap, i=in_ap:
                         e.dma_start(out=o, in_=i),
                         waits=waits, inc=(self._qsem(semn), 16))

    def dma_t(self, eng, semn, out_ap, in_ap, waits=()):
        return self.emit(eng, lambda e, o=out_ap, i=in_ap:
                         e.dma_start(out=o, in_=i, transpose=True),
                         waits=waits, inc=(self._qsem(semn), 16))

    def build_block(self):
        nc = self.nc
        with nc.Block() as block:
            @block.sync
            def _(sync):
                for f in self.q["sync"]:
                    f(sync)

            @block.vector
            def _(vector):
                for f in self.q["vector"]:
                    f(vector)

            @block.scalar
            def _(scalar):
                for f in self.q["scalar"]:
                    f(scalar)

            @block.tensor
            def _(tensor):
                for f in self.q["tensor"]:
                    f(tensor)

            @block.gpsimd
            def _(gpsimd):
                for f in self.q["gpsimd"]:
                    f(gpsimd)



def build_nc(dbg=False):
    nc = bass.Bass("TRN2", target_bir_lowering=False)
    k = K(nc)
    te, ve, se = k.sems["te"], k.sems["ve"], k.sems["se"]

    dx = nc.dram_tensor("x", [NSEQ, T, D], F32, kind="ExternalInput")
    dy = nc.dram_tensor("y", [NSEQ, T, D], F32, kind="ExternalOutput")
    d_c2 = nc.dram_tensor("c2", [128, T], BF16, kind="ExternalInput")
    d_s2 = nc.dram_tensor("s2", [128, T], BF16, kind="ExternalInput")
    d_p2 = nc.dram_tensor("p2", [128, 128], BF16, kind="ExternalInput")
    d_i128 = nc.dram_tensor("i128", [128, 128], BF16, kind="ExternalInput")
    d_tri = nc.dram_tensor("tri", [128, 128], BF16, kind="ExternalInput")
    d_q2cl = nc.dram_tensor("q2cl", [128, 160], BF16, kind="ExternalInput")
    d_repc = nc.dram_tensor("repc", [16, 1024], BF16, kind="ExternalInput")
    d_onesbf = nc.dram_tensor("onesbf", [1, 128], BF16, kind="ExternalInput")
    d_onesf32 = nc.dram_tensor("onesf32", [1, 128], F32, kind="ExternalInput")
    d_wq = nc.dram_tensor("wq_sl", [8, 128, 8, 128], F32, kind="ExternalInput")
    d_wk = nc.dram_tensor("wk_sl", [8, 128, 8, 128], F32, kind="ExternalInput")
    d_wv = nc.dram_tensor("wv_r", [128, 8, 1024], F32, kind="ExternalInput")
    d_wo = nc.dram_tensor("wo_r", [128, 8, 1024], F32, kind="ExternalInput")
    d_wf = nc.dram_tensor("wf_r", [128, 8, 1024], F32, kind="ExternalInput")
    d_vr = nc.dram_tensor("vr_r", [128, 8, 32], F32, kind="ExternalInput")
    d_ct = nc.dram_tensor("ct_r", [32, 1024], F32, kind="ExternalInput")
    d_b = nc.dram_tensor("b_r", [1, D], F32, kind="ExternalInput")
    d_wqml = nc.dram_tensor("wqm_l", [128, 8, 128], F32, kind="ExternalInput")
    d_wkml = nc.dram_tensor("wkm_l", [128, 8, 128], F32, kind="ExternalInput")
    d_wul = nc.dram_tensor("wu_l", [128, 8, 128], F32, kind="ExternalInput")
    d_w74 = nc.dram_tensor("wm3r74", [2, 128, 10], F32, kind="ExternalInput")
    d_h1 = nc.dram_tensor("h1", [64, 512], F32, kind="ExternalInput")
    d_h2 = nc.dram_tensor("h2", [64, 512], F32, kind="ExternalInput")
    d_temp = nc.dram_tensor("temp", [1], F32, kind="ExternalInput")
    d_x1 = nc.dram_tensor("x1scr", [NSEQ, T, D], F32)
    d_rs = nc.dram_tensor("rsscr", [16, 512], BF16)
    d_xnt = nc.dram_tensor("xntscr", [NG, 4, 128, 1024], BF16)

    dbg_t = {}
    if dbg:
        dbg_t["rsA"] = nc.dram_tensor("dbg_rsA", [NSEQ, NG, 128, 2048], BF16,
                                      kind="ExternalOutput")
        dbg_t["rsT"] = nc.dram_tensor("dbg_rsT", [NSEQ, NG, 16, 512], BF16,
                                      kind="ExternalOutput")
        dbg_t["auU"] = nc.dram_tensor("dbg_auU", [NSEQ, NG, 128, 4096], BF16,
                                      kind="ExternalOutput")
        dbg_t["auN"] = nc.dram_tensor("dbg_auN", [NSEQ, NG, 128, 4096], BF16,
                                      kind="ExternalOutput")
        dbg_t["qf"] = nc.dram_tensor("dbg_qf", [NSEQ, NG, H, 74, 512], BF16,
                                     kind="ExternalOutput")

    sb = nc.alloc_sbuf_tensor
    wvof_sb = sb("wvof_sb", [128, 8 * 1024], BF16)
    wqml_sb = sb("wqml_sb", [128, 8 * 128], BF16)
    wkml_sb = sb("wkml_sb", [128, 8 * 128], BF16)
    wul_sb = sb("wul_sb", [128, 8 * 128], BF16)
    vr_sb = sb("vr_sb", [128, 8 * 32], BF16)
    ct_sb = sb("ct_sb", [32, 1024], BF16)
    b_sb = sb("b_sb", [1, D], BF16)
    c2_sb = sb("c2_sb", [128, 512], BF16)
    s2_sb = sb("s2_sb", [128, 512], BF16)
    p2_sb = sb("p2_sb", [128, 128], BF16)
    i128_sb = sb("i128_sb", [128, 128], BF16)
    tri_sb = sb("tri_sb", [128, 128], BF16)
    q2cl_sb = sb("q2cl_sb", [128, 160], BF16)
    repc_sb = sb("repc_sb", [16, 1024], BF16)
    onesb_sb = sb("onesb_sb", [1, 128], BF16)
    onesf_sb = sb("onesf_sb", [1, 128], F32)
    w74_sb = sb("w74_sb", [128, 20], BF16)
    mq_sb = sb("mq_sb", [128, 128], BF16)
    negc_sb = sb("negc_sb", [128, 1], F32)
    epsc_sb = sb("epsc_sb", [128, 1], F32)
    bmat_sb = sb("bmat_sb", [64, 64], F32)
    bscr_sb = sb("bscr_sb", [64, 64], F32)
    w74f_sb = sb("w74f_sb", [128, 20], F32)
    tmp1_sb = sb("tmp1_sb", [1, 1], F32)
    wqs_sb = sb("wqs_sb", [128, 2 * 1024], BF16)
    wks_sb = sb("wks_sb", [128, 2 * 1024], BF16)
    kxt_sb = sb("kxt_sb", [128, H * T], BF16, is_transpose_dest=True)
    vtm_sb = sb("vtm_sb", [128, NT * 1040], BF16)
    qmT_sb = sb("qmT_sb", [128, T], BF16)
    kmT_sb = sb("kmT_sb", [128, T], BF16)
    xq_sb = sb("xq_sb", [128, 8 * 512], BF16, is_transpose_dest=True)
    xnT_sb = xq_sb
    qq1T_sb = xq_sb
    quT_sb = sb("quT_sb", [128, 8 * 512], BF16, is_transpose_dest=True)
    qaT_sb = sb("qaT_sb", [128, 2 * 512], BF16, is_transpose_dest=True)
    qf_sb = sb("qf_sb", [74, 2 * 512], BF16)
    e_sb = sb("e_sb", [128, 4 * 512], BF16)
    scrA_sb = sb("scrA_sb", [128, 8 * 512], BF16)
    au_sb = scrA_sb
    kxtm_sb_ap = scrA_sb[:, 0:2048]
    qq1tm_ap = scrA_sb[:, 2048:3072]
    qutm_ap = scrA_sb[:, 3072:4096]
    qaux_ap = scrA_sb[:, 0:1024]
    uT_sb = sb("uT_sb", [128, 512], BF16)
    spr2_ap = uT_sb[:]
    qmtm_sb = sb("qmtm_sb", [128, 512], BF16, is_transpose_dest=True)
    utm_sb = sb("utm_sb", [128, 512], BF16, is_transpose_dest=True)
    kmtm_sb = sb("kmtm_sb", [128, 128], BF16, is_transpose_dest=True)
    sq_sb = sb("sq_sb", [128, 512], BF16)
    rep_sb = sb("rep_sb", [128, 1024], BF16)
    red1_sb = sb("red1_sb", [128, 64], F32)
    red2_sb = sb("red2_sb", [128, 64], F32)
    x1t_sb = sb("x1t_sb", [128, D], F32)
    xt_sb = sb("xt_sb", [128, D], F32)
    xorig_sb = xt_sb
    h1_sb = x1t_sb[0:64, 0:512]
    h2_sb = xt_sb[0:64, 0:512]
    xn_sb = sb("xn_sb", [128, D], BF16)
    yt_sb = sb("yt_sb", [128, 512], F32)
    st_sum = sb("st_sum", [128, 8], F32)
    st_sq = sb("st_sq", [128, 8], F32)
    st_mean = sb("st_mean", [128, 2], F32)
    st_var = sb("st_var", [128, 2], F32)
    st_var2 = sb("st_var2", [128, 2], F32)
    st_inv = sb("st_inv", [128, 2], F32)
    st_nb = sb("st_nb", [128, 2], F32)
    qT_sb = sb("qT_sb", [128, 512], BF16)
    kT_sb = sb("kT_sb", [128, 512], BF16)
    qc_sb = sb("qc_sb", [128, 512], BF16)
    qs_sb = sb("qs_sb", [128, 512], BF16)
    kc_sb = sb("kc_sb", [128, 512], BF16)
    ks_sb = sb("ks_sb", [128, 512], BF16)
    rsA_sb = sb("rsA_sb", [128, 2048], BF16)
    rsT_sb = sb("rsT_sb", [16, 512], BF16)
    rcpb_sb = rep_sb[0:16, 0:512]
    z1T_sb = rep_sb[0:32, 512:1024]

    ps = [nc.alloc_psum_tensor(f"ps{i}", [128, 512], F32) for i in range(8)]
    psb = [Buf() for _ in range(8)]
    bufs = {}

    def B_(name):
        if name not in bufs:
            bufs[name] = Buf()
        return bufs[name]

    def t_load(eng, semn, out_ap, in_ap, wbufs=(), rbufs=(), extra=()):
        waits = list(extra)
        for n in wbufs:
            waits += B_(n).for_write()
        for n in rbufs:
            waits += B_(n).for_read()
        tok = k.dma(eng, semn, out_ap, in_ap, waits=waits)
        for n in wbufs:
            B_(n).wrote(tok)
        for n in rbufs:
            B_(n).read(tok)
        return tok

    def t_tr(eng, out_ap, in_ap, wbufs=(), rbufs=(), extra=()):
        waits = list(extra)
        for n in wbufs:
            waits += B_(n).for_write()
        for n in rbufs:
            waits += B_(n).for_read()
        tok = k.dma_t("scalar", "dt", out_ap, in_ap, waits=waits)
        for n in wbufs:
            B_(n).wrote(tok)
        for n in rbufs:
            B_(n).read(tok)
        return tok

    def t_mm(bank, out_ap, lhsT, rhs, start, stop, rbufs=(), extra=(), skip=False):
        waits = list(extra)
        if start:
            waits += psb[bank].for_write()
        for n in rbufs:
            waits += B_(n).for_read()
        tok = k.emit(
            "tensor",
            lambda e, o=out_ap, l=lhsT, r=rhs, st=start, sp=stop, sk=skip:
                e.matmul(o, l, r, start=st, stop=sp, skip_group_check=sk),
            waits=waits, inc=(te, 1))
        if start:
            psb[bank].wrote(tok)
        else:
            psb[bank].w = tok
        for n in rbufs:
            B_(n).read(tok)
        return tok

    def t_ve(fn, wr=(), rd=(), banks_rd=(), extra=(), eng="vector"):
        sem = ve if eng == "vector" else se
        waits = list(extra)
        for n in rd:
            waits += B_(n).for_read()
        for n in wr:
            waits += B_(n).for_write()
        for b in banks_rd:
            if psb[b].w is not None:
                waits.append(psb[b].w)
        tok = k.emit(eng, fn, waits=waits, inc=(sem, 1))
        for n in rd:
            B_(n).read(tok)
        for n in wr:
            B_(n).wrote(tok)
        for b in banks_rd:
            psb[b].read(tok)
        return tok

    def outer_split(fast_src_ap, fast_name, slow_src_ap, slow_name, out3,
                    out_name, rd_extra=()):
        f3 = fast_src_ap.rearrange("p (h b) -> p h b", b=8)
        s3 = slow_src_ap.rearrange("p (h a) -> p h a", a=8)
        for aa in range(8):
            t_ve(lambda e, a_=aa: e.tensor_mul(
                out3[:, :, a_ * 8:(a_ + 1) * 8], f3,
                s3[:, :, a_:a_ + 1].broadcast_to([128, 16, 8])),
                wr=[out_name], rd=[fast_name, slow_name] + list(rd_extra))

    def ka():
        return

    def dbg_dump(name, src_ap, rd=(), banks_rd=()):
        if not dbg or name not in dbg_t:
            return
        scr = B_("dbgscr")
        shp = dbg_t[name].shape
        tmp = sb("dbgs_" + name, shp, F32)
        t_ve(lambda e, o=tmp, s=src_ap: e.tensor_copy(o[:], s),
             wr=["dbgs" + name], rd=list(rd), banks_rd=list(banks_rd))
        t_load("gpsimd", "do", dbg_t[name][:], tmp[:], rbufs=["dbgs" + name])

    def phase_consts():
        g = "gpsimd"
        t_load(g, "dg", p2_sb[:], d_p2[:], ["p2"])
        t_load(g, "dg", i128_sb[:], d_i128[:], ["i128"])
        t_load(g, "dg", tri_sb[:], d_tri[:], ["tri"])
        t_load(g, "dg", q2cl_sb[:], d_q2cl[:], ["q2cl"])
        t_load(g, "dg", repc_sb[:], d_repc[:], ["repc"])
        t_load(g, "dg", onesb_sb[:], d_onesbf[:], ["onesb"])
        t_load(g, "dg", onesf_sb[:], d_onesf32[:], ["onesf"])
        t_load(g, "dg", wqml_sb[:].rearrange("p (c j) -> p c j", j=128), d_wqml[:], ["wqml"])
        t_load(g, "dg", wkml_sb[:].rearrange("p (c j) -> p c j", j=128), d_wkml[:], ["wkml"])
        t_load(g, "dg", wul_sb[:].rearrange("p (c j) -> p c j", j=128), d_wul[:], ["wul"])
        t_load(g, "dg", vr_sb[:].rearrange("p (c j) -> p c j", j=32), d_vr[:], ["vr"])
        t_load(g, "dg", ct_sb[:], d_ct[:], ["ct"])
        t_load(g, "dg", b_sb[:], d_b[:], ["b"])
        t_load(g, "dg", h1_sb, d_h1[:], ["x1t"])
        t_load(g, "dg", h2_sb, d_h2[:], ["xt"])
        t_load(g, "dg", w74f_sb[:, 0:10], d_w74[0], ["w74f"])
        t_load(g, "dg", w74f_sb[:, 10:20], d_w74[1], ["w74f"])
        t_load(g, "dg", tmp1_sb[:], d_temp[None, :], ["tmpv"])

        t_ve(lambda e: e.memset(epsc_sb[:], LNEPS), wr=["epsc"])
        t_ve(lambda e: e.memset(bmat_sb[:], 0.0), wr=["bmat"])
        for e_i in range(8):
            t_ve(lambda e, i=e_i: e.tensor_mul(
                bscr_sb[:], h1_sb[:, i * 64:(i + 1) * 64],
                h2_sb[:, i * 64:(i + 1) * 64]),
                wr=["bscr"], rd=["x1t", "xt"])
            t_ve(lambda e: e.tensor_add(bmat_sb[:], bmat_sb[:], bscr_sb[:]),
                 wr=["bmat"], rd=["bscr", "bmat"])
        t_ve(lambda e: e.memset(mq_sb[:], 0.0), wr=["mq"])
        t_ve(lambda e: e.tensor_copy(mq_sb[0:64, 0:64], bmat_sb[:]),
             wr=["mq"], rd=["bmat"])
        t_ve(lambda e: e.tensor_copy(mq_sb[64:128, 64:128], bmat_sb[:]),
             wr=["mq"], rd=["bmat"])
        t_ve(lambda e: e.tensor_scalar_mul(w74_sb[:], w74f_sb[:], -2.0),
             wr=["w74"], rd=["w74f"])
        t_mm(7, ps[7][0:128, 0:1], onesf_sb[:], tmp1_sb[:], True, True,
             rbufs=["onesf", "tmpv"])
        t_ve(lambda e: e.tensor_scalar_max(negc_sb[:], ps[7][0:128, 0:1], 0.1),
             wr=["negc"], banks_rd=[7])
        t_ve(lambda e: e.reciprocal(negc_sb[:], negc_sb[:]),
             wr=["negc"], rd=["negc"])
        t_ve(lambda e: e.tensor_scalar_mul(negc_sb[:], negc_sb[:], -1.0),
             wr=["negc"], rd=["negc"])

    def ln_steps(src_name, src_ap, out_ap, out_name, par):
        sm = "stm%d" % par
        si = "sti%d" % par
        sn = "stn%d" % par
        ss = "sts%d" % par
        sqn = "stq%d" % par
        dummy = out_ap
        dn = out_name

        def s1():
            t_ve(lambda e: e.activation(
                dummy, src_ap, AF.Square, accum_out=st_sq[:, par:par + 1]),
                eng="scalar", wr=[dn, sqn], rd=[src_name])

        def s2():
            t_ve(lambda e: e.reduce_sum(st_sum[:, par:par + 1], src_ap, axis=AX),
                 wr=[ss], rd=[src_name])

        def s3():
            t_ve(lambda e: e.tensor_scalar_mul(
                st_mean[:, par:par + 1], st_sum[:, par:par + 1], 1.0 / D),
                wr=[sm], rd=[ss])

        def s4():
            t_ve(lambda e: e.tensor_mul(
                st_var[:, par:par + 1], st_mean[:, par:par + 1],
                st_mean[:, par:par + 1]), wr=["stv%d" % par], rd=[sm])

        def s5():
            t_ve(lambda e: e.scalar_tensor_tensor(
                st_var2[:, par:par + 1], st_sq[:, par:par + 1], 1.0 / D,
                st_var[:, par:par + 1], OP.mult, OP.subtract),
                wr=["st2%d" % par], rd=[sqn, "stv%d" % par])

        def s6():
            t_ve(lambda e: e.activation(
                st_inv[:, par:par + 1], st_var2[:, par:par + 1], AF.Ln,
                bias=epsc_sb[:]), eng="scalar", wr=[si],
                rd=["st2%d" % par, "epsc"])

        def s7():
            t_ve(lambda e: e.activation(
                st_inv[:, par:par + 1], st_inv[:, par:par + 1], AF.Exp,
                scale=-0.5), eng="scalar", wr=[si], rd=[si])

        def s8():
            t_ve(lambda e: e.scalar_tensor_tensor(
                st_nb[:, par:par + 1], st_mean[:, par:par + 1], -1.0,
                st_inv[:, par:par + 1], OP.mult, OP.mult),
                wr=[sn], rd=[sm, si])

        def s9():
            t_ve(lambda e: e.activation(
                out_ap, src_ap, AF.Identity, bias=st_nb[:, par:par + 1],
                scale=st_inv[:, par:par + 1]),
                eng="scalar", wr=[out_name], rd=[src_name, sn, si])

        return [s1, s2, s3, s4, s5, s6, s7, s8, s9]

    def ln_tile(src_name, src_ap, out_ap, out_name, par=0):
        for s in ln_steps(src_name, src_ap, out_ap, out_name, par):
            s()

    def phase_p2_tile(tile):
        sl = slice(tile * 128, (tile + 1) * 128)
        t_tr("scalar", kmtm_sb[:], kmT_sb[:, sl], wbufs=["kmtm"], rbufs=["kmT"])
        kx4 = kxtm_sb_ap.rearrange("p (h c) -> p h c", c=128)
        km3 = kmtm_sb[:].rearrange("p (h i) -> p h i", i=8)
        for aa in range(8):
            t_ve(lambda e, a_=aa: e.tensor_mul(
                kx4[:, :, a_ * 8:(a_ + 1) * 8],
                km3, km3[:, :, a_:a_ + 1].broadcast_to([128, 16, 8])),
                wr=["sA0", "sA1"], rd=["kmtm"])
        t_ve(lambda e: e.tensor_copy(kx4[:, :, 64:72], km3),
             wr=["sA0", "sA1"], rd=["kmtm"])
        t_ve(lambda e: e.memset(kx4[:, :, 72:73], 1.0), wr=["sA0", "sA1"])
        t_ve(lambda e: e.tensor_mul(sq_sb[:, 0:128], kmtm_sb[:], kmtm_sb[:]),
             wr=["sq"], rd=["kmtm"])
        t_ve(lambda e: e.reduce_sum(
            red1_sb[:, 0:16],
            sq_sb[:, 0:128].rearrange("p (h i) -> p h i", i=8), axis=AX),
            wr=["red1"], rd=["sq"])
        t_ve(lambda e: e.tensor_scalar(
            kx4[:, :, 73:74].rearrange("p h c -> p (h c)"),
            red1_sb[:, 0:16], EPS, None, OP.mult),
            wr=["sA0", "sA1"], rd=["red1"])
        src = kxtm_sb_ap.rearrange("t (h p) -> t h p", p=128)
        dst = kxt_sb[:].rearrange("p (h t) -> p h t", t=T)[
            :, :, tile * 128:(tile + 1) * 128]
        t_tr("sync", dst, src, wbufs=["kxt%d" % tile], rbufs=["sA0", "sA1"])
        ka()

    def rope_och(och, xnb, xnn):
        par = och % 2
        for c in range(8):
            t_mm(0, ps[0][0:128, :],
                 wqs_sb[:, par * 1024 + c * 128:par * 1024 + (c + 1) * 128],
                 xnb[:, c * 512:(c + 1) * 512],
                 start=(c == 0), stop=(c == 7), rbufs=["wqs%d" % par, xnn])
        t_ve(lambda e: e.tensor_copy(qT_sb[:], ps[0][0:128, :]),
             wr=["qTb"], banks_rd=[0])
        for c in range(8):
            t_mm(1, ps[1][0:128, :],
                 wks_sb[:, par * 1024 + c * 128:par * 1024 + (c + 1) * 128],
                 xnb[:, c * 512:(c + 1) * 512],
                 start=(c == 0), stop=(c == 7), rbufs=["wks%d" % par, xnn])
        t_ve(lambda e: e.tensor_copy(kT_sb[:], ps[1][0:128, :]),
             wr=["kTb"], banks_rd=[1])
        t_mm(2, ps[2][0:128, :], p2_sb[:], qT_sb[:],
             start=True, stop=True, rbufs=["p2", "qTb"])
        t_mm(5, ps[5][0:128, :], p2_sb[:], kT_sb[:],
             start=True, stop=True, rbufs=["p2", "kTb"])
        t_ve(lambda e: e.tensor_mul(qc_sb[:], qT_sb[:], c2_sb[:]),
             wr=["qcb"], rd=["qTb", "c2"])
        t_ve(lambda e: e.tensor_mul(qs_sb[:], ps[2][0:128, :], s2_sb[:]),
             wr=["qsb"], rd=["s2"], banks_rd=[2])
        t_ve(lambda e: e.tensor_mul(kc_sb[:], kT_sb[:], c2_sb[:]),
             wr=["kcb"], rd=["kTb", "c2"])
        t_ve(lambda e: e.tensor_mul(ks_sb[:], ps[5][0:128, :], s2_sb[:]),
             wr=["ksb"], rd=["s2"], banks_rd=[5])
        t_mm(3, ps[3][0:128, :], wqml_sb[:, och * 128:(och + 1) * 128], qc_sb[:],
             start=(och == 0), stop=False, rbufs=["wqml", "qcb"], skip=True)
        t_mm(3, ps[3][0:128, :], wqml_sb[:, och * 128:(och + 1) * 128], qs_sb[:],
             start=False, stop=(och == 7), rbufs=["wqml", "qsb"], skip=True)
        t_mm(4, ps[4][0:128, :], wkml_sb[:, och * 128:(och + 1) * 128], kc_sb[:],
             start=(och == 0), stop=False, rbufs=["wkml", "kcb"], skip=True)
        t_mm(4, ps[4][0:128, :], wkml_sb[:, och * 128:(och + 1) * 128], ks_sb[:],
             start=False, stop=(och == 7), rbufs=["wkml", "ksb"], skip=True)

    def ln_group_chunks(b, g):
        xnb = xnT_sb if g % 2 == 0 else quT_sb
        xnn = "xq" if g % 2 == 0 else "quT"
        chunks = []
        for half in range(2):
            plan = []
            for par in range(2):
                tl = half * 2 + par
                tile = g * 4 + tl
                rows = slice(tile * 128, (tile + 1) * 128)
                xbuf = xt_sb if par == 0 else x1t_sb
                xbn = "xt" if par == 0 else "x1t"
                onb = xn_sb[:] if par == 0 else rsA_sb[:, 0:1024]
                onn = "xn" if par == 0 else "rsAa"
                plan.append((rows, xbuf, xbn, onb, onn, par, tl))

            def c_load(plan=plan):
                for rows, xbuf, xbn, _, _, _, _ in plan:
                    t_load("sync", "di", xbuf[:], dx[b, rows, :], [xbn])
            chunks.append(c_load)
            stepss = []

            def c_mksteps(plan=plan, stepss=stepss):
                for _, xbuf, xbn, onb, onn, par, _ in plan:
                    stepss.append(ln_steps(xbn, xbuf[:], onb, onn, par))
            chunks.append(c_mksteps)
            for i in range(9):
                def c_step(i=i, stepss=stepss):
                    for st_ in stepss:
                        st_[i]()
                chunks.append(c_step)

            def c_tr(plan=plan, xnb=xnb, xnn=xnn):
                for _, _, _, onb, onn, _, tl in plan:
                    t_tr("scalar",
                         xnb[:].rearrange("p (c t) -> p c t", t=512)[
                             :, :, tl * 128:(tl + 1) * 128],
                         onb.rearrange("t (c p) -> t c p", p=128),
                         wbufs=[xnn], rbufs=[onn])
            chunks.append(c_tr)
        return chunks

    def phase_p1(b, m0_och=None, m0_tail=None):
        t_load("gpsimd", "dg", wvof_sb[:].rearrange("p (c o) -> p c o", o=1024),
               d_wv[:], ["wvof"])
        t_load("gpsimd", "dg", c2_sb[:], d_c2[:, 0:512], ["c2"])
        t_load("gpsimd", "dg", s2_sb[:], d_s2[:, 0:512], ["s2"])
        for c in ln_group_chunks(b, 0):
            c()
        for g in range(NG):
            xnb = xnT_sb if g % 2 == 0 else quT_sb
            xnn = "xq" if g % 2 == 0 else "quT"
            nxt = ln_group_chunks(b, g + 1) if g + 1 < NG else []
            ni = 0
            for och in range(8):
                par = och % 2
                t_load("gpsimd", "dg",
                       wqs_sb[:, par * 1024:(par + 1) * 1024].rearrange(
                           "p (c j) -> p c j", j=128),
                       d_wq[och], ["wqs%d" % par])
                t_load("gpsimd", "dg",
                       wks_sb[:, par * 1024:(par + 1) * 1024].rearrange(
                           "p (c j) -> p c j", j=128),
                       d_wk[och], ["wks%d" % par])
                rope_och(och, xnb, xnn)
                take = 4 if och < 6 else len(nxt) - ni
                for _ in range(max(0, take)):
                    if ni < len(nxt):
                        nxt[ni]()
                        ni += 1
                if g == NG - 1 and m0_och is not None:
                    m0_och(2)
            while ni < len(nxt):
                nxt[ni]()
                ni += 1
            if g == NG - 1 and m0_och is not None:
                m0_och(-1)
            T_sl = slice(g * 512, (g + 1) * 512)
            t_ve(lambda e, s=T_sl: e.activation(
                qmT_sb[:, s], ps[3][0:128, :], AF.Tanh, scale=0.5),
                eng="scalar", wr=["qmT"], banks_rd=[3])
            t_ve(lambda e, s=T_sl: e.tensor_scalar(
                qmT_sb[:, s], qmT_sb[:, s], 0.5, 0.5, OP.mult, OP.add),
                wr=["qmT"], rd=["qmT"])
            t_ve(lambda e, s=T_sl: e.activation(
                kmT_sb[:, s], ps[4][0:128, :], AF.Tanh, scale=0.5),
                eng="scalar", wr=["kmT"], banks_rd=[4])
            t_ve(lambda e, s=T_sl: e.tensor_scalar(
                kmT_sb[:, s], kmT_sb[:, s], 0.5, 0.5, OP.mult, OP.add),
                wr=["kmT"], rd=["kmT"])
            if g + 1 < NG:
                T_n = slice((g + 1) * 512, (g + 2) * 512)
                t_load("gpsimd", "dg", c2_sb[:], d_c2[:, T_n], ["c2"])
                t_load("gpsimd", "dg", s2_sb[:], d_s2[:, T_n], ["s2"])
            for tl in range(4):
                tile = g * 4 + tl
                for hf in range(2):
                    bank = 5 + hf
                    for c in range(8):
                        t_mm(bank, ps[bank][0:128, :],
                             xnb[:, c * 512 + tl * 128: c * 512 + (tl + 1) * 128],
                             wvof_sb[:, c * 1024 + hf * 512: c * 1024 + (hf + 1) * 512],
                             start=(c == 0), stop=(c == 7), rbufs=[xnn, "wvof"])
                    dstv = vtm_sb[:].rearrange(
                        "p (tt h j) -> p tt h j", tt=NT, h=H)[
                        :, tile, hf * 8:(hf + 1) * 8, 0:64]
                    t_ve(lambda e, d=dstv, bk=bank: e.tensor_copy(
                        d, ps[bk][0:128, :].rearrange("p (h j) -> p h j", j=64)),
                        wr=["vtm"], banks_rd=[bank])
                dstone = vtm_sb[:].rearrange(
                    "p (tt h j) -> p tt h j", tt=NT, h=H)[:, tile, :, 64:65]
                t_ve(lambda e, d=dstone: e.memset(d, 1.0), wr=["vtm"])
                if g == NG - 1 and tl == 3 and m0_tail is not None:
                    m0_tail()
                phase_p2_tile(tile)

    def metric_chunks(b, g, pre=False):
        T_sl = slice(g * 512, (g + 1) * 512)
        qm_ap, qm_n = qmtm_sb[:], "qmtm"
        ut_ap, ut_n = utm_sb[:], "utm"
        uT_ap, uT_n = uT_sb[:], "uT"
        if pre:
            homes = [(scrA_sb[:, 2048:3072], "sA2"),
                     (scrA_sb[:, 3072:4096], "sA3")]
        else:
            homes = [(rsA_sb[:, 0:1024], "rsAa"),
                     (rsA_sb[:, 1024:2048], "rsAb")]
        qq_homes = homes
        qu_homes = homes
        qa_ap, qa_n = homes[0]
        ubank = 2
        early = []
        late = []

        def c_qmtr():
            t_tr("scalar", qm_ap.rearrange("p (tt t) -> p tt t", t=128),
                 qmT_sb[:, T_sl].rearrange("p (tt t) -> p tt t", t=128),
                 wbufs=[qm_n], rbufs=["qmT"])
        early.append(c_qmtr)
        for tl in range(4):
            def c_qqo(tl=tl):
                qqr, qqn = qq_homes[tl % 2]
                outer_split(qm_ap[:, tl * 128:(tl + 1) * 128], qm_n,
                            qm_ap[:, tl * 128:(tl + 1) * 128], qm_n,
                            qqr.rearrange("p (h c) -> p h c", c=64), qqn)
            late.append(c_qqo)

            def c_qqt(tl=tl):
                qqr, qqn = qq_homes[tl % 2]
                t_tr("scalar",
                     qq1T_sb[:].rearrange("p (c t) -> p c t", t=512)[
                         :, :, tl * 128:(tl + 1) * 128],
                     qqr.rearrange("t (c p) -> t c p", p=128),
                     wbufs=["xq"], rbufs=[qqn])
            late.append(c_qqt)

        def c_umm():
            for c in range(8):
                t_mm(ubank, ps[ubank][0:128, :],
                     wul_sb[:, c * 128:(c + 1) * 128],
                     qq1T_sb[:, c * 512:(c + 1) * 512],
                     start=(c == 0), stop=(c == 7), rbufs=["wul", "xq"])
            t_ve(lambda e: e.tensor_copy(uT_ap, ps[ubank][0:128, :]),
                 wr=[uT_n], banks_rd=[ubank])
        late.append(c_umm)

        def c_utr():
            t_tr("scalar", ut_ap.rearrange("p (tt t) -> p tt t", t=128),
                 uT_ap.rearrange("p (tt t) -> p tt t", t=128),
                 wbufs=[ut_n], rbufs=[uT_n])
        late.append(c_utr)

        def c_sq():
            t_ve(lambda e: e.tensor_mul(sq_sb[:], ut_ap, ut_ap),
                 wr=["sq"], rd=[ut_n])
            t_ve(lambda e: e.reduce_sum(
                red1_sb[:], sq_sb[:].rearrange("p (th i) -> p th i", i=8),
                axis=AX), wr=["red1"], rd=["sq"])
            t_ve(lambda e: e.tensor_mul(sq_sb[:], qm_ap, qm_ap),
                 wr=["sq"], rd=[qm_n])
            t_ve(lambda e: e.reduce_sum(
                red2_sb[:], sq_sb[:].rearrange("p (th i) -> p th i", i=8),
                axis=AX), wr=["red2"], rd=["sq"])
        late.append(c_sq)

        def c_qaux():
            qa5 = qa_ap.rearrange(
                "p (c tt l i) -> p c tt l i", c=2, tt=4, i=16)
            for c2 in range(2):
                t_ve(lambda e, c_=c2: e.scalar_tensor_tensor(
                    qa5[:, c_, :, :, 8:9].squeeze(3),
                    red2_sb[:].rearrange("p (tt c l) -> p tt c l", tt=4, c=2)[
                        :, :, c_, :],
                    EPS,
                    red1_sb[:].rearrange("p (tt c l) -> p tt c l", tt=4, c=2)[
                        :, :, c_, :],
                    OP.mult, OP.add),
                    wr=[qa_n], rd=["red1", "red2"])
                for tl in range(4):
                    t_ve(lambda e, c_=c2, t_=tl: e.tensor_copy(
                        qa5[:, c_, t_, :, 0:8],
                        qm_ap.rearrange(
                            "p (tt c l i) -> p tt c l i", tt=4, c=2, i=8)[
                            :, t_, c_, :, :]),
                        wr=[qa_n], rd=[qm_n])
                t_ve(lambda e, c_=c2: e.memset(
                    qa5[:, c_, :, :, 9:10].squeeze(3), 1.0), wr=[qa_n])

        for tl in range(4):
            def c_quo(tl=tl):
                qur, qun = qu_homes[tl % 2]
                outer_split(ut_ap[:, tl * 128:(tl + 1) * 128], ut_n,
                            qm_ap[:, tl * 128:(tl + 1) * 128], qm_n,
                            qur.rearrange("p (h c) -> p h c", c=64), qun)
            late.append(c_quo)
            def c_qut(tl=tl):
                qur, qun = qu_homes[tl % 2]
                t_tr("scalar",
                     quT_sb[:].rearrange("p (c t) -> p c t", t=512)[
                         :, :, tl * 128:(tl + 1) * 128],
                     qur.rearrange("t (c p) -> t c p", p=128),
                     wbufs=["quT"], rbufs=[qun])
            late.append(c_qut)
        late.append(c_qaux)

        def c_qat():
            for c2 in range(2):
                t_tr("scalar",
                     qaT_sb[:, c2 * 512:(c2 + 1) * 512].rearrange(
                         "p (tt q) -> p tt q", q=128),
                     qa_ap[:, c2 * 512:(c2 + 1) * 512].rearrange(
                         "t (tt p) -> t tt p", p=128),
                     wbufs=["qaT"], rbufs=[qa_n])
        late.append(c_qat)
        return early, late

    def phase_p4(b, g, nxt_early, mfeed):
        nch = 4 * (g + 1)

        def qf_build(h):
            ch = h // 2
            par = h % 2
            qfs = h % 2
            qf_ap = qf_sb[:, qfs * 512:(qfs + 1) * 512]
            qfn = "qf%d" % qfs
            t_mm(2, ps[2][0:64, :], mq_sb[:, par * 64:(par + 1) * 64],
                 qq1T_sb[:, ch * 512:(ch + 1) * 512],
                 start=True, stop=True, rbufs=["mq", "xq"], skip=True)
            t_mm(2, ps[2][64:74, :], w74_sb[:, par * 10:(par + 1) * 10],
                 quT_sb[:, ch * 512:(ch + 1) * 512],
                 start=True, stop=False, rbufs=["w74", "quT"], skip=True)
            t_mm(2, ps[2][64:74, :], q2cl_sb[:, h * 10:(h + 1) * 10],
                 qaT_sb[:, (h // 8) * 512:(h // 8 + 1) * 512],
                 start=False, stop=True, rbufs=["q2cl", "qaT"], skip=True)
            t_ve(lambda e, o=qf_ap: e.tensor_copy(o[0:74, :], ps[2][0:74, :]),
                 wr=[qfn], banks_rd=[2])

        ZB = (0, 1, 5)
        ei = 0
        qf_build(0)
        for h in range(H):
            qfn = "qf%d" % (h % 2)
            qf_ap = qf_sb[:, (h % 2) * 512:(h % 2 + 1) * 512]
            av_bank = 6 + (h % 2)

            def z_tile(c):
                zb = ZB[c % 3]
                diag = (c >= 4 * g)
                t_mm(zb, ps[zb][0:128, :],
                     kxt_sb[0:74, h * T + c * 128: h * T + (c + 1) * 128],
                     qf_ap[0:74, :],
                     start=True, stop=(not diag),
                     rbufs=["kxt%d" % c, qfn], skip=True)
                if diag:
                    lo = c * 128 - g * 512
                    t_mm(zb, ps[zb][0:128, lo:lo + 128], i128_sb[:], tri_sb[:],
                         start=False, stop=True, rbufs=["i128", "tri"], skip=True)
                es = c % 4
                e_ap = e_sb[:, es * 512:(es + 1) * 512]
                en = "e%d" % es
                lo = max(0, c * 128 - g * 512)
                if lo > 0:
                    t_ve(lambda e, a=e_ap, l=lo: e.memset(a[:, 0:l], 0.0),
                         wr=[en])
                t_ve(lambda e, a=e_ap, l=lo, bk=zb: e.activation(
                    a[:, l:512], ps[bk][0:128, l:512], AF.Exp, scale=negc_sb[:]),
                    eng="scalar", wr=[en], rd=["negc"], banks_rd=[zb])
                t_ve(lambda e, a=e_ap, l=lo: e.tensor_scalar_min(
                    a[:, l:512], a[:, l:512], 1.0), wr=[en], rd=[en])

            def av_tile(c):
                en = "e%d" % (c % 4)
                e_ap = e_sb[:, (c % 4) * 512:(c % 4 + 1) * 512]
                vsl = vtm_sb[:].rearrange(
                    "p (tt hh j) -> p tt hh j", tt=NT, hh=H)[:, c, h, :]
                t_mm(av_bank, ps[av_bank][0:65, :], vsl, e_ap,
                     start=(c == 0), stop=(c == nch - 1),
                     rbufs=["vtm", en], skip=True)

            for c in range(nch):
                z_tile(c)
                if c == 1 and h + 1 < H:
                    qf_build(h + 1)
                if c >= 1:
                    av_tile(c - 1)
            av_tile(nch - 1)
            if h >= 2 and ei < len(nxt_early):
                nxt_early[ei]()
                ei += 1
            rr = (h // 4) * 32
            rc = (h % 4) * 512
            t_ve(lambda e, r=rr, cc=rc, bk=av_bank: e.tensor_copy(
                rsA_sb[r:r + 1, cc:cc + 512], ps[bk][64:65, :]),
                wr=["rsAa" if (h % 4) < 2 else "rsAb"], banks_rd=[av_bank])
            ch = h // 2
            base = (h % 2) * 64
            t_ve(lambda e, bk=av_bank, c_=ch, b_=base: e.tensor_copy(
                au_sb[b_:b_ + 64, c_ * 512:(c_ + 1) * 512], ps[bk][0:64, :]),
                wr=["sA%d" % (ch // 2)], banks_rd=[av_bank])
        while ei < len(nxt_early):
            nxt_early[ei]()
            ei += 1
        srcr = rsA_sb[0:128:32, :].rearrange("r (s t) -> r s t", s=4)
        tok1 = t_load("sync", "di", d_rs[:].rearrange("(r s) t -> r s t", r=4),
                      srcr, rbufs=["rsAa", "rsAb"], wbufs=["drs"])
        t_load("sync", "di", rsT_sb[:], d_rs[:], wbufs=["rsT"], rbufs=["drs"],
               extra=[tok1])
        mfeed(1)
        rcp_ap = yt_sb[0:16, :]
        t_ve(lambda e, o=rcp_ap: e.reciprocal(o, rsT_sb[:]),
             wr=["yt"], rd=["rsT"])
        t_ve(lambda e, i=rcp_ap: e.tensor_copy(rcpb_sb, i),
             wr=["rep"], rd=["yt"])
        for c in range(8):
            t_mm(2, ps[2][0:128, :], repc_sb[:, c * 128:(c + 1) * 128],
                 rcpb_sb, start=True, stop=True, rbufs=["repc", "rep"])
            t_ve(lambda e: e.tensor_copy(spr2_ap, ps[2][0:128, :]),
                 wr=["uT"], banks_rd=[2])
            t_ve(lambda e, c_=c: e.tensor_mul(
                au_sb[:, c_ * 512:(c_ + 1) * 512],
                au_sb[:, c_ * 512:(c_ + 1) * 512], spr2_ap),
                wr=["sA%d" % (c // 2)], rd=["sA%d" % (c // 2), "uT"])
            mfeed(1)

    def phase_p5(b, g, mfeed):
        for tl in range(4):
            tile = g * 4 + tl
            rows = slice(tile * 128, (tile + 1) * 128)
            t_load("sync", "di", xorig_sb[:], dx[b, rows, :], ["xt"])
            for hf in range(2):
                bank = 3 + hf
                for c in range(8):
                    t_mm(bank, ps[bank][0:128, :],
                         au_sb[:, c * 512 + tl * 128: c * 512 + (tl + 1) * 128],
                         wvof_sb[:, c * 1024 + hf * 512: c * 1024 + (hf + 1) * 512],
                         start=(c == 0), stop=(c == 7),
                         rbufs=["sA%d" % (c // 2), "wvof"])
                t_ve(lambda e, bk=bank, h_=hf: e.scalar_tensor_tensor(
                    x1t_sb[:, h_ * 512:(h_ + 1) * 512], ps[bk][0:128, :], 1.0,
                    xorig_sb[:, h_ * 512:(h_ + 1) * 512], OP.mult, OP.add),
                    wr=["x1t"], rd=["xt"], banks_rd=[bank])
                mfeed(1)
            t_load("gpsimd", "dg", d_x1[b, rows, :], x1t_sb[:], rbufs=["x1t"])
            ln_tile("x1t", x1t_sb[:], xn_sb[:], "xn", par=0)
            t_load("gpsimd", "dg", d_xnt[g, tl],
                   xn_sb[:], wbufs=["dxnt%d" % g], rbufs=["xn"])
            mfeed(1)
        mfeed(-1)

    def p6_prefetch(b, g, use_e=False):
        xnb = xnT_sb if g % 2 == 0 else quT_sb
        xnn = "xq" if g % 2 == 0 else "quT"
        if use_e:
            scrs = [(e_sb[:, 0:1024], ["e0", "e1"]),
                    (e_sb[:, 1024:2048], ["e2", "e3"])] * 2
        else:
            scrs = [(xn_sb[:], ["xn"]), (rsA_sb[:, 0:1024], ["rsAa"]),
                    (rsA_sb[:, 1024:2048], ["rsAb"]),
                    (scrA_sb[:, 0:1024], ["sA0"])]
        for tl in range(4):
            scr, scrn = scrs[(g % 2) * 2 + (tl % 2)]
            t_load("sync", "di", scr, d_xnt[g, tl],
                   wbufs=scrn, rbufs=["dxnt%d" % g])
            t_tr("scalar",
                 xnb[:].rearrange("p (c t) -> p c t", t=512)[
                     :, :, tl * 128:(tl + 1) * 128],
                 scr.rearrange("t (c p) -> t c p", p=128),
                 wbufs=[xnn], rbufs=scrn)

    def phase_p6(b, g):
        xnb = xnT_sb if g % 2 == 0 else quT_sb
        xnn = "xq" if g % 2 == 0 else "quT"
        ka()
        for c in range(8):
            t_mm(2, ps[2][0:32, :], vr_sb[:, c * 32:(c + 1) * 32],
                 xnb[:, c * 512:(c + 1) * 512],
                 start=(c == 0), stop=(c == 7), rbufs=["vr", xnn])
        t_ve(lambda e: e.tensor_copy(z1T_sb, ps[2][0:32, :]),
             wr=["rep"], banks_rd=[2])
        for tl in range(4):
            tile = g * 4 + tl
            rows = slice(tile * 128, (tile + 1) * 128)
            par = tl % 2
            xbuf = x1t_sb if par == 0 else xt_sb
            xbn = "x1t" if par == 0 else "xt"
            t_load("sync", "di", xbuf[:], d_x1[b, rows, :], wbufs=[xbn],
                   extra=k.qall("dg"))
            for hf in range(2):
                bank = 3 + hf
                for c in range(8):
                    t_mm(bank, ps[bank][0:128, :],
                         xnb[:, c * 512 + tl * 128: c * 512 + (tl + 1) * 128],
                         wvof_sb[:, c * 1024 + hf * 512: c * 1024 + (hf + 1) * 512],
                         start=(c == 0), stop=False, rbufs=[xnn, "wvof"], skip=True)
                t_mm(bank, ps[bank][0:128, :],
                     z1T_sb[0:32, tl * 128:(tl + 1) * 128],
                     ct_sb[0:32, hf * 512:(hf + 1) * 512],
                     start=False, stop=False, rbufs=["rep", "ct"], skip=True)
                t_mm(bank, ps[bank][0:128, :], onesb_sb[:],
                     b_sb[:, hf * 512:(hf + 1) * 512],
                     start=False, stop=True, rbufs=["onesb", "b"], skip=True)
                t_ve(lambda e, bk=bank, h_=hf, xb=xbuf: e.scalar_tensor_tensor(
                    yt_sb[:], ps[bk][0:128, :], 1.0,
                    xb[:, h_ * 512:(h_ + 1) * 512], OP.mult, OP.add),
                    wr=["yt"], rd=[xbn], banks_rd=[bank])
                tok = k.dma("sync", "do", dy[b, rows, hf * 512:(hf + 1) * 512],
                            yt_sb[:], waits=B_("yt").for_write())
                B_("yt").read(tok)
        if g + 2 < NG:
            p6_prefetch(b, g + 2)

    def mk_feeder(chunks):
        st = {"i": 0}
        def feed(n):
            if n < 0:
                n = len(chunks)
            while n > 0 and st["i"] < len(chunks):
                chunks[st["i"]]()
                st["i"] += 1
                n -= 1
        return feed

    phase_consts()
    for b in range(NSEQ):
        me, ml = metric_chunks(b, 0, pre=True)
        m0_och = mk_feeder(me + ml[:11])
        ml_tail = ml[11:]

        def m0_tail(ml_tail=ml_tail):
            for c_fn in ml_tail:
                c_fn()
        phase_p1(b, m0_och, m0_tail)
        t_load("gpsimd", "dg", wvof_sb[:].rearrange("p (c o) -> p c o", o=1024),
               d_wo[:], ["wvof"])
        for g in range(NG):
            if g + 1 < NG:
                ne, nl = metric_chunks(b, g + 1)
            else:
                ne, nl = [], []
            mf = mk_feeder(nl)
            phase_p4(b, g, ne, mf)
            if g == NG - 1:
                p6_prefetch(b, 0, use_e=True)
                p6_prefetch(b, 1, use_e=True)
            phase_p5(b, g, mf)
        t_load("gpsimd", "dg", wvof_sb[:].rearrange("p (c o) -> p c o", o=1024),
               d_wf[:], ["wvof"])
        for g in range(NG):
            phase_p6(b, g)
    k.emit("sync", lambda e: e.nop(), waits=tuple(k.qall("do")))
    k.build_block()
    return nc



def kernel(**inputs):
    consts = _host_consts()
    lays = _host_layouts(inputs)
    x = np.ascontiguousarray(np.asarray(inputs["L0_attn_x"], np.float32))

    nc = build_nc(dbg=_DEBUG)
    shared = {}
    shared.update({kk: np.ascontiguousarray(v) for kk, v in consts.items()})
    shared.update({kk: np.ascontiguousarray(v) for kk, v in lays.items()})
    in_maps = []
    for ci in range(NCORE):
        m = dict(shared)
        m["x"] = x[ci * NSEQ:(ci + 1) * NSEQ]
        in_maps.append(m)
    res = run_bass_kernel_spmd(nc, in_maps, core_ids=list(range(NCORE)))
    out = np.empty((B_TOT, T, D), np.float32)
    for ci in range(NCORE):
        out[ci * NSEQ:(ci + 1) * NSEQ] = res.results[ci]["y"]
    if _DEBUG:
        kernel._dbg = [{kk: np.asarray(v) for kk, v in r.items() if kk.startswith("dbg_")}
                       for r in res.results]
    return out


# version 36
# speedup vs baseline: 1.0741x; 1.0050x over previous
import numpy as np
import ml_dtypes

import concourse.bass as bass
import concourse.mybir as mybir
from concourse.bass_utils import run_bass_kernel_spmd

F32 = mybir.dt.float32
BF16 = mybir.dt.bfloat16
AX = mybir.AxisListType.X
OP = mybir.AluOpType
AF = mybir.ActivationFunctionType
BFNP = ml_dtypes.bfloat16

NCORE = 8
B_TOT = 16
NSEQ = 2
T = 2048
D = 1024
H = 16
NT = 16
NG = 4
EPS = 1e-3
LNEPS = 1e-5
BIG = float(2.0 ** 100)

_DEBUG = False
MHOMES = {}



def _host_consts():
    c = {}
    inv_freq = 1.0 / 10000.0 ** (np.arange(0, 64, 2, dtype=np.float32) / 64)
    t = np.arange(T, dtype=np.float32)
    freqs = np.outer(t, inv_freq)
    cos = np.cos(freqs).astype(np.float32)
    sin = np.sin(freqs).astype(np.float32)
    C2 = np.zeros((128, T), np.float32)
    S2 = np.zeros((128, T), np.float32)
    for h2 in range(2):
        for j in range(64):
            C2[h2 * 64 + j] = cos[:, j % 32]
            S2[h2 * 64 + j] = -sin[:, j % 32] if j < 32 else sin[:, j - 32]
    c["c2"] = C2.astype(BFNP)
    c["s2"] = S2.astype(BFNP)

    P2 = np.zeros((128, 128), np.float32)
    for h2 in range(2):
        for i in range(64):
            P2[h2 * 64 + ((i + 32) % 64), h2 * 64 + i] = 1.0
    c["p2"] = P2.astype(BFNP)
    c["i128"] = np.eye(128, dtype=np.float32).astype(BFNP)
    c["tri"] = np.where(np.arange(128)[:, None] > np.arange(128)[None, :],
                        BIG, 0.0).astype(np.float32).astype(BFNP)

    q2cl = np.zeros((16, 128, 10), np.float32)
    for h in range(16):
        hl = h % 8
        for d in range(8):
            q2cl[h, hl * 16 + d, d] = -2.0 * EPS
        q2cl[h, hl * 16 + 8, 8] = 1.0
        q2cl[h, hl * 16 + 9, 9] = 1.0
    c["q2cl"] = np.ascontiguousarray(
        q2cl.transpose(1, 0, 2).reshape(128, 160)).astype(BFNP)

    rep = np.zeros((8, 16, 128), np.float32)
    for cc in range(8):
        rep[cc, 2 * cc, 0:64] = 1.0
        rep[cc, 2 * cc + 1, 64:128] = 1.0
    c["repc"] = np.ascontiguousarray(
        rep.transpose(1, 0, 2).reshape(16, 1024)).astype(BFNP)

    c["onesbf"] = np.ones((1, 128), BFNP)
    c["onesf32"] = np.ones((1, 128), np.float32)
    return c


def _host_layouts(ins):
    a = {}
    Wq = np.asarray(ins["L0_attn_Wq"], np.float32)
    Wk = np.asarray(ins["L0_attn_Wk"], np.float32)
    Wv = np.asarray(ins["L0_attn_Wv"], np.float32)
    Wo = np.asarray(ins["L0_attn_Wo"], np.float32)
    Wqm = np.asarray(ins["L0_attn_Wqm"], np.float32)
    Wkm = np.asarray(ins["L0_attn_Wkm"], np.float32)
    Wm = np.asarray(ins["L0_attn_W_metric"], np.float32)
    Wf = np.asarray(ins["L0_ffn_W"], np.float32)
    Vr = np.asarray(ins["L0_ffn_V_r"], np.float32)
    Cf = np.asarray(ins["L0_ffn_C"], np.float32)
    bf = np.asarray(ins["L0_ffn_b"], np.float32)

    def slab(W):
        return np.ascontiguousarray(
            W.reshape(8, 128, 8, 128).transpose(2, 1, 0, 3))

    a["wq_sl"] = slab(Wq)
    a["wk_sl"] = slab(Wk)
    a["wv_r"] = np.ascontiguousarray(Wv.reshape(8, 128, D).transpose(1, 0, 2))
    a["wo_r"] = np.ascontiguousarray(Wo.reshape(8, 128, D).transpose(1, 0, 2))
    a["wf_r"] = np.ascontiguousarray(
        np.ascontiguousarray(Wf.T).reshape(8, 128, D).transpose(1, 0, 2))
    a["vr_r"] = np.ascontiguousarray(Vr.reshape(8, 128, 32).transpose(1, 0, 2))
    a["ct_r"] = np.ascontiguousarray(Cf.T)
    a["b_r"] = np.ascontiguousarray(bf.reshape(1, D))

    def mlift(W8):
        L = np.zeros((D, 128), np.float32)
        for h in range(16):
            L[h * 64:(h + 1) * 64, h * 8:(h + 1) * 8] = W8
        return np.ascontiguousarray(L.reshape(8, 128, 128).transpose(1, 0, 2))

    a["wqm_l"] = mlift(Wqm)
    a["wkm_l"] = mlift(Wkm)
    Wu = np.zeros((64, 8), np.float32)
    for aa in range(8):
        for bb in range(8):
            Wu[aa * 8 + bb, :] = Wm[aa, bb * 8: bb * 8 + 8]
    a["wu_l"] = mlift(Wu)

    w74 = np.zeros((2, 128, 10), np.float32)
    for par in range(2):
        for m in range(8):
            for e in range(8):
                for d in range(8):
                    w74[par, par * 64 + m * 8 + e, d] = Wm[m, d * 8 + e]
    a["wm3r74"] = np.ascontiguousarray(w74)

    W3 = Wm.reshape(8, 8, 8)
    H1 = np.zeros((64, 8, 64), np.float32)
    H2 = np.zeros((64, 8, 64), np.float32)
    ab_a = np.arange(64) // 8
    ab_b = np.arange(64) % 8
    xy_x = np.arange(64) // 8
    xy_y = np.arange(64) % 8
    for e in range(8):
        H1[:, e, :] = W3[ab_a][:, xy_x, e]
        H2[:, e, :] = W3[ab_b][:, xy_y, e]
    a["h1"] = np.ascontiguousarray(H1.reshape(64, 512))
    a["h2"] = np.ascontiguousarray(H2.reshape(64, 512))

    a["temp"] = np.asarray(ins["L0_attn_temperature"], np.float32).reshape(1)
    return a



ENGS = ("sync", "vector", "scalar", "tensor", "gpsimd")


class Sem:
    def __init__(self, h, dma=False):
        self.h = h
        self.n = 0
        self.dma = dma

    def bump(self, kk):
        self.n += kk
        return self.n


class Buf:
    def __init__(self):
        self.w = None
        self.r = []

    def for_write(self):
        out = list(self.r)
        if self.w is not None:
            out.append(self.w)
        return out

    def wrote(self, tok):
        self.w = tok
        self.r = []

    def for_read(self):
        return [self.w] if self.w is not None else []

    def read(self, tok):
        self.r.append(tok)


NQSEM = {"di": 24, "dt": 24, "dts": 2, "dg": 18, "do": 8}


class K:
    def __init__(self, nc):
        self.nc = nc
        self.q = {e: [] for e in ENGS}
        self.sems = {}
        self.qsems = {}
        self.qcnt = {}
        for name in ("te", "ve", "se"):
            self.sems[name] = Sem(nc.alloc_semaphore(name=name))
        for name in ("di", "dt", "dts", "dg", "do"):
            self.qsems[name] = [Sem(nc.alloc_semaphore(name=f"{name}{i}"),
                                    dma=True) for i in range(NQSEM[name])]
            self.qcnt[name] = 0

    def qall(self, name):
        return [(s, s.n) for s in self.qsems[name] if s.n > 0]

    def emit(self, eng, fn, waits=(), inc=None):
        wd = {}
        for s, t in waits:
            if s.dma:
                t = s.n
            if s not in wd or t > wd[s]:
                wd[s] = t
        witems = tuple(wd.items())
        tok = None
        if inc is not None:
            tok = (inc[0], inc[0].bump(inc[1]))

        def g(e, fn=fn, witems=witems, inc=inc):
            for s, t in witems:
                e.wait_ge(s.h, t)
            ins = fn(e)
            if inc is not None and ins is not None:
                ins.then_inc(inc[0].h, inc[1])

        self.q[eng].append(g)
        return tok

    def _qsem(self, semn):
        s = self.qsems[semn][self.qcnt[semn] % NQSEM[semn]]
        self.qcnt[semn] += 1
        return s

    def dma(self, eng, semn, out_ap, in_ap, waits=()):
        return self.emit(eng, lambda e, o=out_ap, i=in_ap:
                         e.dma_start(out=o, in_=i),
                         waits=waits, inc=(self._qsem(semn), 16))

    def dma_t(self, eng, semn, out_ap, in_ap, waits=()):
        return self.emit(eng, lambda e, o=out_ap, i=in_ap:
                         e.dma_start(out=o, in_=i, transpose=True),
                         waits=waits, inc=(self._qsem(semn), 16))

    def build_block(self):
        nc = self.nc
        with nc.Block() as block:
            @block.sync
            def _(sync):
                for f in self.q["sync"]:
                    f(sync)

            @block.vector
            def _(vector):
                for f in self.q["vector"]:
                    f(vector)

            @block.scalar
            def _(scalar):
                for f in self.q["scalar"]:
                    f(scalar)

            @block.tensor
            def _(tensor):
                for f in self.q["tensor"]:
                    f(tensor)

            @block.gpsimd
            def _(gpsimd):
                for f in self.q["gpsimd"]:
                    f(gpsimd)



def build_nc(dbg=False):
    nc = bass.Bass("TRN2", target_bir_lowering=False)
    k = K(nc)
    te, ve, se = k.sems["te"], k.sems["ve"], k.sems["se"]

    dx = nc.dram_tensor("x", [NSEQ, T, D], F32, kind="ExternalInput")
    dy = nc.dram_tensor("y", [NSEQ, T, D], F32, kind="ExternalOutput")
    d_c2 = nc.dram_tensor("c2", [128, T], BF16, kind="ExternalInput")
    d_s2 = nc.dram_tensor("s2", [128, T], BF16, kind="ExternalInput")
    d_p2 = nc.dram_tensor("p2", [128, 128], BF16, kind="ExternalInput")
    d_i128 = nc.dram_tensor("i128", [128, 128], BF16, kind="ExternalInput")
    d_tri = nc.dram_tensor("tri", [128, 128], BF16, kind="ExternalInput")
    d_q2cl = nc.dram_tensor("q2cl", [128, 160], BF16, kind="ExternalInput")
    d_repc = nc.dram_tensor("repc", [16, 1024], BF16, kind="ExternalInput")
    d_onesbf = nc.dram_tensor("onesbf", [1, 128], BF16, kind="ExternalInput")
    d_onesf32 = nc.dram_tensor("onesf32", [1, 128], F32, kind="ExternalInput")
    d_wq = nc.dram_tensor("wq_sl", [8, 128, 8, 128], F32, kind="ExternalInput")
    d_wk = nc.dram_tensor("wk_sl", [8, 128, 8, 128], F32, kind="ExternalInput")
    d_wv = nc.dram_tensor("wv_r", [128, 8, 1024], F32, kind="ExternalInput")
    d_wo = nc.dram_tensor("wo_r", [128, 8, 1024], F32, kind="ExternalInput")
    d_wf = nc.dram_tensor("wf_r", [128, 8, 1024], F32, kind="ExternalInput")
    d_vr = nc.dram_tensor("vr_r", [128, 8, 32], F32, kind="ExternalInput")
    d_ct = nc.dram_tensor("ct_r", [32, 1024], F32, kind="ExternalInput")
    d_b = nc.dram_tensor("b_r", [1, D], F32, kind="ExternalInput")
    d_wqml = nc.dram_tensor("wqm_l", [128, 8, 128], F32, kind="ExternalInput")
    d_wkml = nc.dram_tensor("wkm_l", [128, 8, 128], F32, kind="ExternalInput")
    d_wul = nc.dram_tensor("wu_l", [128, 8, 128], F32, kind="ExternalInput")
    d_w74 = nc.dram_tensor("wm3r74", [2, 128, 10], F32, kind="ExternalInput")
    d_h1 = nc.dram_tensor("h1", [64, 512], F32, kind="ExternalInput")
    d_h2 = nc.dram_tensor("h2", [64, 512], F32, kind="ExternalInput")
    d_temp = nc.dram_tensor("temp", [1], F32, kind="ExternalInput")
    d_x1 = nc.dram_tensor("x1scr", [NSEQ, T, D], F32)
    d_rs = nc.dram_tensor("rsscr", [16, 512], BF16)
    d_xnt = nc.dram_tensor("xntscr", [NG, 4, 128, 1024], BF16)

    dbg_t = {}
    if dbg:
        dbg_t["rsA"] = nc.dram_tensor("dbg_rsA", [NSEQ, NG, 128, 2048], BF16,
                                      kind="ExternalOutput")
        dbg_t["rsT"] = nc.dram_tensor("dbg_rsT", [NSEQ, NG, 16, 512], BF16,
                                      kind="ExternalOutput")
        dbg_t["auU"] = nc.dram_tensor("dbg_auU", [NSEQ, NG, 128, 4096], BF16,
                                      kind="ExternalOutput")
        dbg_t["auN"] = nc.dram_tensor("dbg_auN", [NSEQ, NG, 128, 4096], BF16,
                                      kind="ExternalOutput")
        dbg_t["qf"] = nc.dram_tensor("dbg_qf", [NSEQ, NG, H, 74, 512], BF16,
                                     kind="ExternalOutput")

    sb = nc.alloc_sbuf_tensor
    wvof_sb = sb("wvof_sb", [128, 8 * 1024], BF16)
    wqml_sb = sb("wqml_sb", [128, 8 * 128], BF16)
    wkml_sb = sb("wkml_sb", [128, 8 * 128], BF16)
    wul_sb = sb("wul_sb", [128, 8 * 128], BF16)
    vr_sb = sb("vr_sb", [128, 8 * 32], BF16)
    ct_sb = sb("ct_sb", [32, 1024], BF16)
    b_sb = sb("b_sb", [1, D], BF16)
    c2_sb = sb("c2_sb", [128, 512], BF16)
    s2_sb = sb("s2_sb", [128, 512], BF16)
    p2_sb = sb("p2_sb", [128, 128], BF16)
    i128_sb = sb("i128_sb", [128, 128], BF16)
    tri_sb = sb("tri_sb", [128, 128], BF16)
    q2cl_sb = sb("q2cl_sb", [128, 160], BF16)
    repc_sb = sb("repc_sb", [16, 1024], BF16)
    onesb_sb = sb("onesb_sb", [1, 128], BF16)
    onesf_sb = sb("onesf_sb", [1, 128], F32)
    w74_sb = sb("w74_sb", [128, 20], BF16)
    mq_sb = sb("mq_sb", [128, 128], BF16)
    negc_sb = sb("negc_sb", [128, 1], F32)
    epsc_sb = sb("epsc_sb", [128, 1], F32)
    bmat_sb = sb("bmat_sb", [64, 64], F32)
    bscr_sb = sb("bscr_sb", [64, 64], F32)
    w74f_sb = sb("w74f_sb", [128, 20], F32)
    tmp1_sb = sb("tmp1_sb", [1, 1], F32)
    wqs_sb = sb("wqs_sb", [128, 2 * 1024], BF16)
    wks_sb = sb("wks_sb", [128, 2 * 1024], BF16)
    kxt_sb = sb("kxt_sb", [128, H * T], BF16, is_transpose_dest=True)
    vtm_sb = sb("vtm_sb", [128, NT * 1040], BF16)
    qmT_sb = sb("qmT_sb", [128, T], BF16)
    kmT_sb = sb("kmT_sb", [128, T], BF16)
    xq_sb = sb("xq_sb", [128, 8 * 512], BF16, is_transpose_dest=True)
    xnT_sb = xq_sb
    qq1T_sb = xq_sb
    quT_sb = sb("quT_sb", [128, 8 * 512], BF16, is_transpose_dest=True)
    qaT_sb = sb("qaT_sb", [128, 2 * 512], BF16, is_transpose_dest=True)
    qf_sb = sb("qf_sb", [74, 2 * 512], BF16)
    e_sb = sb("e_sb", [128, 4 * 512], BF16)
    scrA_sb = sb("scrA_sb", [128, 8 * 512], BF16)
    au_sb = scrA_sb
    kxtm_sb_ap = scrA_sb[:, 0:2048]
    qq1tm_ap = scrA_sb[:, 2048:3072]
    qutm_ap = scrA_sb[:, 3072:4096]
    qaux_ap = scrA_sb[:, 0:1024]
    uT_sb = sb("uT_sb", [128, 512], BF16)
    spr2_ap = uT_sb[:]
    qmtm_sb = sb("qmtm_sb", [128, 512], BF16, is_transpose_dest=True)
    utm_sb = sb("utm_sb", [128, 512], BF16, is_transpose_dest=True)
    kmtm_sb = sb("kmtm_sb", [128, 128], BF16, is_transpose_dest=True)
    sq_sb = sb("sq_sb", [128, 512], BF16)
    rep_sb = sb("rep_sb", [128, 1024], BF16)
    red1_sb = sb("red1_sb", [128, 64], F32)
    red2_sb = sb("red2_sb", [128, 64], F32)
    x1t_sb = sb("x1t_sb", [128, D], F32)
    xt_sb = sb("xt_sb", [128, D], F32)
    xorig_sb = xt_sb
    h1_sb = x1t_sb[0:64, 0:512]
    h2_sb = xt_sb[0:64, 0:512]
    xn_sb = sb("xn_sb", [128, D], BF16)
    yt_sb = sb("yt_sb", [128, 512], F32)
    st_sum = sb("st_sum", [128, 8], F32)
    st_sq = sb("st_sq", [128, 8], F32)
    st_mean = sb("st_mean", [128, 2], F32)
    st_var = sb("st_var", [128, 2], F32)
    st_var2 = sb("st_var2", [128, 2], F32)
    st_inv = sb("st_inv", [128, 2], F32)
    st_nb = sb("st_nb", [128, 2], F32)
    qT_sb = sb("qT_sb", [128, 512], BF16)
    kT_sb = sb("kT_sb", [128, 512], BF16)
    qc_sb = sb("qc_sb", [128, 512], BF16)
    qs_sb = sb("qs_sb", [128, 512], BF16)
    kc_sb = sb("kc_sb", [128, 512], BF16)
    ks_sb = sb("ks_sb", [128, 512], BF16)
    rsA_sb = sb("rsA_sb", [128, 2048], BF16)
    rsT_sb = sb("rsT_sb", [16, 512], BF16)
    rcpb_sb = rep_sb[0:16, 0:512]
    z1T_sb = rep_sb[0:32, 512:1024]

    ps = [nc.alloc_psum_tensor(f"ps{i}", [128, 512], F32) for i in range(8)]
    psb = [Buf() for _ in range(8)]
    bufs = {}

    def B_(name):
        if name not in bufs:
            bufs[name] = Buf()
        return bufs[name]

    def t_load(eng, semn, out_ap, in_ap, wbufs=(), rbufs=(), extra=()):
        waits = list(extra)
        for n in wbufs:
            waits += B_(n).for_write()
        for n in rbufs:
            waits += B_(n).for_read()
        tok = k.dma(eng, semn, out_ap, in_ap, waits=waits)
        for n in wbufs:
            B_(n).wrote(tok)
        for n in rbufs:
            B_(n).read(tok)
        return tok

    def t_tr(eng, out_ap, in_ap, wbufs=(), rbufs=(), extra=()):
        waits = list(extra)
        for n in wbufs:
            waits += B_(n).for_write()
        for n in rbufs:
            waits += B_(n).for_read()
        tok = k.dma_t("scalar", "dt", out_ap, in_ap, waits=waits)
        for n in wbufs:
            B_(n).wrote(tok)
        for n in rbufs:
            B_(n).read(tok)
        return tok

    def t_mm(bank, out_ap, lhsT, rhs, start, stop, rbufs=(), extra=(), skip=False):
        waits = list(extra)
        if start:
            waits += psb[bank].for_write()
        for n in rbufs:
            waits += B_(n).for_read()
        tok = k.emit(
            "tensor",
            lambda e, o=out_ap, l=lhsT, r=rhs, st=start, sp=stop, sk=skip:
                e.matmul(o, l, r, start=st, stop=sp, skip_group_check=sk),
            waits=waits, inc=(te, 1))
        if start:
            psb[bank].wrote(tok)
        else:
            psb[bank].w = tok
        for n in rbufs:
            B_(n).read(tok)
        return tok

    def t_ve(fn, wr=(), rd=(), banks_rd=(), extra=(), eng="vector"):
        sem = ve if eng == "vector" else se
        waits = list(extra)
        for n in rd:
            waits += B_(n).for_read()
        for n in wr:
            waits += B_(n).for_write()
        for b in banks_rd:
            if psb[b].w is not None:
                waits.append(psb[b].w)
        tok = k.emit(eng, fn, waits=waits, inc=(sem, 1))
        for n in rd:
            B_(n).read(tok)
        for n in wr:
            B_(n).wrote(tok)
        for b in banks_rd:
            psb[b].read(tok)
        return tok

    def outer_split(fast_src_ap, fast_name, slow_src_ap, slow_name, out3,
                    out_name, rd_extra=()):
        f3 = fast_src_ap.rearrange("p (h b) -> p h b", b=8)
        s3 = slow_src_ap.rearrange("p (h a) -> p h a", a=8)
        for aa in range(8):
            t_ve(lambda e, a_=aa: e.tensor_mul(
                out3[:, :, a_ * 8:(a_ + 1) * 8], f3,
                s3[:, :, a_:a_ + 1].broadcast_to([128, 16, 8])),
                wr=[out_name], rd=[fast_name, slow_name] + list(rd_extra))

    def ka():
        return

    def dbg_dump(name, src_ap, rd=(), banks_rd=()):
        if not dbg or name not in dbg_t:
            return
        scr = B_("dbgscr")
        shp = dbg_t[name].shape
        tmp = sb("dbgs_" + name, shp, F32)
        t_ve(lambda e, o=tmp, s=src_ap: e.tensor_copy(o[:], s),
             wr=["dbgs" + name], rd=list(rd), banks_rd=list(banks_rd))
        t_load("gpsimd", "do", dbg_t[name][:], tmp[:], rbufs=["dbgs" + name])

    def phase_consts():
        g = "gpsimd"
        t_load(g, "dg", p2_sb[:], d_p2[:], ["p2"])
        t_load(g, "dg", i128_sb[:], d_i128[:], ["i128"])
        t_load(g, "dg", tri_sb[:], d_tri[:], ["tri"])
        t_load(g, "dg", q2cl_sb[:], d_q2cl[:], ["q2cl"])
        t_load(g, "dg", repc_sb[:], d_repc[:], ["repc"])
        t_load(g, "dg", onesb_sb[:], d_onesbf[:], ["onesb"])
        t_load(g, "dg", onesf_sb[:], d_onesf32[:], ["onesf"])
        t_load(g, "dg", wqml_sb[:].rearrange("p (c j) -> p c j", j=128), d_wqml[:], ["wqml"])
        t_load(g, "dg", wkml_sb[:].rearrange("p (c j) -> p c j", j=128), d_wkml[:], ["wkml"])
        t_load(g, "dg", wul_sb[:].rearrange("p (c j) -> p c j", j=128), d_wul[:], ["wul"])
        t_load(g, "dg", vr_sb[:].rearrange("p (c j) -> p c j", j=32), d_vr[:], ["vr"])
        t_load(g, "dg", ct_sb[:], d_ct[:], ["ct"])
        t_load(g, "dg", b_sb[:], d_b[:], ["b"])
        t_load(g, "dg", h1_sb, d_h1[:], ["x1t"])
        t_load(g, "dg", h2_sb, d_h2[:], ["xt"])
        t_load(g, "dg", w74f_sb[:, 0:10], d_w74[0], ["w74f"])
        t_load(g, "dg", w74f_sb[:, 10:20], d_w74[1], ["w74f"])
        t_load(g, "dg", tmp1_sb[:], d_temp[None, :], ["tmpv"])

        t_ve(lambda e: e.memset(epsc_sb[:], LNEPS), wr=["epsc"])
        t_ve(lambda e: e.memset(bmat_sb[:], 0.0), wr=["bmat"])
        for e_i in range(8):
            t_ve(lambda e, i=e_i: e.tensor_mul(
                bscr_sb[:], h1_sb[:, i * 64:(i + 1) * 64],
                h2_sb[:, i * 64:(i + 1) * 64]),
                wr=["bscr"], rd=["x1t", "xt"])
            t_ve(lambda e: e.tensor_add(bmat_sb[:], bmat_sb[:], bscr_sb[:]),
                 wr=["bmat"], rd=["bscr", "bmat"])
        t_ve(lambda e: e.memset(mq_sb[:], 0.0), wr=["mq"])
        t_ve(lambda e: e.tensor_copy(mq_sb[0:64, 0:64], bmat_sb[:]),
             wr=["mq"], rd=["bmat"])
        t_ve(lambda e: e.tensor_copy(mq_sb[64:128, 64:128], bmat_sb[:]),
             wr=["mq"], rd=["bmat"])
        t_ve(lambda e: e.tensor_scalar_mul(w74_sb[:], w74f_sb[:], -2.0),
             wr=["w74"], rd=["w74f"])
        t_mm(7, ps[7][0:128, 0:1], onesf_sb[:], tmp1_sb[:], True, True,
             rbufs=["onesf", "tmpv"])
        t_ve(lambda e: e.tensor_scalar_max(negc_sb[:], ps[7][0:128, 0:1], 0.1),
             wr=["negc"], banks_rd=[7])
        t_ve(lambda e: e.reciprocal(negc_sb[:], negc_sb[:]),
             wr=["negc"], rd=["negc"])
        t_ve(lambda e: e.tensor_scalar_mul(negc_sb[:], negc_sb[:], -1.0),
             wr=["negc"], rd=["negc"])

    def ln_steps(src_name, src_ap, out_ap, out_name, par):
        sm = "stm%d" % par
        si = "sti%d" % par
        sn = "stn%d" % par
        ss = "sts%d" % par
        sqn = "stq%d" % par
        dummy = out_ap
        dn = out_name

        def s1():
            t_ve(lambda e: e.activation(
                dummy, src_ap, AF.Square, accum_out=st_sq[:, par:par + 1]),
                eng="scalar", wr=[dn, sqn], rd=[src_name])

        def s2():
            t_ve(lambda e: e.reduce_sum(st_sum[:, par:par + 1], src_ap, axis=AX),
                 wr=[ss], rd=[src_name])

        def s3():
            t_ve(lambda e: e.tensor_scalar_mul(
                st_mean[:, par:par + 1], st_sum[:, par:par + 1], 1.0 / D),
                wr=[sm], rd=[ss])

        def s4():
            t_ve(lambda e: e.tensor_mul(
                st_var[:, par:par + 1], st_mean[:, par:par + 1],
                st_mean[:, par:par + 1]), wr=["stv%d" % par], rd=[sm])

        def s5():
            t_ve(lambda e: e.scalar_tensor_tensor(
                st_var2[:, par:par + 1], st_sq[:, par:par + 1], 1.0 / D,
                st_var[:, par:par + 1], OP.mult, OP.subtract),
                wr=["st2%d" % par], rd=[sqn, "stv%d" % par])

        def s6():
            t_ve(lambda e: e.activation(
                st_inv[:, par:par + 1], st_var2[:, par:par + 1], AF.Ln,
                bias=epsc_sb[:]), eng="scalar", wr=[si],
                rd=["st2%d" % par, "epsc"])

        def s7():
            t_ve(lambda e: e.activation(
                st_inv[:, par:par + 1], st_inv[:, par:par + 1], AF.Exp,
                scale=-0.5), eng="scalar", wr=[si], rd=[si])

        def s8():
            t_ve(lambda e: e.scalar_tensor_tensor(
                st_nb[:, par:par + 1], st_mean[:, par:par + 1], -1.0,
                st_inv[:, par:par + 1], OP.mult, OP.mult),
                wr=[sn], rd=[sm, si])

        def s9():
            t_ve(lambda e: e.activation(
                out_ap, src_ap, AF.Identity, bias=st_nb[:, par:par + 1],
                scale=st_inv[:, par:par + 1]),
                eng="scalar", wr=[out_name], rd=[src_name, sn, si])

        return [s1, s2, s3, s4, s5, s6, s7, s8, s9]

    def ln_tile(src_name, src_ap, out_ap, out_name, par=0):
        for s in ln_steps(src_name, src_ap, out_ap, out_name, par):
            s()

    def phase_p2_tile(tile):
        sl = slice(tile * 128, (tile + 1) * 128)
        t_tr("scalar", kmtm_sb[:], kmT_sb[:, sl], wbufs=["kmtm"], rbufs=["kmT"])
        kx4 = kxtm_sb_ap.rearrange("p (h c) -> p h c", c=128)
        km3 = kmtm_sb[:].rearrange("p (h i) -> p h i", i=8)
        for aa in range(8):
            t_ve(lambda e, a_=aa: e.tensor_mul(
                kx4[:, :, a_ * 8:(a_ + 1) * 8],
                km3, km3[:, :, a_:a_ + 1].broadcast_to([128, 16, 8])),
                wr=["sA0", "sA1"], rd=["kmtm"])
        t_ve(lambda e: e.tensor_copy(kx4[:, :, 64:72], km3),
             wr=["sA0", "sA1"], rd=["kmtm"])
        t_ve(lambda e: e.memset(kx4[:, :, 72:73], 1.0), wr=["sA0", "sA1"])
        t_ve(lambda e: e.tensor_mul(sq_sb[:, 0:128], kmtm_sb[:], kmtm_sb[:]),
             wr=["sq"], rd=["kmtm"])
        t_ve(lambda e: e.reduce_sum(
            red1_sb[:, 0:16],
            sq_sb[:, 0:128].rearrange("p (h i) -> p h i", i=8), axis=AX),
            wr=["red1"], rd=["sq"])
        t_ve(lambda e: e.tensor_scalar(
            kx4[:, :, 73:74].rearrange("p h c -> p (h c)"),
            red1_sb[:, 0:16], EPS, None, OP.mult),
            wr=["sA0", "sA1"], rd=["red1"])
        src = kxtm_sb_ap.rearrange("t (h p) -> t h p", p=128)
        dst = kxt_sb[:].rearrange("p (h t) -> p h t", t=T)[
            :, :, tile * 128:(tile + 1) * 128]
        t_tr("sync", dst, src, wbufs=["kxt%d" % tile], rbufs=["sA0", "sA1"])
        ka()

    def rope_och(och, xnb, xnn):
        par = och % 2
        for c in range(8):
            t_mm(0, ps[0][0:128, :],
                 wqs_sb[:, par * 1024 + c * 128:par * 1024 + (c + 1) * 128],
                 xnb[:, c * 512:(c + 1) * 512],
                 start=(c == 0), stop=(c == 7), rbufs=["wqs%d" % par, xnn])
        t_ve(lambda e: e.tensor_copy(qT_sb[:], ps[0][0:128, :]),
             wr=["qTb"], banks_rd=[0])
        for c in range(8):
            t_mm(1, ps[1][0:128, :],
                 wks_sb[:, par * 1024 + c * 128:par * 1024 + (c + 1) * 128],
                 xnb[:, c * 512:(c + 1) * 512],
                 start=(c == 0), stop=(c == 7), rbufs=["wks%d" % par, xnn])
        t_ve(lambda e: e.tensor_copy(kT_sb[:], ps[1][0:128, :]),
             wr=["kTb"], banks_rd=[1])
        t_mm(2, ps[2][0:128, :], p2_sb[:], qT_sb[:],
             start=True, stop=True, rbufs=["p2", "qTb"])
        t_mm(5, ps[5][0:128, :], p2_sb[:], kT_sb[:],
             start=True, stop=True, rbufs=["p2", "kTb"])
        t_ve(lambda e: e.tensor_mul(qc_sb[:], qT_sb[:], c2_sb[:]),
             wr=["qcb"], rd=["qTb", "c2"])
        t_ve(lambda e: e.tensor_mul(qs_sb[:], ps[2][0:128, :], s2_sb[:]),
             wr=["qsb"], rd=["s2"], banks_rd=[2])
        t_ve(lambda e: e.tensor_mul(kc_sb[:], kT_sb[:], c2_sb[:]),
             wr=["kcb"], rd=["kTb", "c2"])
        t_ve(lambda e: e.tensor_mul(ks_sb[:], ps[5][0:128, :], s2_sb[:]),
             wr=["ksb"], rd=["s2"], banks_rd=[5])
        t_mm(3, ps[3][0:128, :], wqml_sb[:, och * 128:(och + 1) * 128], qc_sb[:],
             start=(och == 0), stop=False, rbufs=["wqml", "qcb"], skip=True)
        t_mm(3, ps[3][0:128, :], wqml_sb[:, och * 128:(och + 1) * 128], qs_sb[:],
             start=False, stop=(och == 7), rbufs=["wqml", "qsb"], skip=True)
        t_mm(4, ps[4][0:128, :], wkml_sb[:, och * 128:(och + 1) * 128], kc_sb[:],
             start=(och == 0), stop=False, rbufs=["wkml", "kcb"], skip=True)
        t_mm(4, ps[4][0:128, :], wkml_sb[:, och * 128:(och + 1) * 128], ks_sb[:],
             start=False, stop=(och == 7), rbufs=["wkml", "ksb"], skip=True)

    def ln_group_chunks(b, g):
        xnb = xnT_sb if g % 2 == 0 else quT_sb
        xnn = "xq" if g % 2 == 0 else "quT"
        chunks = []
        for half in range(2):
            plan = []
            for par in range(2):
                tl = half * 2 + par
                tile = g * 4 + tl
                rows = slice(tile * 128, (tile + 1) * 128)
                xbuf = xt_sb if par == 0 else x1t_sb
                xbn = "xt" if par == 0 else "x1t"
                onb = xn_sb[:] if par == 0 else rsA_sb[:, 0:1024]
                onn = "xn" if par == 0 else "rsAa"
                plan.append((rows, xbuf, xbn, onb, onn, par, tl))

            def c_load(plan=plan):
                for rows, xbuf, xbn, _, _, _, _ in plan:
                    t_load("sync", "di", xbuf[:], dx[b, rows, :], [xbn])
            chunks.append(c_load)
            stepss = []

            def c_mksteps(plan=plan, stepss=stepss):
                for _, xbuf, xbn, onb, onn, par, _ in plan:
                    stepss.append(ln_steps(xbn, xbuf[:], onb, onn, par))
            chunks.append(c_mksteps)
            for i in range(9):
                def c_step(i=i, stepss=stepss):
                    for st_ in stepss:
                        st_[i]()
                chunks.append(c_step)

            def c_tr(plan=plan, xnb=xnb, xnn=xnn):
                for _, _, _, onb, onn, _, tl in plan:
                    t_tr("scalar",
                         xnb[:].rearrange("p (c t) -> p c t", t=512)[
                             :, :, tl * 128:(tl + 1) * 128],
                         onb.rearrange("t (c p) -> t c p", p=128),
                         wbufs=[xnn], rbufs=[onn])
            chunks.append(c_tr)
        return chunks

    def phase_p1(b, m0_och=None, m0_tail=None):
        t_load("gpsimd", "dg", wvof_sb[:].rearrange("p (c o) -> p c o", o=1024),
               d_wv[:], ["wvof"])
        t_load("gpsimd", "dg", c2_sb[:], d_c2[:, 0:512], ["c2"])
        t_load("gpsimd", "dg", s2_sb[:], d_s2[:, 0:512], ["s2"])
        for c in ln_group_chunks(b, 0):
            c()
        for g in range(NG):
            xnb = xnT_sb if g % 2 == 0 else quT_sb
            xnn = "xq" if g % 2 == 0 else "quT"
            nxt = ln_group_chunks(b, g + 1) if g + 1 < NG else []
            ni = 0
            for och in range(8):
                par = och % 2
                t_load("gpsimd", "dg",
                       wqs_sb[:, par * 1024:(par + 1) * 1024].rearrange(
                           "p (c j) -> p c j", j=128),
                       d_wq[och], ["wqs%d" % par])
                t_load("gpsimd", "dg",
                       wks_sb[:, par * 1024:(par + 1) * 1024].rearrange(
                           "p (c j) -> p c j", j=128),
                       d_wk[och], ["wks%d" % par])
                rope_och(och, xnb, xnn)
                take = 4 if och < 6 else len(nxt) - ni
                for _ in range(max(0, take)):
                    if ni < len(nxt):
                        nxt[ni]()
                        ni += 1
                if g == NG - 1 and m0_och is not None:
                    m0_och(2)
            while ni < len(nxt):
                nxt[ni]()
                ni += 1
            if g == NG - 1 and m0_och is not None:
                m0_och(-1)
            T_sl = slice(g * 512, (g + 1) * 512)
            t_ve(lambda e, s=T_sl: e.activation(
                qmT_sb[:, s], ps[3][0:128, :], AF.Tanh, scale=0.5),
                eng="scalar", wr=["qmT"], banks_rd=[3])
            t_ve(lambda e, s=T_sl: e.tensor_scalar(
                qmT_sb[:, s], qmT_sb[:, s], 0.5, 0.5, OP.mult, OP.add),
                wr=["qmT"], rd=["qmT"])
            t_ve(lambda e, s=T_sl: e.activation(
                kmT_sb[:, s], ps[4][0:128, :], AF.Tanh, scale=0.5),
                eng="scalar", wr=["kmT"], banks_rd=[4])
            t_ve(lambda e, s=T_sl: e.tensor_scalar(
                kmT_sb[:, s], kmT_sb[:, s], 0.5, 0.5, OP.mult, OP.add),
                wr=["kmT"], rd=["kmT"])
            if g + 1 < NG:
                T_n = slice((g + 1) * 512, (g + 2) * 512)
                t_load("gpsimd", "dg", c2_sb[:], d_c2[:, T_n], ["c2"])
                t_load("gpsimd", "dg", s2_sb[:], d_s2[:, T_n], ["s2"])
            for tl in range(4):
                tile = g * 4 + tl
                for hf in range(2):
                    bank = 5 + hf
                    for c in range(8):
                        t_mm(bank, ps[bank][0:128, :],
                             xnb[:, c * 512 + tl * 128: c * 512 + (tl + 1) * 128],
                             wvof_sb[:, c * 1024 + hf * 512: c * 1024 + (hf + 1) * 512],
                             start=(c == 0), stop=(c == 7), rbufs=[xnn, "wvof"])
                    dstv = vtm_sb[:].rearrange(
                        "p (tt h j) -> p tt h j", tt=NT, h=H)[
                        :, tile, hf * 8:(hf + 1) * 8, 0:64]
                    t_ve(lambda e, d=dstv, bk=bank: e.tensor_copy(
                        d, ps[bk][0:128, :].rearrange("p (h j) -> p h j", j=64)),
                        wr=["vtm"], banks_rd=[bank])
                dstone = vtm_sb[:].rearrange(
                    "p (tt h j) -> p tt h j", tt=NT, h=H)[:, tile, :, 64:65]
                t_ve(lambda e, d=dstone: e.memset(d, 1.0), wr=["vtm"])
                if g == NG - 1 and tl == 3 and m0_tail is not None:
                    m0_tail()
                phase_p2_tile(tile)

    def metric_chunks(b, g, pre=False):
        T_sl = slice(g * 512, (g + 1) * 512)
        qm_ap, qm_n = qmtm_sb[:], "qmtm"
        ut_ap, ut_n = utm_sb[:], "utm"
        uT_ap, uT_n = uT_sb[:], "uT"
        if pre:
            homes = [(scrA_sb[:, 2048:3072], "sA2"),
                     (scrA_sb[:, 3072:4096], "sA3")]
        else:
            homes = [(rsA_sb[:, 0:1024], "rsAa"),
                     (rsA_sb[:, 1024:2048], "rsAb")]
        qq_homes = homes
        qu_homes = homes
        qa_ap, qa_n = homes[0]
        ubank = 2
        early = []
        late = []

        def c_qmtr():
            t_tr("scalar", qm_ap.rearrange("p (tt t) -> p tt t", t=128),
                 qmT_sb[:, T_sl].rearrange("p (tt t) -> p tt t", t=128),
                 wbufs=[qm_n], rbufs=["qmT"])
        early.append(c_qmtr)
        for tl in range(4):
            def c_qqo(tl=tl):
                qqr, qqn = qq_homes[tl % 2]
                outer_split(qm_ap[:, tl * 128:(tl + 1) * 128], qm_n,
                            qm_ap[:, tl * 128:(tl + 1) * 128], qm_n,
                            qqr.rearrange("p (h c) -> p h c", c=64), qqn)
            late.append(c_qqo)

            def c_qqt(tl=tl):
                qqr, qqn = qq_homes[tl % 2]
                t_tr("scalar",
                     qq1T_sb[:].rearrange("p (c t) -> p c t", t=512)[
                         :, :, tl * 128:(tl + 1) * 128],
                     qqr.rearrange("t (c p) -> t c p", p=128),
                     wbufs=["xq"], rbufs=[qqn])
            late.append(c_qqt)

        def c_umm():
            for c in range(8):
                t_mm(ubank, ps[ubank][0:128, :],
                     wul_sb[:, c * 128:(c + 1) * 128],
                     qq1T_sb[:, c * 512:(c + 1) * 512],
                     start=(c == 0), stop=(c == 7), rbufs=["wul", "xq"])
            t_ve(lambda e: e.tensor_copy(uT_ap, ps[ubank][0:128, :]),
                 wr=[uT_n], banks_rd=[ubank])
        late.append(c_umm)

        def c_utr():
            t_tr("scalar", ut_ap.rearrange("p (tt t) -> p tt t", t=128),
                 uT_ap.rearrange("p (tt t) -> p tt t", t=128),
                 wbufs=[ut_n], rbufs=[uT_n])
        late.append(c_utr)

        def c_sq():
            t_ve(lambda e: e.tensor_mul(sq_sb[:], ut_ap, ut_ap),
                 wr=["sq"], rd=[ut_n])
            t_ve(lambda e: e.reduce_sum(
                red1_sb[:], sq_sb[:].rearrange("p (th i) -> p th i", i=8),
                axis=AX), wr=["red1"], rd=["sq"])
            t_ve(lambda e: e.tensor_mul(sq_sb[:], qm_ap, qm_ap),
                 wr=["sq"], rd=[qm_n])
            t_ve(lambda e: e.reduce_sum(
                red2_sb[:], sq_sb[:].rearrange("p (th i) -> p th i", i=8),
                axis=AX), wr=["red2"], rd=["sq"])
        late.append(c_sq)

        def c_qaux():
            qa5 = qa_ap.rearrange(
                "p (c tt l i) -> p c tt l i", c=2, tt=4, i=16)
            for c2 in range(2):
                t_ve(lambda e, c_=c2: e.scalar_tensor_tensor(
                    qa5[:, c_, :, :, 8:9].squeeze(3),
                    red2_sb[:].rearrange("p (tt c l) -> p tt c l", tt=4, c=2)[
                        :, :, c_, :],
                    EPS,
                    red1_sb[:].rearrange("p (tt c l) -> p tt c l", tt=4, c=2)[
                        :, :, c_, :],
                    OP.mult, OP.add),
                    wr=[qa_n], rd=["red1", "red2"])
                for tl in range(4):
                    t_ve(lambda e, c_=c2, t_=tl: e.tensor_copy(
                        qa5[:, c_, t_, :, 0:8],
                        qm_ap.rearrange(
                            "p (tt c l i) -> p tt c l i", tt=4, c=2, i=8)[
                            :, t_, c_, :, :]),
                        wr=[qa_n], rd=[qm_n])
                t_ve(lambda e, c_=c2: e.memset(
                    qa5[:, c_, :, :, 9:10].squeeze(3), 1.0), wr=[qa_n])

        for tl in range(4):
            def c_quo(tl=tl):
                qur, qun = qu_homes[tl % 2]
                outer_split(ut_ap[:, tl * 128:(tl + 1) * 128], ut_n,
                            qm_ap[:, tl * 128:(tl + 1) * 128], qm_n,
                            qur.rearrange("p (h c) -> p h c", c=64), qun)
            late.append(c_quo)
            def c_qut(tl=tl):
                qur, qun = qu_homes[tl % 2]
                t_tr("scalar",
                     quT_sb[:].rearrange("p (c t) -> p c t", t=512)[
                         :, :, tl * 128:(tl + 1) * 128],
                     qur.rearrange("t (c p) -> t c p", p=128),
                     wbufs=["quT"], rbufs=[qun])
            late.append(c_qut)
        late.append(c_qaux)

        def c_qat():
            for c2 in range(2):
                t_tr("scalar",
                     qaT_sb[:, c2 * 512:(c2 + 1) * 512].rearrange(
                         "p (tt q) -> p tt q", q=128),
                     qa_ap[:, c2 * 512:(c2 + 1) * 512].rearrange(
                         "t (tt p) -> t tt p", p=128),
                     wbufs=["qaT"], rbufs=[qa_n])
        late.append(c_qat)
        return early, late

    def phase_p4(b, g, nxt_early, mfeed):
        nch = 4 * (g + 1)

        def qf_build(h):
            ch = h // 2
            par = h % 2
            qfs = h % 2
            qf_ap = qf_sb[:, qfs * 512:(qfs + 1) * 512]
            qfn = "qf%d" % qfs
            t_mm(2, ps[2][0:64, :], mq_sb[:, par * 64:(par + 1) * 64],
                 qq1T_sb[:, ch * 512:(ch + 1) * 512],
                 start=True, stop=True, rbufs=["mq", "xq"], skip=True)
            t_mm(2, ps[2][64:74, :], w74_sb[:, par * 10:(par + 1) * 10],
                 quT_sb[:, ch * 512:(ch + 1) * 512],
                 start=True, stop=False, rbufs=["w74", "quT"], skip=True)
            t_mm(2, ps[2][64:74, :], q2cl_sb[:, h * 10:(h + 1) * 10],
                 qaT_sb[:, (h // 8) * 512:(h // 8 + 1) * 512],
                 start=False, stop=True, rbufs=["q2cl", "qaT"], skip=True)
            t_ve(lambda e, o=qf_ap: e.tensor_copy(o[0:74, :], ps[2][0:74, :]),
                 wr=[qfn], banks_rd=[2])

        ZB = (0, 1, 5)
        ei = 0
        qf_build(0)
        for h in range(H):
            qfn = "qf%d" % (h % 2)
            qf_ap = qf_sb[:, (h % 2) * 512:(h % 2 + 1) * 512]
            av_bank = 6 + (h % 2)

            def z_tile(c):
                zb = ZB[c % 3]
                diag = (c >= 4 * g)
                t_mm(zb, ps[zb][0:128, :],
                     kxt_sb[0:74, h * T + c * 128: h * T + (c + 1) * 128],
                     qf_ap[0:74, :],
                     start=True, stop=(not diag),
                     rbufs=["kxt%d" % c, qfn], skip=True)
                if diag:
                    lo = c * 128 - g * 512
                    t_mm(zb, ps[zb][0:128, lo:lo + 128], i128_sb[:], tri_sb[:],
                         start=False, stop=True, rbufs=["i128", "tri"], skip=True)
                es = c % 4
                e_ap = e_sb[:, es * 512:(es + 1) * 512]
                en = "e%d" % es
                lo = max(0, c * 128 - g * 512)
                if lo > 0:
                    t_ve(lambda e, a=e_ap, l=lo: e.memset(a[:, 0:l], 0.0),
                         wr=[en])
                t_ve(lambda e, a=e_ap, l=lo, bk=zb: e.activation(
                    a[:, l:512], ps[bk][0:128, l:512], AF.Exp, scale=negc_sb[:]),
                    eng="scalar", wr=[en], rd=["negc"], banks_rd=[zb])
                t_ve(lambda e, a=e_ap, l=lo: e.tensor_scalar_min(
                    a[:, l:512], a[:, l:512], 1.0), wr=[en], rd=[en])

            def av_tile(c):
                en = "e%d" % (c % 4)
                e_ap = e_sb[:, (c % 4) * 512:(c % 4 + 1) * 512]
                vsl = vtm_sb[:].rearrange(
                    "p (tt hh j) -> p tt hh j", tt=NT, hh=H)[:, c, h, :]
                t_mm(av_bank, ps[av_bank][0:65, :], vsl, e_ap,
                     start=(c == 0), stop=(c == nch - 1),
                     rbufs=["vtm", en], skip=True)

            for c in range(nch):
                z_tile(c)
                if c == 1 and h + 1 < H:
                    qf_build(h + 1)
                if c >= 1:
                    av_tile(c - 1)
            av_tile(nch - 1)
            if h >= 2 and ei < len(nxt_early):
                nxt_early[ei]()
                ei += 1
            rr = (h // 4) * 32
            rc = (h % 4) * 512
            t_ve(lambda e, r=rr, cc=rc, bk=av_bank: e.tensor_copy(
                rsA_sb[r:r + 1, cc:cc + 512], ps[bk][64:65, :]),
                wr=["rsAa" if (h % 4) < 2 else "rsAb"], banks_rd=[av_bank])
            ch = h // 2
            base = (h % 2) * 64
            t_ve(lambda e, bk=av_bank, c_=ch, b_=base: e.tensor_copy(
                au_sb[b_:b_ + 64, c_ * 512:(c_ + 1) * 512], ps[bk][0:64, :]),
                wr=["sA%d" % (ch // 2)], banks_rd=[av_bank])
        while ei < len(nxt_early):
            nxt_early[ei]()
            ei += 1
        srcr = rsA_sb[0:128:32, :].rearrange("r (s t) -> r s t", s=4)
        tok1 = t_load("sync", "di", d_rs[:].rearrange("(r s) t -> r s t", r=4),
                      srcr, rbufs=["rsAa", "rsAb"], wbufs=["drs"])
        t_load("sync", "di", rsT_sb[:], d_rs[:], wbufs=["rsT"], rbufs=["drs"],
               extra=[tok1])
        mfeed(1)
        rcp_ap = yt_sb[0:16, :]
        t_ve(lambda e, o=rcp_ap: e.reciprocal(o, rsT_sb[:]),
             wr=["yt"], rd=["rsT"])
        t_ve(lambda e, i=rcp_ap: e.tensor_copy(rcpb_sb, i),
             wr=["rep"], rd=["yt"])
        for c in range(8):
            t_mm(2, ps[2][0:128, :], repc_sb[:, c * 128:(c + 1) * 128],
                 rcpb_sb, start=True, stop=True, rbufs=["repc", "rep"])
            t_ve(lambda e: e.tensor_copy(spr2_ap, ps[2][0:128, :]),
                 wr=["uT"], banks_rd=[2])
            t_ve(lambda e, c_=c: e.tensor_mul(
                au_sb[:, c_ * 512:(c_ + 1) * 512],
                au_sb[:, c_ * 512:(c_ + 1) * 512], spr2_ap),
                wr=["sA%d" % (c // 2)], rd=["sA%d" % (c // 2), "uT"])
            mfeed(1)

    def phase_p5(b, g, mfeed):
        for tl in range(4):
            tile = g * 4 + tl
            rows = slice(tile * 128, (tile + 1) * 128)
            t_load("sync", "di", xorig_sb[:], dx[b, rows, :], ["xt"])
            for hf in range(2):
                bank = 3 + hf
                for c in range(8):
                    t_mm(bank, ps[bank][0:128, :],
                         au_sb[:, c * 512 + tl * 128: c * 512 + (tl + 1) * 128],
                         wvof_sb[:, c * 1024 + hf * 512: c * 1024 + (hf + 1) * 512],
                         start=(c == 0), stop=(c == 7),
                         rbufs=["sA%d" % (c // 2), "wvof"])
                t_ve(lambda e, bk=bank, h_=hf: e.scalar_tensor_tensor(
                    x1t_sb[:, h_ * 512:(h_ + 1) * 512], ps[bk][0:128, :], 1.0,
                    xorig_sb[:, h_ * 512:(h_ + 1) * 512], OP.mult, OP.add),
                    wr=["x1t"], rd=["xt"], banks_rd=[bank])
                mfeed(1)
            t_load("gpsimd", "dg", d_x1[b, rows, :], x1t_sb[:], rbufs=["x1t"])
            ln_tile("x1t", x1t_sb[:], xn_sb[:], "xn", par=0)
            t_load("gpsimd", "dg", d_xnt[g, tl],
                   xn_sb[:], wbufs=["dxnt%d" % g], rbufs=["xn"])
            mfeed(1)
        mfeed(-1)

    def p6_prefetch(b, g, use_e=False):
        xnb = xnT_sb if g % 2 == 0 else quT_sb
        xnn = "xq" if g % 2 == 0 else "quT"
        if use_e:
            scrs = [(e_sb[:, 0:1024], ["e0", "e1"]),
                    (e_sb[:, 1024:2048], ["e2", "e3"])] * 2
        else:
            scrs = [(xn_sb[:], ["xn"]), (rsA_sb[:, 0:1024], ["rsAa"]),
                    (rsA_sb[:, 1024:2048], ["rsAb"]),
                    (scrA_sb[:, 0:1024], ["sA0"])]
        for tl in range(4):
            scr, scrn = scrs[(g % 2) * 2 + (tl % 2)]
            t_load("sync", "di", scr, d_xnt[g, tl],
                   wbufs=scrn, rbufs=["dxnt%d" % g])
            t_tr("scalar",
                 xnb[:].rearrange("p (c t) -> p c t", t=512)[
                     :, :, tl * 128:(tl + 1) * 128],
                 scr.rearrange("t (c p) -> t c p", p=128),
                 wbufs=[xnn], rbufs=scrn)

    def phase_p6(b, g):
        xnb = xnT_sb if g % 2 == 0 else quT_sb
        xnn = "xq" if g % 2 == 0 else "quT"
        ka()
        for c in range(8):
            t_mm(2, ps[2][0:32, :], vr_sb[:, c * 32:(c + 1) * 32],
                 xnb[:, c * 512:(c + 1) * 512],
                 start=(c == 0), stop=(c == 7), rbufs=["vr", xnn])
        t_ve(lambda e: e.tensor_copy(z1T_sb, ps[2][0:32, :]),
             wr=["rep"], banks_rd=[2])
        for tl in range(4):
            tile = g * 4 + tl
            rows = slice(tile * 128, (tile + 1) * 128)
            par = tl % 2
            xbuf = x1t_sb if par == 0 else xt_sb
            xbn = "x1t" if par == 0 else "xt"
            t_load("sync", "di", xbuf[:], d_x1[b, rows, :], wbufs=[xbn],
                   extra=k.qall("dg"))
            for hf in range(2):
                bank = 3 + hf
                for c in range(8):
                    t_mm(bank, ps[bank][0:128, :],
                         xnb[:, c * 512 + tl * 128: c * 512 + (tl + 1) * 128],
                         wvof_sb[:, c * 1024 + hf * 512: c * 1024 + (hf + 1) * 512],
                         start=(c == 0), stop=False, rbufs=[xnn, "wvof"], skip=True)
                t_mm(bank, ps[bank][0:128, :],
                     z1T_sb[0:32, tl * 128:(tl + 1) * 128],
                     ct_sb[0:32, hf * 512:(hf + 1) * 512],
                     start=False, stop=False, rbufs=["rep", "ct"], skip=True)
                t_mm(bank, ps[bank][0:128, :], onesb_sb[:],
                     b_sb[:, hf * 512:(hf + 1) * 512],
                     start=False, stop=True, rbufs=["onesb", "b"], skip=True)
                t_ve(lambda e, bk=bank, h_=hf, xb=xbuf: e.scalar_tensor_tensor(
                    yt_sb[:], ps[bk][0:128, :], 1.0,
                    xb[:, h_ * 512:(h_ + 1) * 512], OP.mult, OP.add),
                    wr=["yt"], rd=[xbn], banks_rd=[bank])
                tok = k.dma("sync", "do", dy[b, rows, hf * 512:(hf + 1) * 512],
                            yt_sb[:], waits=B_("yt").for_write())
                B_("yt").read(tok)
        if g + 2 < NG:
            p6_prefetch(b, g + 2)

    def mk_feeder(chunks):
        st = {"i": 0}
        def feed(n):
            if n < 0:
                n = len(chunks)
            while n > 0 and st["i"] < len(chunks):
                chunks[st["i"]]()
                st["i"] += 1
                n -= 1
        return feed

    phase_consts()
    for b in range(NSEQ):
        me, ml = metric_chunks(b, 0, pre=True)
        m0_och = mk_feeder(me + ml[:11])
        ml_tail = ml[11:]

        def m0_tail(ml_tail=ml_tail):
            for c_fn in ml_tail:
                c_fn()
        phase_p1(b, m0_och, m0_tail)
        t_load("gpsimd", "dg", wvof_sb[:].rearrange("p (c o) -> p c o", o=1024),
               d_wo[:], ["wvof"])
        for g in range(NG):
            if g + 1 < NG:
                ne, nl = metric_chunks(b, g + 1)
            else:
                ne, nl = [], []
            mf = mk_feeder(nl)
            phase_p4(b, g, ne, mf)
            if g == NG - 1:
                p6_prefetch(b, 0, use_e=True)
                p6_prefetch(b, 1, use_e=True)
            phase_p5(b, g, mf)
        t_load("gpsimd", "dg", wvof_sb[:].rearrange("p (c o) -> p c o", o=1024),
               d_wf[:], ["wvof"])
        for g in range(NG):
            phase_p6(b, g)
    k.emit("sync", lambda e: e.nop(), waits=tuple(k.qall("do")))
    k.build_block()
    return nc



def kernel(**inputs):
    consts = _host_consts()
    lays = _host_layouts(inputs)
    x = np.ascontiguousarray(np.asarray(inputs["L0_attn_x"], np.float32))

    nc = build_nc(dbg=_DEBUG)
    shared = {}
    shared.update({kk: np.ascontiguousarray(v) for kk, v in consts.items()})
    shared.update({kk: np.ascontiguousarray(v) for kk, v in lays.items()})
    in_maps = []
    for ci in range(NCORE):
        m = dict(shared)
        m["x"] = x[ci * NSEQ:(ci + 1) * NSEQ]
        in_maps.append(m)
    res = run_bass_kernel_spmd(nc, in_maps, core_ids=list(range(NCORE)))
    out = np.empty((B_TOT, T, D), np.float32)
    for ci in range(NCORE):
        out[ci * NSEQ:(ci + 1) * NSEQ] = res.results[ci]["y"]
    if _DEBUG:
        kernel._dbg = [{kk: np.asarray(v) for kk, v in r.items() if kk.startswith("dbg_")}
                       for r in res.results]
    return out
